# version 07
# baseline (speedup 1.0000x reference)
import contextlib
import numpy as np
import concourse.bass as bass
import concourse.mybir as mybir
from concourse.bass_utils import run_bass_kernel_spmd
from concourse.bass_core_rust import IndirectOffsetOnAxis

F32 = mybir.dt.float32
BF16 = mybir.dt.bfloat16
I32 = mybir.dt.int32
U32 = mybir.dt.uint32
AF = mybir.ActivationFunctionType
ALU = mybir.AluOpType
AXX = mybir.AxisListType.X

CFG = {
    "T": 4096,
    "D": 1024,
    "MOE_CAP": 384,
    "DEBUG_OUT": [],
}
C_RNE = float(np.float32(1.5 * 2 ** 23))

ENGINES = ("tensor", "vector", "scalar", "gpsimd", "sync")
SELF_SERIAL = {"vector", "scalar", "gpsimd"}


def _key(x):
    if isinstance(x, str):
        return x
    t = getattr(x, "tensor", None)
    if t is not None:
        return t.name
    return x.name


class Prog:

    def __init__(self, nc):
        self.nc = nc
        self.items = {e: [] for e in ENGINES}
        self.count = {e: 0 for e in ENGINES}
        self.dma_count = {}
        self.watermark = {}
        self.last_write = {}
        self.readers = {}
        self.bar = []

    def _emit_wait(self, eng_name, tok):
        if tok is None:
            return
        sem_name, val = tok
        if val <= 0:
            return
        if sem_name == "c_" + eng_name:
            return
        key = (eng_name, sem_name)
        if self.watermark.get(key, -1) >= val:
            return
        self.watermark[key] = val
        self.items[eng_name].append(("wait", sem_name, val))

    def _deps_for(self, reads, writes, deps):
        toks = list(deps) + list(self.bar)
        for r in reads:
            toks.append(self.last_write.get(_key(r)))
        for w in writes:
            kw = _key(w)
            toks.append(self.last_write.get(kw))
            toks.extend(self.readers.get(kw, ()))
        return toks

    def _record(self, tok, reads, writes):
        for r in reads:
            self.readers.setdefault(_key(r), []).append(tok)
        for w in writes:
            kw = _key(w)
            self.last_write[kw] = tok
            self.readers[kw] = []

    def op(self, eng_name, calls, r=(), w=(), deps=()):
        if isinstance(calls, tuple):
            calls = [calls]
        for t in self._deps_for(r, w, deps):
            self._emit_wait(eng_name, t)
        if eng_name in SELF_SERIAL:
            kk = self.count[eng_name]
            if kk > 0:
                self.items[eng_name].append(("selfwait", "c_" + eng_name, kk))
        self.count[eng_name] += 1
        tok = ("c_" + eng_name, self.count[eng_name])
        self.items[eng_name].append(("op", list(calls), "c_" + eng_name))
        self._record(tok, r, w)
        return tok

    def dma(self, queue, out, in_, sem=None, r=(), w=(), deps=(), loose=False,
            serial=False):
        sem = sem or ("d_" + queue)
        reads = list(r) + [in_]
        writes = list(w) + [out]
        deps = list(deps)
        if serial and sem in self.dma_count:
            deps.append((sem, self.dma_count[sem]))
        for t in self._deps_for(reads, writes, deps):
            self._emit_wait(queue, t)
        self.dma_count[sem] = self.dma_count.get(sem, 0) + 16
        tok = (sem, self.dma_count[sem])
        self.items[queue].append(("dma", out, in_, sem, loose))
        self._record(tok, reads, writes)
        return tok

    def idma(self, queue, out, out_off, in_, in_off, sem, r=(), w=(),
             deps=()):
        reads = list(r) + [in_]
        writes = list(w) + [out]
        if out_off is not None:
            reads.append(out_off[0])
        if in_off is not None:
            reads.append(in_off[0])
        for t in self._deps_for(reads, writes, deps):
            self._emit_wait(queue, t)
        self.dma_count[sem] = self.dma_count.get(sem, 0) + 16
        tok = (sem, self.dma_count[sem])
        oo = None if out_off is None else IndirectOffsetOnAxis(*out_off)
        io = None if in_off is None else IndirectOffsetOnAxis(*in_off)
        self.items[queue].append(("op16", ("indirect_dma_start", dict(
            out=out, out_offset=oo, in_=in_, in_offset=io)), sem))
        self._record(tok, reads, writes)
        return tok

    def cc(self, kind, op_, groups, cc_in, cc_out, deps=()):
        reads = [cc_in]
        writes = [cc_out]
        for t in self._deps_for(reads, writes, deps):
            self._emit_wait("gpsimd", t)
        if self.count["gpsimd"] > 0:
            self.items["gpsimd"].append(
                ("selfwait", "c_gpsimd", self.count["gpsimd"]))
        self.count["gpsimd"] += 1
        tok = ("c_gpsimd", self.count["gpsimd"])
        self.items["gpsimd"].append(("cc", kind, op_, groups, cc_in, cc_out,
                                     "c_gpsimd"))
        self._record(tok, reads, writes)
        return tok

    def barrier(self):
        toks = [("c_" + e, self.count[e]) for e in ENGINES if self.count[e] > 0]
        toks += [(s, v) for s, v in self.dma_count.items()]
        self.bar = toks
        self.last_write = {}
        self.readers = {}

    def raw(self, eng_name, fn):
        self.items[eng_name].append(("raw", fn))

    def final_wait(self, eng_name):
        for sem, v in self.dma_count.items():
            self._emit_wait(eng_name, (sem, v))
        for e in ENGINES:
            if e != eng_name and self.count[e] > 0:
                self._emit_wait(eng_name, ("c_" + e, self.count[e]))

    def make_sems(self, es):
        names = set("c_" + e for e in ENGINES if self.count[e] > 0)
        names |= set(self.dma_count.keys())
        for e in ENGINES:
            for it in self.items[e]:
                if it[0] in ("wait", "selfwait"):
                    names.add(it[1])
        return {n: es.enter_context(self.nc.semaphore(n)) for n in sorted(names)}

    def emit(self, block, ctx):
        def make_body(eng_name):
            def body(eng):
                for it in self.items[eng_name]:
                    kind = it[0]
                    if kind in ("wait", "selfwait"):
                        eng.wait_ge(ctx[it[1]], it[2])
                    elif kind == "op":
                        _, calls, sem_name = it
                        for mth, kw in calls[:-1]:
                            getattr(eng, mth)(**kw)
                        mth, kw = calls[-1]
                        getattr(eng, mth)(**kw).then_inc(ctx[sem_name], 1)
                    elif kind == "op16":
                        _, (mth, kw), sem_name = it
                        getattr(eng, mth)(**kw).then_inc(ctx[sem_name], 16)
                    elif kind == "dma":
                        _, out, in_, sem_name, loose = it
                        eng.dma_start(
                            out=out, in_=in_,
                            allow_slow_non_contiguous=loose,
                        ).then_inc(ctx[sem_name], 16)
                    elif kind == "cc":
                        _, k_, op_, groups, ci, co, sem_name = it
                        eng.collective_compute(
                            k_, op_, replica_groups=groups, ins=[ci], outs=[co],
                        ).then_inc(ctx[sem_name], 1)
                    elif kind == "raw":
                        it[1](eng)
            return body

        for eng_name in ENGINES:
            if not self.items[eng_name]:
                continue
            getattr(block, eng_name)(make_body(eng_name))


class K:
    def __init__(self, cfg):
        self.cfg = cfg
        self.T = cfg["T"]
        self.TC = self.T // 2
        self.D = cfg["D"]
        self.DI = 2 * self.D
        self.DIH = self.DI // 2
        self.NIT = self.DIH // 128
        self.DS = 16
        self.DTR = self.D // 16
        self.HD = 64
        self.E = 8
        self.HID = 2 * self.D
        self.I3 = 3 * self.D
        self.CAP = cfg["MOE_CAP"]
        self.nc = bass.Bass("TRN2", target_bir_lowering=False)
        self.p = Prog(self.nc)
        self.es = contextlib.ExitStack()
        self.phase_stacks = []
        self.dbg_outs = {}

    def sb(self, name, shape, dt=F32, es=None):
        return (es or self.es).enter_context(
            self.nc.sbuf_tensor(name, shape, dt))

    def dram(self, name, shape, dt=F32, kind="Internal"):
        return self.nc.dram_tensor(name, shape, dt, kind=kind).ap()

    def inline(self, name, arr):
        return self.nc.inline_tensor(np.ascontiguousarray(arr), name).ap()

    def v(self, meth, kw, r=(), w=(), deps=()):
        return self.p.op("vector", (meth, kw), r, w, deps)

    def s(self, meth, kw, r=(), w=(), deps=()):
        return self.p.op("scalar", (meth, kw), r, w, deps)

    def g(self, meth, kw, r=(), w=(), deps=()):
        return self.p.op("gpsimd", (meth, kw), r, w, deps)

    def t(self, meth, kw, r=(), w=(), deps=()):
        return self.p.op("tensor", (meth, kw), r, w, deps)

    def tm(self, calls, r=(), w=(), deps=()):
        return self.p.op("tensor", calls, r, w, deps)

    def emit_cc(self, kind, op_, groups, cc_in, cc_out):
        if self.cfg.get("LOCAL_CC"):
            n = cc_out.shape[0]
            for j in range(n):
                self.p.dma("gpsimd", cc_out[j], cc_in[0], sem="d_lcc")
            return
        self.p.cc(kind, op_, groups, cc_in, cc_out)

    def ones_colT_tile(self):
        if not hasattr(self, "_onec"):
            self._onec = self.sb("k_onec", [128, 1])
            self.v("memset", dict(ap=self._onec[:], constant=1.0),
                   w=["k_onec"])
        return self._onec[:]

    def const_col(self, val):
        if not hasattr(self, "_cc_tile"):
            self._cc_tile = self.sb("k_consts", [128, 32])
            self._cc_map = {}
        v = float(val)
        if v not in self._cc_map:
            i = len(self._cc_map)
            assert i < 32
            self._cc_map[v] = i
            ap = self._cc_tile[:, i:i + 1]
            self.v("memset", dict(ap=ap, constant=v), w=["k_consts"])
        i = self._cc_map[v]
        return self._cc_tile[:, i:i + 1]

    def debug(self, name, src_ap, shape):
        if name not in self.cfg["DEBUG_OUT"]:
            return
        o = self.dram("dbg_" + name, shape, kind="ExternalOutput")
        self.p.dma("gpsimd", o, src_ap, sem="d_dbg")
        self.dbg_outs[name] = shape

    def debug_dram(self, name, shape):
        if name in self.cfg["DEBUG_OUT"]:
            o = self.dram("dbg_" + name, shape, kind="ExternalOutput")
            self.dbg_outs[name] = shape
            return o
        return self.dram("scr_" + name, shape, kind="Internal")


def ones_colT(k):
    return k.ones_colT_tile()


def mm3(k, ps_ap, lhi, llo, rhi, rlo, start, stop, r=(), w=(), deps=()):
    calls = [
        ("matmul", dict(out=ps_ap, lhsT=lhi, rhs=rhi, start=start, stop=False)),
        ("matmul", dict(out=ps_ap, lhsT=lhi, rhs=rlo, start=False, stop=False)),
        ("matmul", dict(out=ps_ap, lhsT=llo, rhs=rhi, start=False, stop=stop)),
    ]
    return k.tm(calls, r=r, w=w, deps=deps)


def split16(k, hi_ap, lo_ap, src_ap, r=(), w=(), deps=()):
    k.s("copy", dict(out=hi_ap, in_=src_ap),
        r=list(r) + [src_ap], w=[hi_ap], deps=deps)
    return k.v("tensor_sub", dict(out=lo_ap, in0=src_ap, in1=hi_ap),
               r=list(r) + [src_ap, hi_ap], w=list(w) + [lo_ap])


def softplus_inplace(k, x_ap, tmp_ap, xkey, tkey):
    k.s("activation", dict(out=tmp_ap, in_=x_ap, func=AF.Abs),
        r=[xkey], w=[tkey])
    k.s("activation", dict(out=tmp_ap, in_=tmp_ap, func=AF.Exp, scale=-1.0),
        r=[tkey], w=[tkey])
    k.s("activation", dict(out=tmp_ap, in_=tmp_ap, func=AF.Ln, bias=1.0),
        r=[tkey], w=[tkey])
    k.v("scalar_tensor_tensor", dict(
        out=x_ap, in0=x_ap, scalar=0.0, in1=tmp_ap,
        op0=ALU.max, op1=ALU.add), r=[xkey, tkey], w=[xkey])


def emit_rne(k, ap, key, itmp, ikey):
    k.v("tensor_scalar", dict(out=ap, in0=ap, scalar1=C_RNE, scalar2=C_RNE,
                              op0=ALU.add, op1=ALU.subtract), r=[key], w=[key])
    k.v("tensor_copy", dict(out=itmp, in_=ap), r=[key], w=[ikey])
    k.v("tensor_copy", dict(out=ap, in_=itmp), r=[ikey], w=[key])


def emit_qdq(k, ap, key, itmp, ikey, sinv_rep, s_rep):
    k.v("tensor_scalar", dict(out=ap, in0=ap, scalar1=sinv_rep, scalar2=None,
                              op0=ALU.mult), r=[key, "s_scr"], w=[key])
    emit_rne(k, ap, key, itmp, ikey)
    k.v("tensor_scalar", dict(out=ap, in0=ap, scalar1=s_rep, scalar2=None,
                              op0=ALU.mult), r=[key, "s_scr"], w=[key])


EPS_LN = 1e-5
EPS_RMS = float(np.finfo(np.float32).eps)
EPS_A = 1e-6


def emit_ln(k, a_x, a_t, a_sq, a_st, eps2, w_rep):
    D = k.D
    k.v("reduce_sum", dict(out=a_st[:, 0:1], in_=a_x[:], axis=AXX),
        r=[a_x], w=["a_st"])
    k.v("tensor_scalar_mul", dict(out=a_st[:, 1:2], in0=a_st[:, 0:1],
                                  scalar1=1.0 / D), r=["a_st"], w=["a_st"])
    k.v("tensor_scalar", dict(out=a_t[:], in0=a_x[:], scalar1=a_st[:, 1:2],
                              scalar2=None, op0=ALU.subtract),
        r=[a_x, "a_st"], w=[a_t])
    k.s("activation", dict(out=a_sq[:], in_=a_t[:], func=AF.Square,
                           accum_out=a_st[:, 2:3]), r=[a_t], w=[a_sq, "a_st"])
    c_invd, c_eps = k.const_col(1.0 / D), k.const_col(EPS_LN)
    k.s("activation", dict(out=a_st[:, 3:4], in_=a_st[:, 2:3], func=AF.Sqrt,
                           scale=c_invd, bias=c_eps),
        r=["a_st", "k_consts"], w=["a_st"])
    k.v("reciprocal", dict(out=a_st[:, 4:5], in_=a_st[:, 3:4]),
        r=["a_st"], w=["a_st"])
    k.v("tensor_scalar", dict(out=a_x[:], in0=a_t[:], scalar1=a_st[:, 4:5],
                              scalar2=None, op0=ALU.mult),
        r=[a_t, "a_st"], w=[a_x])
    if eps2 is None:
        return
    k.s("activation", dict(out=a_sq[:], in_=a_x[:], func=AF.Square,
                           accum_out=a_st[:, 5:6]), r=[a_x], w=[a_sq, "a_st"])
    c_eps2 = k.const_col(eps2)
    k.s("activation", dict(out=a_st[:, 6:7], in_=a_st[:, 5:6], func=AF.Sqrt,
                           scale=c_invd, bias=c_eps2),
        r=["a_st", "k_consts"], w=["a_st"])
    k.v("reciprocal", dict(out=a_st[:, 7:8], in_=a_st[:, 6:7]),
        r=["a_st"], w=["a_st"])
    if w_rep is not None:
        k.v("scalar_tensor_tensor", dict(
            out=a_t[:], in0=a_x[:], scalar=a_st[:, 7:8], in1=w_rep[:],
            op0=ALU.mult, op1=ALU.mult), r=[a_x, "a_st", w_rep], w=[a_t])
    else:
        k.v("tensor_scalar", dict(out=a_t[:], in0=a_x[:], scalar1=a_st[:, 7:8],
                                  scalar2=None, op0=ALU.mult),
            r=[a_x, "a_st"], w=[a_t])


def build(cfg):
    k = K(cfg)
    nc, p = k.nc, k.p
    T, TC, D = k.T, k.TC, k.D
    DIH, NIT, DS, DTR = k.DIH, k.NIT, k.DS, k.DTR
    NDC = D // 128
    NTT = T // 128
    SS = 128 if T >= 2048 else max(64, T // 4)
    NST = T // SS
    SF = DS * SS
    NS512 = T // 512 if T >= 512 else 1
    W512 = min(512, T)

    x0 = k.dram("x0", [T, D], kind="ExternalInput")
    w_ipT = k.dram("w_ipT", [D, 2 * DIH], kind="ExternalInput")
    conv_w = k.dram("conv_w", [DIH, 4], kind="ExternalInput")
    conv_b = k.dram("conv_b", [DIH], kind="ExternalInput")
    w_xpT = k.dram("w_xpT", [DIH, 96], kind="ExternalInput")
    w_dtT = k.dram("w_dtT", [DTR, DIH], kind="ExternalInput")
    dt_b = k.dram("dt_b", [DIH], kind="ExternalInput")
    A_log = k.dram("A_log", [DIH, DS], kind="ExternalInput")
    D_par = k.dram("D_par", [DIH], kind="ExternalInput")
    w_opT = k.dram("w_opT", [DIH, D], kind="ExternalInput")
    norm_w = k.dram("norm_w", [D], kind="ExternalInput")

    ident = k.inline("ident", np.eye(128, dtype=np.float32))
    ones_row = k.inline("ones_row", np.ones((1, 128), np.float32))
    selbc = np.zeros((96, 32), np.float32)
    for n in range(32):
        selbc[64 + n, n] = 1.0
    sel_bc = k.inline("sel_bc", selbc)

    u_spill = k.dram("u_spill", [DIH, T])
    z_spill = k.dram("z_spill", [DIH, T])
    cc0_in = k.dram("cc0_in", [1, 96 * T])
    cc0_out = k.dram("cc0_out", [1, 96 * T])
    cc1_in = k.dram("cc1_in", [1, T * D])
    cc1_out = k.dram("cc1_out", [1, TC * D])
    PAIRS = [[0, 1], [2, 3], [4, 5], [6, 7]]
    ALL8 = [list(range(8))]

    k.const_col(0.0)
    k.ones_colT_tile()
    sid = k.sb("sid", [128, 128])
    s_ones = k.sb("s_ones", [1, 128])
    s_st8 = k.sb("a_st", [128, 8])

    PSB = [k.es.enter_context(nc.psum_tensor(f"psb{i}", [128, 512], F32))
           for i in range(8)]
    PK = [f"psb{i}" for i in range(8)]

    p.dma("sync", sid[:], ident, sem="d_ini", serial=True)
    p.dma("sync", s_ones[:], ones_row, sem="d_ini", serial=True)

    es_m = contextlib.ExitStack()
    m_xdbc = k.sb("m_xdbc", [128, T], es=es_m)
    s_selbc = k.sb("s_selbc", [96, 32], es=es_m)
    s_convw = k.sb("s_convw", [128, 4 * NIT], es=es_m)
    s_convb = k.sb("s_convb", [128, NIT], es=es_m)
    s_dtb = k.sb("s_dtb", [128, NIT], es=es_m)
    s_A = k.sb("s_A", [128, DS * NIT], es=es_m)
    s_D = k.sb("s_D", [128, NIT], es=es_m)
    s_dtw = k.sb("s_dtw", [DTR, DIH], es=es_m)
    s_xpw = k.sb("s_xpw", [128, NIT * 96], es=es_m)
    s_nw1 = k.sb("s_nw1", [1, D], es=es_m)
    p.dma("sync", s_selbc[:], sel_bc, sem="d_ini", serial=True)
    p.dma("sync", s_convw[:].rearrange("p (n c) -> p n c", c=4),
          conv_w.rearrange("(n p) c -> p n c", p=128), sem="d_ini", serial=True)
    p.dma("sync", s_convb[:], conv_b.rearrange("(n p) -> p n", p=128),
          sem="d_ini", serial=True, loose=True)
    p.dma("sync", s_dtb[:], dt_b.rearrange("(n p) -> p n", p=128),
          sem="d_ini", serial=True, loose=True)
    p.dma("sync", s_A[:].rearrange("p (n s) -> p n s", s=DS),
          A_log.rearrange("(n p) s -> p n s", p=128), sem="d_ini", serial=True)
    p.dma("sync", s_D[:], D_par.rearrange("(n p) -> p n", p=128),
          sem="d_ini", serial=True, loose=True)
    p.dma("sync", s_dtw[:], w_dtT, sem="d_ini", serial=True)
    p.dma("sync", s_xpw[:].rearrange("p (n o) -> p n o", o=96),
          w_xpT.rearrange("(n p) o -> p n o", p=128), sem="d_ini", serial=True)
    k.s("activation", dict(out=s_A[:], in_=s_A[:], func=AF.Exp),
        r=[s_A], w=[s_A])
    k.v("tensor_scalar_mul", dict(out=s_A[:], in0=s_A[:], scalar1=-1.0),
        r=[s_A], w=[s_A])
    p.dma("sync", s_nw1[:], norm_w[None, :], sem="d_ini", serial=True)

    es_1a = contextlib.ExitStack()
    m_xmTh = k.sb("m_xmTh", [128, NDC * T], BF16, es=es_1a)
    m_xmTl = k.sb("m_xmTl", [128, NDC * T], BF16, es=es_1a)

    es_ln = contextlib.ExitStack()
    s_nw = k.sb("s_nw", [128, D], es=es_ln)
    a_x = k.sb("a_x", [128, D], es=es_ln)
    a_t = k.sb("a_t", [128, D], es=es_ln)
    a_sq = k.sb("a_sq", [128, D], es=es_ln)
    for dc in range(NDC):
        k.t("matmul", dict(out=PSB[0][:, 0:128], lhsT=s_ones[:],
                           rhs=s_nw1[:, 128 * dc:128 * (dc + 1)],
                           start=True, stop=True),
            r=[s_ones, s_nw1], w=[PK[0]])
        k.s("copy", dict(out=s_nw[:, 128 * dc:128 * (dc + 1)],
                         in_=PSB[0][:, 0:128]), r=[PK[0]], w=[s_nw])
    for tt in range(NTT):
        p.dma("sync", a_x[:], x0[128 * tt:128 * (tt + 1), :], sem="d_x")
        emit_ln(k, a_x, a_t, a_sq, s_st8, EPS_RMS, s_nw)
        for dc in range(NDC):
            bank, bk = PSB[dc % 2], PK[dc % 2]
            k.t("transpose", dict(out=bank[:, 0:128],
                                  in_=a_t[:, 128 * dc:128 * (dc + 1)],
                                  identity=sid[:]),
                r=[a_t, sid], w=[bk])
            fo = dc * T + 128 * tt
            split16(k, m_xmTh[:, fo:fo + 128], m_xmTl[:, fo:fo + 128],
                    bank[:, 0:128], r=[bk], w=[m_xmTh, m_xmTl])
    es_ln.close()
    p.barrier()

    es_ip = contextlib.ExitStack()
    TMPW = [k.sb(f"s_tmpw{i}", [128, DIH], es=es_ip) for i in range(2)]
    WIH = [k.sb("m_wih0", [128, 2 * DIH], BF16, es=es_ip)] * 2
    WIL = [k.sb("m_wil0", [128, 2 * DIH], BF16, es=es_ip)] * 2
    m_xs = k.sb("m_xs", [128, T + 4], es=es_ip)
    m_u = k.sb("m_u", [128, T], es=es_ip)
    m_z = k.sb("m_z", [128, W512], es=es_ip)
    k.v("memset", dict(ap=m_xs[:, 0:4], constant=0.0), w=[m_xs])

    for it in range(NIT):
        s_tmpw, m_wih, m_wil = TMPW[it % 2], WIH[it % 2], WIL[it % 2]
        p.dma("sync", s_tmpw[:, 0:128 * NDC].rearrange("p (c m) -> p c m", m=128),
              w_ipT[:, 128 * it:128 * (it + 1)].rearrange(
                  "(c p) m -> p c m", p=128), sem="d_wip" + str(it % 2))
        split16(k, m_wih[:, 0:128 * NDC], m_wil[:, 0:128 * NDC],
                s_tmpw[:, 0:128 * NDC], r=[s_tmpw], w=[m_wih, m_wil])
        p.dma("sync", s_tmpw[:, 0:128 * NDC].rearrange("p (c m) -> p c m", m=128),
              w_ipT[:, DIH + 128 * it:DIH + 128 * (it + 1)].rearrange(
                  "(c p) m -> p c m", p=128), sem="d_wip" + str(it % 2))
        split16(k, m_wih[:, 128 * NDC:256 * NDC], m_wil[:, 128 * NDC:256 * NDC],
                s_tmpw[:, 0:128 * NDC], r=[s_tmpw], w=[m_wih, m_wil])
        for st in range(NS512):
            bank, bk = PSB[2 + (st % 2)], PK[2 + (st % 2)]
            for dc in range(NDC):
                mm3(k, bank[:, 0:W512],
                    m_wih[:, 128 * dc:128 * (dc + 1)],
                    m_wil[:, 128 * dc:128 * (dc + 1)],
                    m_xmTh[:, dc * T + W512 * st: dc * T + W512 * (st + 1)],
                    m_xmTl[:, dc * T + W512 * st: dc * T + W512 * (st + 1)],
                    start=(dc == 0), stop=(dc == NDC - 1),
                    r=[m_wih, m_wil, m_xmTh, m_xmTl], w=[bk])
            k.s("copy", dict(out=m_xs[:, 4 + W512 * st:4 + W512 * (st + 1)],
                             in_=bank[:, 0:W512]), r=[bk], w=[m_xs])
            bankz, bkz = PSB[4 + (st % 2)], PK[4 + (st % 2)]
            for dc in range(NDC):
                mm3(k, bankz[:, 0:W512],
                    m_wih[:, 128 * NDC + 128 * dc:128 * NDC + 128 * (dc + 1)],
                    m_wil[:, 128 * NDC + 128 * dc:128 * NDC + 128 * (dc + 1)],
                    m_xmTh[:, dc * T + W512 * st: dc * T + W512 * (st + 1)],
                    m_xmTl[:, dc * T + W512 * st: dc * T + W512 * (st + 1)],
                    start=(dc == 0), stop=(dc == NDC - 1),
                    r=[m_wih, m_wil, m_xmTh, m_xmTl], w=[bkz])
            k.s("activation", dict(out=m_z[:, 0:W512], in_=bankz[:, 0:W512],
                                   func=AF.Silu), r=[bkz], w=[m_z])
            p.dma("sync", z_spill[128 * it:128 * (it + 1),
                                  W512 * st:W512 * (st + 1)],
                  m_z[:, 0:W512], sem="d_zsp")
        cw = [s_convw[:, 4 * it + kk:4 * it + kk + 1] for kk in range(4)]
        k.v("tensor_scalar", dict(out=m_u[:], in0=m_xs[:, 1:1 + T],
                                  scalar1=cw[0], scalar2=None, op0=ALU.mult),
            r=[m_xs, s_convw], w=[m_u])
        for kk in range(1, 4):
            k.v("scalar_tensor_tensor", dict(
                out=m_u[:], in0=m_xs[:, 1 + kk:1 + kk + T], scalar=cw[kk],
                in1=m_u[:], op0=ALU.mult, op1=ALU.add),
                r=[m_xs, s_convw, m_u], w=[m_u])
        k.s("activation", dict(out=m_u[:], in_=m_u[:], func=AF.Silu,
                               bias=s_convb[:, it:it + 1]),
            r=[m_u, s_convb], w=[m_u])
        p.dma("sync", u_spill[128 * it:128 * (it + 1), :], m_u[:], sem="d_usp")
        for st in range(NS512):
            bank, bk = PSB[6 + (st % 2)], PK[6 + (st % 2)]
            k.t("matmul", dict(out=bank[0:96, 0:W512],
                               lhsT=s_xpw[:, 96 * it:96 * (it + 1)],
                               rhs=m_u[:, W512 * st:W512 * (st + 1)],
                               start=True, stop=True),
                r=[s_xpw, m_u], w=[bk])
            if it == 0:
                k.s("copy", dict(out=m_xdbc[0:96, W512 * st:W512 * (st + 1)],
                                 in_=bank[0:96, 0:W512]), r=[bk], w=[m_xdbc])
            else:
                k.v("tensor_add", dict(
                    out=m_xdbc[0:96, W512 * st:W512 * (st + 1)],
                    in0=m_xdbc[0:96, W512 * st:W512 * (st + 1)],
                    in1=bank[0:96, 0:W512]), r=[bk, m_xdbc], w=[m_xdbc])
    es_ip.close()
    es_1a.close()
    p.barrier()

    p.dma("sync", cc0_in[0, :].rearrange("(p f) -> p f", p=96),
          m_xdbc[0:96, :], sem="d_cc0")
    k.emit_cc("AllReduce", ALU.add, PAIRS, cc0_in, cc0_out)
    es_1b = contextlib.ExitStack()
    p.dma("sync", m_xdbc[0:96, :], cc0_out[0, :].rearrange("(p f) -> p f", p=96),
          sem="d_cc0c")
    k.debug("xdbc", m_xdbc[0:96, :], [96, T])

    BREP = [k.sb(f"s_brep{i}", [128, SF], es=es_1b) for i in range(2)]
    CREP = [k.sb(f"s_crep{i}", [128, SF], es=es_1b) for i in range(2)]
    DA = [k.sb(f"s_dA{i}", [128, SF], es=es_1b) for i in range(2)]
    DBU = [k.sb(f"s_dBu{i}", [128, SF], es=es_1b) for i in range(2)]
    HH = [k.sb(f"s_h{i}", [128, SF], es=es_1b) for i in range(2)]
    HC = [k.sb(f"s_hC{i}", [128, SF], es=es_1b) for i in range(2)]
    DLY = [k.sb(f"s_dly{i}", [128, SS], es=es_1b) for i in range(2)]
    DU = [k.sb(f"s_du{i}", [128, SS], es=es_1b) for i in range(2)]
    UT = [k.sb(f"s_ut{i}", [128, SS], es=es_1b) for i in range(2)]
    ZT = [k.sb(f"s_zt{i}", [128, SS], es=es_1b) for i in range(2)]
    SPT = [k.sb(f"s_spt{i}", [128, SS], es=es_1b) for i in range(2)]
    YG = [k.sb(f"s_yg{i}", [128, NIT * SS], es=es_1b) for i in range(2)]
    YGH = [k.sb(f"s_ygh{i}", [128, NIT * SS], BF16, es=es_1b) for i in range(2)]
    YGL = [k.sb(f"s_ygl{i}", [128, NIT * SS], BF16, es=es_1b) for i in range(2)]
    MO = [k.sb(f"s_mo{i}", [128, SS], es=es_1b) for i in range(2)]
    MO2 = [k.sb(f"s_mo2{i}", [128, 128], es=es_1b) for i in range(2)]
    s_hcar = k.sb("s_hcar", [128, DS * NIT], es=es_1b)
    s_opwh = k.sb("s_opwh", [128, NDC * NIT * 128], BF16, es=es_1b)
    s_opwl = k.sb("s_opwl", [128, NDC * NIT * 128], BF16, es=es_1b)
    s_tw2 = k.sb("s_tw2", [128, NIT * 128], es=es_1b)

    k.v("memset", dict(ap=s_hcar[:], constant=0.0), w=[s_hcar])
    for dc in range(NDC):
        p.dma("sync", s_tw2[:].rearrange("p (n m) -> p n m", m=128),
              w_opT[:, 128 * dc:128 * (dc + 1)].rearrange(
                  "(n p) m -> p n m", p=128), sem="d_wop")
        split16(k, s_opwh[:, 1024 * dc:1024 * (dc + 1)],
                s_opwl[:, 1024 * dc:1024 * (dc + 1)],
                s_tw2[:], r=[s_tw2], w=[s_opwh, s_opwl])

    NGI = NST * NIT

    def stage_A(gi):
        st, it = gi // NIT, gi % NIT
        ip = gi % 2
        sp = st % 2
        s_brep, s_crep = BREP[sp], CREP[sp]
        if it == 0:
            NPB = max(1, 512 // SS)
            for half, dst in ((0, s_brep), (1, s_crep)):
                for n0 in range(0, DS, NPB):
                    bank = PSB[(n0 // NPB) % 2]
                    bk = PK[(n0 // NPB) % 2]
                    nn = min(NPB, DS - n0)
                    calls = []
                    for j in range(nn):
                        n = n0 + j
                        calls.append(("matmul", dict(
                            out=bank[:, SS * j:SS * (j + 1)],
                            lhsT=s_selbc[:, 16 * half + n:16 * half + n + 1
                                         ].broadcast_to([96, 128]),
                            rhs=m_xdbc[0:96, SS * st:SS * (st + 1)],
                            start=True, stop=True)))
                    k.tm(calls, r=[s_selbc, m_xdbc], w=[bk])
                    k.s("copy", dict(out=dst[:, SS * n0:SS * (n0 + nn)],
                                     in_=bank[:, 0:SS * nn]), r=[bk], w=[dst])
        s_dly, s_du, s_ut, s_zt = DLY[ip], DU[ip], UT[ip], ZT[ip]
        s_dA, s_dBu, s_sptmp = DA[ip], DBU[ip], SPT[ip]
        dtb_, dtk = PSB[2 + ip], PK[2 + ip]
        k.t("matmul", dict(out=dtb_[:, 0:SS],
                           lhsT=s_dtw[:, 128 * it:128 * (it + 1)],
                           rhs=m_xdbc[0:DTR, SS * st:SS * (st + 1)],
                           start=True, stop=True),
            r=[s_dtw, m_xdbc], w=[dtk])
        k.v("tensor_scalar", dict(out=s_dly[:], in0=dtb_[:, 0:SS],
                                  scalar1=s_dtb[:, it:it + 1], scalar2=None,
                                  op0=ALU.add), r=[dtk, s_dtb], w=[s_dly])
        softplus_inplace(k, s_dly[:], s_sptmp[:], s_dly, s_sptmp)
        p.dma("sync", s_ut[:], u_spill[128 * it:128 * (it + 1),
                                       SS * st:SS * (st + 1)],
              sem="d_uld" + str(ip))
        p.dma("sync", s_zt[:], z_spill[128 * it:128 * (it + 1),
                                       SS * st:SS * (st + 1)],
              sem="d_zld" + str(ip))
        k.v("tensor_mul", dict(out=s_du[:], in0=s_dly[:], in1=s_ut[:]),
            r=[s_dly, s_ut], w=[s_du])
        dA3 = s_dA[:].rearrange("p (n f) -> p n f", n=DS)
        dly3 = s_dly[:].unsqueeze(1).broadcast_to([128, DS, SS])
        A3 = s_A[:, DS * it:DS * (it + 1)].unsqueeze(2).broadcast_to(
            [128, DS, SS])
        k.g("tensor_tensor", dict(out=dA3, in0=dly3, in1=A3, op=ALU.mult),
            r=[s_dly, s_A], w=[s_dA])
        k.s("activation", dict(out=s_dA[:], in_=s_dA[:], func=AF.Exp),
            r=[s_dA], w=[s_dA])
        du3 = s_du[:].unsqueeze(1).broadcast_to([128, DS, SS])
        dBu3 = s_dBu[:].rearrange("p (n f) -> p n f", n=DS)
        br3 = s_brep[:].rearrange("p (n f) -> p n f", n=DS)
        k.g("tensor_tensor", dict(out=dBu3, in0=du3, in1=br3, op=ALU.mult),
            r=[s_du, s_brep], w=[s_dBu])

    def stage_B(gi):
        st, it = gi // NIT, gi % NIT
        ip = gi % 2
        sp = st % 2
        s_crep = CREP[sp]
        s_yg = YG[sp]
        s_ut, s_zt = UT[ip], ZT[ip]
        s_dA, s_dBu, s_h, s_hC, s_sptmp = (DA[ip], DBU[ip], HH[ip],
                                           HC[ip], SPT[ip])
        car = s_hcar[:, DS * it:DS * (it + 1)]
        car3 = car.rearrange("p (n f) -> p n f", f=1)
        dA0 = s_dA[:].rearrange("p (n f) -> p n f", n=DS)[:, :, 0:1]
        dBu0 = s_dBu[:].rearrange("p (n f) -> p n f", n=DS)[:, :, 0:1]
        tmpc3 = s_sptmp[:, 0:DS].rearrange("p (n f) -> p n f", f=1)
        k.v("tensor_tensor", dict(out=tmpc3, in0=dA0, in1=car3, op=ALU.mult),
            r=[s_dA, s_hcar], w=[s_sptmp])
        k.v("tensor_tensor", dict(out=dBu0, in0=dBu0, in1=tmpc3, op=ALU.add),
            r=[s_sptmp, s_dBu], w=[s_dBu])
        k.v("memset", dict(ap=dA0, constant=0.0), r=[], w=[s_dA])
        k.v("tensor_tensor_scan", dict(
            out=s_h[:], data0=s_dA[:], data1=s_dBu[:], initial=0.0,
            op0=ALU.mult, op1=ALU.add), r=[s_dA, s_dBu], w=[s_h])
        hlast = s_h[:].rearrange("p (n f) -> p n f", n=DS)[:, :, SS - 1:SS]
        k.v("tensor_copy", dict(out=car3, in_=hlast), r=[s_h], w=[s_hcar])
        h3 = s_h[:].rearrange("p (n f) -> p n f", n=DS)
        cr3 = s_crep[:].rearrange("p (n f) -> p n f", n=DS)
        hc3t = s_hC[:].rearrange("p (f n) -> p n f", n=DS)
        k.v("tensor_tensor", dict(out=hc3t, in0=h3, in1=cr3, op=ALU.mult),
            r=[s_h, s_crep], w=[s_hC])
        yg = s_yg[:, SS * it:SS * (it + 1)]
        k.v("tensor_reduce", dict(
            out=yg, in_=s_hC[:].rearrange("p (f n) -> p f n", n=DS),
            axis=AXX, op=ALU.add), r=[s_hC], w=[s_yg])
        k.v("scalar_tensor_tensor", dict(
            out=yg, in0=s_ut[:], scalar=s_D[:, it:it + 1], in1=yg,
            op0=ALU.mult, op1=ALU.add), r=[s_ut, s_D, s_yg], w=[s_yg])
        k.v("tensor_mul", dict(out=yg, in0=yg, in1=s_zt[:]),
            r=[s_yg, s_zt], w=[s_yg])

    def stripe_tail(st):
        sp = st % 2
        s_yg, s_ygh, s_ygl = YG[sp], YGH[sp], YGL[sp]
        split16(k, s_ygh[:], s_ygl[:], s_yg[:], r=[s_yg], w=[s_ygh, s_ygl])
        for dc in range(NDC):
            bank, bkk = PSB[4 + (dc % 2)], PK[4 + (dc % 2)]
            for it in range(NIT):
                mm3(k, bank[:, 0:SS],
                    s_opwh[:, 1024 * dc + 128 * it:1024 * dc + 128 * (it + 1)],
                    s_opwl[:, 1024 * dc + 128 * it:1024 * dc + 128 * (it + 1)],
                    s_ygh[:, SS * it:SS * (it + 1)],
                    s_ygl[:, SS * it:SS * (it + 1)],
                    start=(it == 0), stop=(it == NIT - 1),
                    r=[s_opwh, s_opwl, s_ygh, s_ygl], w=[bkk])
            s_mo = MO[dc % 2]
            s_mo2 = MO2[dc % 2]
            k.s("copy", dict(out=s_mo[:], in_=bank[:, 0:SS]), r=[bkk], w=[s_mo])
            for sub in range(SS // 128):
                tb, tbk = PSB[6 + (dc % 2)], PK[6 + (dc % 2)]
                k.t("transpose", dict(out=tb[:, 0:128],
                                      in_=s_mo[:, 128 * sub:128 * (sub + 1)],
                                      identity=sid[:]), r=[s_mo, sid], w=[tbk])
                k.s("copy", dict(out=s_mo2[:, 0:128], in_=tb[:, 0:128]),
                    r=[tbk], w=[s_mo2])
                p.dma("sync",
                      cc1_in[0, :].rearrange("(t d) -> t d", d=D)[
                          SS * st + 128 * sub:SS * st + 128 * (sub + 1),
                          128 * dc:128 * (dc + 1)],
                      s_mo2[:, 0:128], sem="d_cc1" + str(dc % 2))

    stage_A(0)
    for gi in range(NGI):
        if gi + 1 < NGI:
            stage_A(gi + 1)
        stage_B(gi)
        if gi % NIT == NIT - 1:
            stripe_tail(gi // NIT)
    k.emit_cc("ReduceScatter", ALU.add, PAIRS, cc1_in, cc1_out)
    if "mamba_mm" in k.cfg["DEBUG_OUT"]:
        o = k.dram("dbg_mamba_mm", [TC, D], kind="ExternalOutput")
        k.dbg_outs["mamba_mm"] = [TC, D]
        s_m1 = k.sb("s_m1", [128, D], es=es_1b)
        for tt in range(TC // 128):
            p.dma("sync", s_m1[:], cc1_out[0, :].rearrange(
                "(t d) -> t d", d=D)[128 * tt:128 * (tt + 1), :], sem="d_dbg1")
            p.dma("gpsimd", o[128 * tt:128 * (tt + 1), :], s_m1[:], sem="d_dbg")
    es_1b.close()
    es_m.close()
    p.barrier()

    if cfg.get("STOP_AFTER") == 1:
        return k, locals()
    HB = 8
    ln1_w = k.dram("ln1_w", [D], kind="ExternalInput")
    wq = k.dram("wq", [D, HB * 64], kind="ExternalInput")
    wk = k.dram("wk", [D, 128], kind="ExternalInput")
    wv = k.dram("wv", [D, 128], kind="ExternalInput")
    qn_w = k.dram("qn_w", [64], kind="ExternalInput")
    kn_w = k.dram("kn_w", [64], kind="ExternalInput")
    wg = k.dram("wg", [64, 64], kind="ExternalInput")
    wo_r = k.dram("wo_r", [4, 128, D], kind="ExternalInput")
    cc2_in = k.dram("cc2_in", [1, T * D])
    cc2_out = k.dram("cc2_out", [1, TC * D])
    _inv = 1.0 / 10000.0 ** (np.arange(0, 64, 2) / 64.0)
    _fr = np.arange(T)[:, None] * _inv[None, :]
    rope_cs = k.inline("rope_cs", np.concatenate(
        [np.cos(_fr), np.sin(_fr)], axis=1).astype(np.float32))
    triuT = k.inline("triuT", np.triu(np.ones((128, 128), np.float32)))

    NQT = T // 128
    es_2 = contextlib.ExitStack()
    a_gt = k.sb("a_gt", [128, 4 * T], es=es_2)
    s_triu = k.sb("s_triu", [128, 128], es=es_2)
    qt_sp_h = k.dram("qt_sp_h", [8, 64 * T], BF16)
    qt_sp_l = k.dram("qt_sp_l", [8, 64 * T], BF16)
    es_2q = contextlib.ExitStack()
    a_qsth = k.sb("a_qsth", [64, 128], BF16, es=es_2q)
    a_qstl = k.sb("a_qstl", [64, 128], BF16, es=es_2q)
    a_kth = k.sb("a_kth", [128, 2 * T], BF16, es=es_2q)
    a_ktl = k.sb("a_ktl", [128, 2 * T], BF16, es=es_2q)
    a_v = k.sb("a_v", [128, NQT * 130], es=es_2q)
    p.dma("sync", s_triu[:], triuT, sem="d_ini", serial=True)

    es_2a = contextlib.ExitStack()
    h_hnh = k.sb("h_hnh", [128, NDC * (T // 4)], BF16, es=es_2a)
    h_hnl = k.sb("h_hnl", [128, NDC * (T // 4)], BF16, es=es_2a)
    a_wqh = k.sb("a_wqh", [128, NDC * HB * 64], BF16, es=es_2a)
    a_wql = k.sb("a_wql", [128, NDC * HB * 64], BF16, es=es_2a)
    a_wkvh = k.sb("a_wkvh", [128, NDC * 256], BF16, es=es_2a)
    a_wkvl = k.sb("a_wkvl", [128, NDC * 256], BF16, es=es_2a)
    a_wtmp = k.sb("a_wtmp", [128, HB * 64], es=es_2a)
    a_x = k.sb("a_x2", [128, D], es=es_2a)
    a_t = k.sb("a_t2", [128, D], es=es_2a)
    a_sq = k.sb("a_sq2", [128, D], es=es_2a)
    a_qtile = k.sb("a_qtile", [128, HB * 64], es=es_2a)
    a_kvt = k.sb("a_kvt", [128, 256], es=es_2a)
    a_cs = k.sb("a_cs", [128, 64], es=es_2a)
    a_rr = k.sb("a_rr", [128, HB * 64], es=es_2a)
    a_ms = k.sb("a_ms", [128, 16], es=es_2a)
    s_ln1r = k.sb("s_ln1r", [128, D], es=es_2a)
    s_qnr = k.sb("s_qnr", [128, 64], es=es_2a)
    s_knr = k.sb("s_knr", [128, 64], es=es_2a)
    s_r1 = k.sb("s_r1", [1, D], es=es_2a)

    for (src, dst, ww) in ((ln1_w, s_ln1r, D), (qn_w, s_qnr, 64),
                           (kn_w, s_knr, 64)):
        p.dma("sync", s_r1[:, 0:ww], src[None, :], sem="d_ini", serial=True)
        for dc in range((ww + 127) // 128):
            wN = min(128, ww - 128 * dc)
            k.t("matmul", dict(out=PSB[0][:, 0:wN], lhsT=s_ones[:],
                               rhs=s_r1[:, 128 * dc:128 * dc + wN],
                               start=True, stop=True),
                r=[s_ones, s_r1], w=[PK[0]])
            k.s("copy", dict(out=dst[:, 128 * dc:128 * dc + wN],
                             in_=PSB[0][:, 0:wN]), r=[PK[0]], w=[dst])

    for (wsrc, hdst, ldst, ncols) in ((wq, a_wqh, a_wql, HB * 64),):
        for dc in range(NDC):
            p.dma("sync", a_wtmp[:, 0:ncols],
                  wsrc[128 * dc:128 * (dc + 1), :], sem="d_wq")
            split16(k, hdst[:, ncols * dc:ncols * (dc + 1)],
                    ldst[:, ncols * dc:ncols * (dc + 1)],
                    a_wtmp[:, 0:ncols], r=[a_wtmp], w=[hdst, ldst])
    for dc in range(NDC):
        p.dma("sync", a_wtmp[:, 0:128], wk[128 * dc:128 * (dc + 1), :],
              sem="d_wq")
        p.dma("sync", a_wtmp[:, 128:256], wv[128 * dc:128 * (dc + 1), :],
              sem="d_wq")
        split16(k, a_wkvh[:, 256 * dc:256 * (dc + 1)],
                a_wkvl[:, 256 * dc:256 * (dc + 1)],
                a_wtmp[:, 0:256], r=[a_wtmp], w=[a_wkvh, a_wkvl])

    for tt in range(NQT):
        k.v("memset", dict(ap=a_v[:, 130 * tt + 64:130 * tt + 65],
                           constant=1.0), w=[a_v])
        k.v("memset", dict(ap=a_v[:, 130 * tt + 129:130 * tt + 130],
                           constant=1.0), w=[a_v])

    TH = T // 4
    for hh in range(4):
        for ttl in range(TH // 128):
            tt = hh * (TH // 128) + ttl
            p.dma("sync", a_x[:], x0[128 * tt:128 * (tt + 1), :], sem="d_x2")
            emit_ln(k, a_x, a_t, a_sq, s_st8, EPS_A, s_ln1r)
            for dc in range(NDC):
                bank, bk = PSB[dc % 2], PK[dc % 2]
                k.t("transpose", dict(out=bank[:, 0:128],
                                      in_=a_t[:, 128 * dc:128 * (dc + 1)],
                                      identity=sid[:]), r=[a_t, sid], w=[bk])
                fo = dc * TH + 128 * ttl
                split16(k, h_hnh[:, fo:fo + 128], h_hnl[:, fo:fo + 128],
                        bank[:, 0:128], r=[bk], w=[h_hnh, h_hnl])
        for ttl in range(TH // 128):
            tt = hh * (TH // 128) + ttl
            bq, bqk = PSB[2], PK[2]
            bkv, bkvk = PSB[3], PK[3]
            for dc in range(NDC):
                mm3(k, bq[:, 0:HB * 64],
                    h_hnh[:, dc * TH + 128 * ttl:dc * TH + 128 * (ttl + 1)],
                    h_hnl[:, dc * TH + 128 * ttl:dc * TH + 128 * (ttl + 1)],
                    a_wqh[:, HB * 64 * dc:HB * 64 * (dc + 1)],
                    a_wql[:, HB * 64 * dc:HB * 64 * (dc + 1)],
                    start=(dc == 0), stop=(dc == NDC - 1),
                    r=[h_hnh, h_hnl, a_wqh, a_wql], w=[bqk])
                mm3(k, bkv[:, 0:256],
                    h_hnh[:, dc * TH + 128 * ttl:dc * TH + 128 * (ttl + 1)],
                    h_hnl[:, dc * TH + 128 * ttl:dc * TH + 128 * (ttl + 1)],
                    a_wkvh[:, 256 * dc:256 * (dc + 1)],
                    a_wkvl[:, 256 * dc:256 * (dc + 1)],
                    start=(dc == 0), stop=(dc == NDC - 1),
                    r=[h_hnh, h_hnl, a_wkvh, a_wkvl], w=[bkvk])
            for g in range(2):
                k.s("copy", dict(out=a_v[:, 130 * tt + 65 * g:
                                        130 * tt + 65 * g + 64],
                                 in_=bkv[:, 128 + 64 * g:128 + 64 * (g + 1)]),
                    r=[bkvk], w=[a_v])
            p.dma("sync", a_cs[:], rope_cs[128 * tt:128 * (tt + 1), :],
                  sem="d_cs")
            for (src_bank, srck, nh, dst, nrm) in (
                    (bq, bqk, HB, a_qtile, s_qnr), (bkv, bkvk, 2, a_kvt, s_knr)):
                cosb = a_cs[:, 0:32].unsqueeze(1).broadcast_to([128, nh, 32])
                sinb = a_cs[:, 32:64].unsqueeze(1).broadcast_to([128, nh, 32])
                x1v = src_bank[:, 0:64 * nh].rearrange(
                    "p (h d) -> p h d", d=64)[:, :, 0:32]
                x2v = src_bank[:, 0:64 * nh].rearrange(
                    "p (h d) -> p h d", d=64)[:, :, 32:64]
                r1v = dst[:, 0:64 * nh].rearrange(
                    "p (h d) -> p h d", d=64)[:, :, 0:32]
                r2v = dst[:, 0:64 * nh].rearrange(
                    "p (h d) -> p h d", d=64)[:, :, 32:64]
                tv = a_rr[:, 0:64 * nh].rearrange(
                    "p (h d) -> p h d", d=64)
                t1v, t2v = tv[:, :, 0:32], tv[:, :, 32:64]
                k.v("tensor_tensor", dict(out=t1v, in0=x1v, in1=cosb,
                                          op=ALU.mult),
                    r=[srck, a_cs], w=[a_rr])
                k.v("tensor_tensor", dict(out=t2v, in0=x2v, in1=sinb,
                                          op=ALU.mult),
                    r=[srck, a_cs], w=[a_rr])
                k.v("tensor_tensor", dict(out=r1v, in0=t1v, in1=t2v,
                                          op=ALU.subtract),
                    r=[a_rr], w=[dst])
                k.v("tensor_tensor", dict(out=t1v, in0=x1v, in1=sinb,
                                          op=ALU.mult),
                    r=[srck, a_cs], w=[a_rr])
                k.v("tensor_tensor", dict(out=t2v, in0=x2v, in1=cosb,
                                          op=ALU.mult),
                    r=[srck, a_cs], w=[a_rr])
                k.v("tensor_tensor", dict(out=r2v, in0=t1v, in1=t2v,
                                          op=ALU.add),
                    r=[a_rr], w=[dst])
                dv = dst[:, 0:64 * nh].rearrange("p (h d) -> p h d", d=64)
                k.s("activation", dict(out=a_rr[:, 0:64 * nh],
                                       in_=dst[:, 0:64 * nh], func=AF.Square),
                    r=[dst], w=[a_rr])
                k.v("tensor_reduce", dict(
                    out=a_ms[:, 0:nh],
                    in_=a_rr[:, 0:64 * nh].rearrange("p (h d) -> p h d", d=64),
                    axis=AXX, op=ALU.add), r=[a_rr], w=[a_ms])
                c_i64 = k.const_col(1.0 / 64.0)
                c_ea = k.const_col(EPS_A)
                k.s("activation", dict(out=a_ms[:, 8:8 + nh],
                                       in_=a_ms[:, 0:nh], func=AF.Sqrt,
                                       scale=c_i64, bias=c_ea),
                    r=[a_ms, "k_consts"], w=[a_ms])
                k.v("reciprocal", dict(out=a_ms[:, 0:nh], in_=a_ms[:, 8:8 + nh]),
                    r=[a_ms], w=[a_ms])
                rstdb = a_ms[:, 0:nh].unsqueeze(2).broadcast_to([128, nh, 64])
                k.v("tensor_tensor", dict(out=dv, in0=dv, in1=rstdb,
                                          op=ALU.mult), r=[dst, a_ms], w=[dst])
                nrmb = nrm[:].unsqueeze(1).broadcast_to([128, nh, 64])
                k.v("tensor_tensor", dict(out=dv, in0=dv, in1=nrmb,
                                          op=ALU.mult), r=[dst, nrm], w=[dst])
            for h in range(HB):
                bank, bk = PSB[4 + (h % 2)], PK[4 + (h % 2)]
                k.t("transpose", dict(out=bank[0:64, 0:128],
                                      in_=a_qtile[:, 64 * h:64 * (h + 1)],
                                      identity=sid[:]),
                    r=[a_qtile, sid], w=[bk])
                split16(k, a_qsth[:], a_qstl[:],
                        bank[0:64, 0:128], r=[bk], w=[a_qsth, a_qstl])
                qv_h = qt_sp_h[h, :].rearrange("(d t) -> d t", d=64)
                qv_l = qt_sp_l[h, :].rearrange("(d t) -> d t", d=64)
                p.dma("sync", qv_h[:, 128 * tt:128 * (tt + 1)], a_qsth[:],
                      sem="d_qsh")
                p.dma("sync", qv_l[:, 128 * tt:128 * (tt + 1)], a_qstl[:],
                      sem="d_qsl")
            for g in range(2):
                bank, bk = PSB[6], PK[6]
                k.t("transpose", dict(out=bank[0:64, 0:128],
                                      in_=a_kvt[:, 64 * g:64 * (g + 1)],
                                      identity=sid[:]),
                    r=[a_kvt, sid], w=[bk])
                fo = g * T + 128 * tt
                split16(k, a_kth[0:64, fo:fo + 128],
                        a_ktl[0:64, fo:fo + 128],
                        bank[0:64, 0:128], r=[bk], w=[a_kth, a_ktl])
    for g in range(2):
        p.dma("sync", a_kth[64:128, g * T:(g + 1) * T],
              a_kth[0:64, g * T:(g + 1) * T], sem=f"d_kdh{g}")
        p.dma("sync", a_ktl[64:128, g * T:(g + 1) * T],
              a_ktl[0:64, g * T:(g + 1) * T], sem=f"d_kdl{g}")
    es_2a.close()
    p.barrier()

    es_2b = contextlib.ExitStack()
    a_pt = k.sb("a_pt", [128, 512], es=es_2b)
    a_on = k.sb("a_on", [128, 512], es=es_2b)
    a_l1 = k.sb("a_l1", [1, 512], es=es_2b)
    a_gg = k.sb("a_gg", [64, 512], es=es_2b)
    s_wg2 = k.sb("s_wg2", [64, 64], es=es_2b)
    c_s8 = k.const_col(0.125)
    p.dma("sync", s_wg2[:], wg, sem="d_ini", serial=True)

    NQM = T // 512
    QW_H = [k.sb(f"a_qwh{i}", [128, 512], BF16, es=es_2b) for i in range(2)]
    QW_L = [k.sb(f"a_qwl{i}", [128, 512], BF16, es=es_2b) for i in range(2)]
    PT = [a_pt, k.sb("a_pt2", [128, 512], es=es_2b)]
    for h in range(HB):
        g = h // 4
        pr_h = (h % 2) * 64
        for qm in range(NQM):
            qpar = (h * NQM + qm) % 2
            a_qh, a_ql = QW_H[qpar], QW_L[qpar]
            nkc = 4 * (qm + 1)
            bo, bok = PSB[7], PK[7]
            qv_h = qt_sp_h[h, :].rearrange("(d t) -> d t", d=64)
            qv_l = qt_sp_l[h, :].rearrange("(d t) -> d t", d=64)
            p.dma("sync", a_qh[pr_h:pr_h + 64, :],
                  qv_h[:, 512 * qm:512 * (qm + 1)], sem="d_qlh" + str(qpar))
            p.dma("sync", a_ql[pr_h:pr_h + 64, :],
                  qv_l[:, 512 * qm:512 * (qm + 1)], sem="d_qll" + str(qpar))

            def emit_qk(kc):
                bs, bsk = PSB[4 + (kc % 2)], PK[4 + (kc % 2)]
                mm3(k, bs[:, 0:512],
                    a_kth[pr_h:pr_h + 64,
                          g * T + 128 * kc:g * T + 128 * (kc + 1)],
                    a_ktl[pr_h:pr_h + 64,
                          g * T + 128 * kc:g * T + 128 * (kc + 1)],
                    a_qh[pr_h:pr_h + 64, :],
                    a_ql[pr_h:pr_h + 64, :],
                    start=True, stop=True,
                    r=[a_kth, a_ktl, a_qh, a_ql], w=[bsk])

            emit_qk(0)
            for kc in range(nkc):
                bs, bsk = PSB[4 + (kc % 2)], PK[4 + (kc % 2)]
                pt = PT[kc % 2]
                ptk = _key(pt[:])
                k.s("activation", dict(out=pt[:], in_=bs[:, 0:512],
                                       func=AF.Exp, scale=c_s8),
                    r=[bsk, "k_consts"], w=[ptk])
                if kc + 1 < nkc:
                    emit_qk(kc + 1)
                j = kc - 4 * qm
                if j >= 0:
                    if j > 0:
                        k.v("memset", dict(ap=pt[:, 0:128 * j], constant=0.0),
                            r=[], w=[ptk])
                    k.v("tensor_mul", dict(out=pt[:, 128 * j:128 * (j + 1)],
                                           in0=pt[:, 128 * j:128 * (j + 1)],
                                           in1=s_triu[:]),
                        r=[s_triu, ptk], w=[ptk])
                k.t("matmul", dict(out=bo[0:65, 0:512],
                                   lhsT=a_v[:, 130 * kc + 65 * g:
                                            130 * kc + 65 * (g + 1)],
                                   rhs=pt[:], start=(kc == 0),
                                   stop=(kc == nkc - 1)),
                    r=[a_v, ptk], w=[bok])
            k.v("tensor_copy", dict(out=a_on[0:65, :], in_=bo[0:65, 0:512]),
                r=[bok], w=[a_on])
            k.v("reciprocal", dict(out=a_l1[:], in_=a_on[64:65, :]),
                r=[a_on], w=[a_l1])
            k.t("matmul", dict(out=PSB[3][0:64, 0:512], lhsT=s_ones[:, 0:64],
                               rhs=a_l1[:], start=True, stop=True),
                r=[s_ones, a_l1], w=[PK[3]])
            k.v("tensor_tensor", dict(out=a_on[0:64, :], in0=a_on[0:64, :],
                                      in1=PSB[3][0:64, 0:512], op=ALU.mult),
                r=[a_on, PK[3]], w=[a_on])
            k.t("matmul", dict(out=PSB[2][0:64, 0:512],
                               lhsT=s_wg2[:],
                               rhs=a_on[0:64, :], start=True, stop=True),
                r=[s_wg2, a_on], w=[PK[2]])
            k.s("activation", dict(out=a_gg[:], in_=PSB[2][0:64, 0:512],
                                   func=AF.Sigmoid), r=[PK[2]], w=[a_gg])
            fo_h = (h // 2) * T
            k.v("tensor_tensor", dict(
                out=a_gt[pr_h:pr_h + 64, fo_h + 512 * qm:fo_h + 512 * (qm + 1)],
                in0=a_on[0:64, :], in1=a_gg[:], op=ALU.mult),
                r=[a_on, a_gg], w=[a_gt])
    es_2b.close()
    es_2q.close()
    p.barrier()
    es_2b = contextlib.ExitStack()
    a_gth = k.sb("a_gth", [128, 4 * T], BF16, es=es_2b)
    a_gtl = k.sb("a_gtl", [128, 4 * T], BF16, es=es_2b)
    a_woh = k.sb("a_woh", [128, 4 * 128], BF16, es=es_2b)
    a_wol = k.sb("a_wol", [128, 4 * 128], BF16, es=es_2b)
    a_wot = k.sb("a_wot", [128, 128], es=es_2b)
    a_mo2 = k.sb("a_mo2", [128, 512], es=es_2b)
    a_mo3 = k.sb("a_mo3", [128, 128], es=es_2b)
    split16(k, a_gth[:], a_gtl[:], a_gt[:], r=[a_gt], w=[a_gth, a_gtl])
    for dc in range(NDC):
        for gg in range(4):
            p.dma("sync", a_wot[:], wo_r[gg, :, 128 * dc:128 * (dc + 1)],
                  sem="d_wo")
            split16(k, a_woh[:, 128 * gg:128 * (gg + 1)],
                    a_wol[:, 128 * gg:128 * (gg + 1)], a_wot[:],
                    r=[a_wot], w=[a_woh, a_wol])
        for ts in range(T // 512):
            bank, bk = PSB[0 + (ts % 2)], PK[0 + (ts % 2)]
            for gg in range(4):
                mm3(k, bank[:, 0:512],
                    a_woh[:, 128 * gg:128 * (gg + 1)],
                    a_wol[:, 128 * gg:128 * (gg + 1)],
                    a_gth[:, gg * T + 512 * ts:gg * T + 512 * (ts + 1)],
                    a_gtl[:, gg * T + 512 * ts:gg * T + 512 * (ts + 1)],
                    start=(gg == 0), stop=(gg == 3),
                    r=[a_woh, a_wol, a_gth, a_gtl], w=[bk])
            k.s("copy", dict(out=a_mo2[:], in_=bank[:, 0:512]), r=[bk],
                w=[a_mo2])
            for sub in range(4):
                tb2, tbk2 = PSB[6], PK[6]
                k.t("transpose", dict(out=tb2[:, 0:128],
                                      in_=a_mo2[:, 128 * sub:128 * (sub + 1)],
                                      identity=sid[:]), r=[a_mo2, sid], w=[tbk2])
                k.s("copy", dict(out=a_mo3[:, 0:128], in_=tb2[:, 0:128]),
                    r=[tbk2], w=[a_mo3])
                p.dma("sync",
                      cc2_in[0, :].rearrange("(t d) -> t d", d=D)[
                          512 * ts + 128 * sub:512 * ts + 128 * (sub + 1),
                          128 * dc:128 * (dc + 1)],
                      a_mo3[:, 0:128], sem="d_cc2")
    k.emit_cc("ReduceScatter", ALU.add, PAIRS, cc2_in, cc2_out)
    if "attn_mm" in k.cfg["DEBUG_OUT"]:
        o = k.dram("dbg_attn_mm", [TC, D], kind="ExternalOutput")
        k.dbg_outs["attn_mm"] = [TC, D]
        s_m1 = k.sb("s_m1b", [128, D], es=es_2b)
        for tt in range(TC // 128):
            p.dma("sync", s_m1[:], cc2_out[0, :].rearrange(
                "(t d) -> t d", d=D)[128 * tt:128 * (tt + 1), :], sem="d_dbg1")
            p.dma("gpsimd", o[128 * tt:128 * (tt + 1), :], s_m1[:], sem="d_dbg")
    es_2b.close()
    es_2.close()
    p.barrier()

    if cfg.get("STOP_AFTER") == 2:
        return k, locals()
    NTC = TC // 128
    CAP = k.CAP
    NSLOT = 8 * CAP
    NSL = CAP // 128
    HID = k.HID
    NUC = HID // 128
    x0o = k.dram("x0o", [TC, D], kind="ExternalInput")
    ln2_w = k.dram("ln2_w", [D], kind="ExternalInput")
    wr = k.dram("wr", [D, 8], kind="ExternalInput")
    w1 = k.dram("w1", [8, D, 2 * HID], kind="ExternalInput")
    w2 = k.dram("w2", [8, HID, D], kind="ExternalInput")
    moe_in = k.dram("moe_in", [NSLOT, D])
    moe_out = k.dram("moe_out", [NSLOT, D])
    erow8 = k.inline("erow8", np.tile(np.arange(8, dtype=np.float32), (128, 1)))
    lstrict = k.inline("lstrict",
                       np.triu(np.ones((128, 128), np.float32), 1).T.copy())

    es_3 = contextlib.ExitStack()
    q_x1 = k.sb("q_x1", [128, NTC * D], es=es_3)
    q_dst = k.sb("q_dst", [128, NTC], U32, es=es_3)

    es_3a = contextlib.ExitStack()
    b_xpp = k.sb("b_xpp", [128, NTC * D], es=es_3a)
    s_ln2r = k.sb("s_ln2r", [128, D], es=es_3a)
    s_wr = k.sb("s_wr", [128, NDC * 8], es=es_3a)
    s_er8 = k.sb("s_er8", [128, 8], es=es_3a)
    s_ls = k.sb("s_ls", [128, 128], es=es_3a)
    a_x = k.sb("a_x3", [128, D], es=es_3a)
    a_t = k.sb("a_t3", [128, D], es=es_3a)
    a_sq = k.sb("a_sq3", [128, D], es=es_3a)
    a_c1 = k.sb("a_c1", [128, D], es=es_3a)
    a_h2 = k.sb("a_h2", [128, D], es=es_3a)
    a_h2T = k.sb("a_h2T", [128, NDC * 128], es=es_3a)
    a_mx8 = k.sb("a_mx8", [128, 8], es=es_3a)
    a_ix8 = k.sb("a_ix8", [128, 8], U32, es=es_3a)
    a_ef = k.sb("a_ef", [128, 1], es=es_3a)
    a_oh = k.sb("a_oh", [128, 8], es=es_3a)
    a_tc = k.sb("a_tc", [128, 8], es=es_3a)
    a_rk = k.sb("a_rk", [128, 1], es=es_3a)
    a_di = k.sb("a_di", [128, 1], U32, es=es_3a)
    s_base = k.sb("s_base", [1, 8], es=es_3a)
    a_zz = k.sb("a_zz", [128, D], es=es_3a)

    p.dma("sync", s_er8[:], erow8, sem="d_ini", serial=True)
    p.dma("sync", s_ls[:], lstrict, sem="d_ini", serial=True)
    p.dma("sync", s_wr[:].rearrange("p (c m) -> p c m", m=8),
          wr.rearrange("(c p) m -> p c m", p=128), sem="d_ini", serial=True)
    s_r2 = k.sb("s_r2", [1, D], es=es_3a)
    p.dma("sync", s_r2[:], ln2_w[None, :], sem="d_ini", serial=True)
    for dc in range(NDC):
        k.t("matmul", dict(out=PSB[0][:, 0:128], lhsT=s_ones[:],
                           rhs=s_r2[:, 128 * dc:128 * (dc + 1)],
                           start=True, stop=True), r=[s_ones, s_r2], w=[PK[0]])
        k.s("copy", dict(out=s_ln2r[:, 128 * dc:128 * (dc + 1)],
                         in_=PSB[0][:, 0:128]), r=[PK[0]], w=[s_ln2r])
    k.v("memset", dict(ap=s_base[:], constant=0.0), w=[s_base])
    k.v("memset", dict(ap=a_zz[:], constant=0.0), w=[a_zz])
    for sz in range(NSLOT // 128):
        p.dma("sync", moe_in[128 * sz:128 * (sz + 1), :], a_zz[:], sem="d_mz",
              r=[a_zz])

    c1v = cc1_out[0, :].rearrange("(t d) -> t d", d=D)
    c2v = cc2_out[0, :].rearrange("(t d) -> t d", d=D)
    for tt in range(NTC):
        sl = slice(128 * tt, 128 * (tt + 1))
        xsl = slice(D * tt, D * (tt + 1))
        p.dma("sync", a_x[:], x0o[sl, :], sem="d_x3")
        emit_ln(k, a_x, a_t, a_sq, s_st8, None, None)
        p.dma("sync", a_c1[:], c2v[sl, :], sem="d_cl")
        k.v("tensor_add", dict(out=b_xpp[:, xsl], in0=a_x[:], in1=a_c1[:]),
            r=[a_x, a_c1], w=[b_xpp])
        k.v("tensor_add", dict(out=q_x1[:, xsl], in0=b_xpp[:, xsl], in1=a_x[:]),
            r=[b_xpp, a_x], w=[q_x1])
        p.dma("sync", a_c1[:], c1v[sl, :], sem="d_cl")
        k.v("tensor_add", dict(out=q_x1[:, xsl], in0=q_x1[:, xsl], in1=a_c1[:]),
            r=[q_x1, a_c1], w=[q_x1])
        p.dma("sync", a_c1[:], x0o[sl, :], sem="d_x3b")
        k.v("tensor_add", dict(out=q_x1[:, xsl], in0=q_x1[:, xsl], in1=a_c1[:]),
            r=[q_x1, a_c1], w=[q_x1])
        k.s("activation", dict(out=a_sq[:], in_=b_xpp[:, xsl], func=AF.Square,
                               accum_out=s_st8[:, 0:1]),
            r=[b_xpp], w=[a_sq, "a_st"])
        c_invd = k.const_col(1.0 / D)
        c_ea2 = k.const_col(EPS_A)
        k.s("activation", dict(out=s_st8[:, 1:2], in_=s_st8[:, 0:1],
                               func=AF.Sqrt, scale=c_invd, bias=c_ea2),
            r=["a_st", "k_consts"], w=["a_st"])
        k.v("reciprocal", dict(out=s_st8[:, 2:3], in_=s_st8[:, 1:2]),
            r=["a_st"], w=["a_st"])
        k.v("scalar_tensor_tensor", dict(
            out=a_h2[:], in0=b_xpp[:, xsl], scalar=s_st8[:, 2:3],
            in1=s_ln2r[:], op0=ALU.mult, op1=ALU.mult),
            r=[b_xpp, "a_st", s_ln2r], w=[a_h2])
        for dc in range(NDC):
            bank, bk = PSB[1 + (dc % 2)], PK[1 + (dc % 2)]
            k.t("transpose", dict(out=bank[:, 0:128],
                                  in_=a_h2[:, 128 * dc:128 * (dc + 1)],
                                  identity=sid[:]), r=[a_h2, sid], w=[bk])
            k.s("copy", dict(out=a_h2T[:, 128 * dc:128 * (dc + 1)],
                             in_=bank[:, 0:128]), r=[bk], w=[a_h2T])
        for dc in range(NDC):
            k.t("matmul", dict(out=PSB[3][:, 0:8],
                               lhsT=a_h2T[:, 128 * dc:128 * (dc + 1)],
                               rhs=s_wr[:, 8 * dc:8 * (dc + 1)],
                               start=(dc == 0), stop=(dc == NDC - 1)),
                r=[a_h2T, s_wr], w=[PK[3]])
        k.v("tensor_copy", dict(out=a_tc[:], in_=PSB[3][:, 0:8]),
            r=[PK[3]], w=[a_tc])
        k.v("max", dict(out=a_mx8[:], in_=a_tc[:]), r=[a_tc], w=[a_mx8])
        k.v("max_index", dict(out=a_ix8[:], in_max=a_mx8[:], in_values=a_tc[:]),
            r=[a_mx8, a_tc], w=[a_ix8])
        k.v("tensor_copy", dict(out=a_ef[:], in_=a_ix8[:, 0:1]),
            r=[a_ix8], w=[a_ef])
        efb = a_ef[:].broadcast_to([128, 8])
        k.v("tensor_tensor", dict(out=a_oh[:], in0=efb, in1=s_er8[:],
                                  op=ALU.is_equal), r=[a_ef, s_er8], w=[a_oh])
        k.t("matmul", dict(out=PSB[3][:, 0:8], lhsT=s_ls[:], rhs=a_oh[:],
                           start=True, stop=True), r=[s_ls, a_oh], w=[PK[3]])
        k.v("tensor_copy", dict(out=a_tc[:], in_=PSB[3][:, 0:8]),
            r=[PK[3]], w=[a_tc])
        k.t("matmul", dict(out=PSB[4][0:1, 0:8], lhsT=ones_colT(k),
                           rhs=a_oh[:], start=True, stop=True),
            r=["k_onec", a_oh], w=[PK[4]])
        k.t("matmul", dict(out=PSB[5][:, 0:8], lhsT=s_ones[:], rhs=s_base[:],
                           start=True, stop=True), r=[s_ones, s_base],
            w=[PK[5]])
        k.v("tensor_tensor", dict(out=a_tc[:], in0=a_tc[:],
                                  in1=PSB[5][:, 0:8], op=ALU.add),
            r=[a_tc, PK[5]], w=[a_tc])
        k.v("tensor_tensor", dict(out=a_tc[:], in0=a_tc[:], in1=a_oh[:],
                                  op=ALU.mult), r=[a_tc, a_oh], w=[a_tc])
        k.v("tensor_reduce", dict(out=a_rk[:], in_=a_tc[:], axis=AXX,
                                  op=ALU.add), r=[a_tc], w=[a_rk])
        k.v("scalar_tensor_tensor", dict(
            out=a_rk[:], in0=a_ef[:], scalar=float(CAP), in1=a_rk[:],
            op0=ALU.mult, op1=ALU.add), r=[a_ef, a_rk], w=[a_rk])
        k.v("tensor_scalar_min", dict(out=a_rk[:], in0=a_rk[:],
                                      scalar1=float(NSLOT - 1)),
            r=[a_rk], w=[a_rk])
        k.v("tensor_copy", dict(out=a_di[:], in_=a_rk[:]), r=[a_rk], w=[a_di])
        k.v("tensor_copy", dict(out=q_dst[:, tt:tt + 1], in_=a_di[:]),
            r=[a_di], w=[q_dst])
        k.v("tensor_tensor", dict(out=s_base[:], in0=s_base[:],
                                  in1=PSB[4][0:1, 0:8], op=ALU.add),
            r=[s_base, PK[4]], w=[s_base])
        p.idma("gpsimd", moe_in, (a_di[:], 0), a_h2[:], None, sem="d_sc",
               r=[], w=[])
    es_3a.close()
    p.barrier()

    es_3b = contextlib.ExitStack()
    XET = [k.sb(f"e_xeT{i}", [128, NDC * CAP], es=es_3b) for i in range(2)]
    ACT2 = [k.sb("e_act0", [128, NUC * CAP], es=es_3b)] * 2
    W1A = [k.sb(f"e_w1a{i}", [128, NDC * 128], es=es_3b) for i in range(2)]
    W1B = [k.sb(f"e_w1b{i}", [128, NDC * 128], es=es_3b) for i in range(2)]
    W2B = [k.sb(f"e_w2_{i}", [128, NUC * 128], es=es_3b) for i in range(2)]
    SIG2 = [k.sb(f"e_sig{i}", [128, CAP], es=es_3b) for i in range(2)]
    YT2 = [k.sb("e_yT0", [128, NDC * CAP], es=es_3b)] * 2
    YTT = [k.sb(f"e_yt{i}", [128, D], es=es_3b) for i in range(2)]
    YTG = [k.sb(f"e_ytg{i}", [128, D], es=es_3b) for i in range(2)]
    for e in range(8):
        ep = e % 2
        e_xeT, e_act, e_sig = XET[ep], ACT2[ep], SIG2[ep]
        e_yT, e_yt, e_ytg = YT2[ep], YTT[ep], YTG[ep]
        for sl2 in range(NSL):
            p.dma("sync", e_yt[:], moe_in[e * CAP + 128 * sl2:
                                          e * CAP + 128 * (sl2 + 1), :],
                  sem="d_mi" + str(ep))
            for dc in range(NDC):
                bank, bk = PSB[0 + (dc % 2)], PK[0 + (dc % 2)]
                k.t("transpose", dict(out=bank[:, 0:128],
                                      in_=e_yt[:, 128 * dc:128 * (dc + 1)],
                                      identity=sid[:]), r=[e_yt, sid], w=[bk])
                k.s("copy", dict(
                    out=e_xeT[:, CAP * dc + 128 * sl2:CAP * dc + 128 * (sl2 + 1)],
                    in_=bank[:, 0:128]), r=[bk], w=[e_xeT])
        for uc in range(NUC):
            wa, wb = W1A[(e * NUC + uc) % 2], W1B[(e * NUC + uc) % 2]
            p.dma("sync", wa[:].rearrange("p (c m) -> p c m", m=128),
                  w1[e, :, 128 * uc:128 * (uc + 1)].rearrange(
                      "(c p) m -> p c m", p=128), sem="d_w1a" + str((e * NUC + uc) % 2))
            p.dma("sync", wb[:].rearrange("p (c m) -> p c m", m=128),
                  w1[e, :, HID + 128 * uc:HID + 128 * (uc + 1)].rearrange(
                      "(c p) m -> p c m", p=128), sem="d_w1b" + str((e * NUC + uc) % 2))
            bka, bkb = PSB[2 + 2 * (uc % 2)], PSB[3 + 2 * (uc % 2)]
            kka, kkb = PK[2 + 2 * (uc % 2)], PK[3 + 2 * (uc % 2)]
            for dc in range(NDC):
                k.t("matmul", dict(out=bka[:, 0:CAP],
                                   lhsT=wa[:, 128 * dc:128 * (dc + 1)],
                                   rhs=e_xeT[:, CAP * dc:CAP * (dc + 1)],
                                   start=(dc == 0), stop=(dc == NDC - 1)),
                    r=[wa, e_xeT], w=[kka])
                k.t("matmul", dict(out=bkb[:, 0:CAP],
                                   lhsT=wb[:, 128 * dc:128 * (dc + 1)],
                                   rhs=e_xeT[:, CAP * dc:CAP * (dc + 1)],
                                   start=(dc == 0), stop=(dc == NDC - 1)),
                    r=[wb, e_xeT], w=[kkb])
            k.s("activation", dict(out=e_sig[:], in_=bkb[:, 0:CAP],
                                   func=AF.Sigmoid), r=[kkb], w=[e_sig])
            k.v("tensor_tensor", dict(out=e_act[:, CAP * uc:CAP * (uc + 1)],
                                      in0=bka[:, 0:CAP], in1=e_sig[:],
                                      op=ALU.mult), r=[kka, e_sig], w=[e_act])
        for dcol in range(NDC):
            w2b = W2B[(e * NDC + dcol) % 2]
            p.dma("sync", w2b[:].rearrange("p (c m) -> p c m", m=128),
                  w2[e, :, 128 * dcol:128 * (dcol + 1)].rearrange(
                      "(c p) m -> p c m", p=128), sem="d_w2" + str((e * NDC + dcol) % 2))
            bank, bk = PSB[6 + (dcol % 2)], PK[6 + (dcol % 2)]
            for hc in range(NUC):
                k.t("matmul", dict(out=bank[:, 0:CAP],
                                   lhsT=w2b[:, 128 * hc:128 * (hc + 1)],
                                   rhs=e_act[:, CAP * hc:CAP * (hc + 1)],
                                   start=(hc == 0), stop=(hc == NUC - 1)),
                    r=[w2b, e_act], w=[bk])
            k.s("copy", dict(out=e_yT[:, CAP * dcol:CAP * (dcol + 1)],
                             in_=bank[:, 0:CAP]), r=[bk], w=[e_yT])
        for sl2 in range(NSL):
            for dc in range(NDC):
                bank, bk = PSB[0 + (dc % 2)], PK[0 + (dc % 2)]
                k.t("transpose", dict(
                    out=bank[:, 0:128],
                    in_=e_yT[:, CAP * dc + 128 * sl2:CAP * dc + 128 * (sl2 + 1)],
                    identity=sid[:]), r=[e_yT, sid], w=[bk])
                k.s("copy", dict(out=e_ytg[:, 128 * dc:128 * (dc + 1)],
                                 in_=bank[:, 0:128]), r=[bk], w=[e_ytg])
            p.dma("sync", moe_out[e * CAP + 128 * sl2:
                                  e * CAP + 128 * (sl2 + 1), :], e_ytg[:],
                  sem="d_mo" + str(ep))
    es_3b.close()
    p.barrier()

    es_3c = contextlib.ExitStack()
    g_y = k.sb("g_y", [128, D], es=es_3c)
    g_i = k.sb("g_i", [128, 1], U32, es=es_3c)
    for tt in range(NTC):
        xsl = slice(D * tt, D * (tt + 1))
        k.v("tensor_copy", dict(out=g_i[:], in_=q_dst[:, tt:tt + 1]),
            r=[q_dst], w=[g_i])
        p.idma("gpsimd", g_y[:], None, moe_out, (g_i[:], 0), sem="d_gy",
               r=[], w=[])
        k.v("tensor_add", dict(out=q_x1[:, xsl], in0=q_x1[:, xsl], in1=g_y[:]),
            r=[q_x1, g_y], w=[q_x1])
    k.debug("x1", q_x1[:], [128, NTC * D])
    es_3c.close()
    p.barrier()

    if cfg.get("STOP_AFTER") == 3:
        k.phase_stacks.append(es_3)
        return k, locals()
    a_param = k.dram("a_param", [D], kind="ExternalInput")
    w_in = k.dram("w_in", [8, 128, 128], kind="ExternalInput")
    w_a = k.dram("w_a", [8, 128, 128], kind="ExternalInput")
    is_odd = k.dram("is_odd", [1, 1], kind="ExternalInput")
    wug = k.dram("wug", [2, k.I3, D], kind="ExternalInput")
    wd = k.dram("wd", [D, k.I3], kind="ExternalInput")
    I3 = k.I3
    NI3 = I3 // 128
    NTS = TC // 512 if TC >= 512 else 1
    WTS = min(512, TC)
    cc3_in = k.dram("cc3_in", [1, 8])
    cc3_out = k.dram("cc3_out", [8, 8])
    cc4_in = k.dram("cc4_in", [1, D])
    cc4_out = k.dram("cc4_out", [1, D])
    cc5_in = k.dram("cc5_in", [1, 8])
    cc5_out = k.dram("cc5_out", [8, 8])
    cc6_in = k.dram("cc6_in", [1, 8])
    cc6_out = k.dram("cc6_out", [8, 8])
    cc7_in = k.dram("cc7_in", [1, 8])
    cc7_out = k.dram("cc7_out", [8, 8])
    ab_spill = k.dram("ab_spill", [2 * D, TC])
    up_spill = k.dram("up_spill", [I3, TC])
    gate_spill = k.dram("gate_spill", [I3, TC])
    g2_spill = k.dram("g2_spill", [I3, TC])
    op_spill = k.dram("op_spill", [I3, TC])
    wqh_sp = k.dram("wqh_sp", [3 * I3, D], BF16)
    wql_sp = k.dram("wql_sp", [3 * I3, D], BF16)

    WTS0 = min(512, TC)
    es_4a = contextlib.ExitStack()
    s_sc = k.sb("s_sc", [1, 16], es=es_4a)
    s_scr = k.sb("s_scr", [128, 12], es=es_4a)
    g_cc = k.sb("g_cc", [1, 16], es=es_4a)
    UTB = [k.sb(f"u_t{i}", [128, WTS0], es=es_4a) for i in range(2)]
    u_mx = k.sb("u_mx", [128, 4], es=es_4a)
    USG = [k.sb(f"u_sg{i}", [128, WTS0], es=es_4a) for i in range(2)]
    UIT = [k.sb(f"u_it{i}", [128, WTS0], I32, es=es_4a) for i in range(2)]
    u_t, u_sg, u_it = UTB[0], USG[0], UIT[0]
    xq_sp_h = k.dram("xq_sp_h", [D, TC], BF16)
    xq_sp_l = k.dram("xq_sp_l", [D, TC], BF16)
    f_xqh = k.sb("f_xqh", [128, WTS0], BF16, es=es_4a)
    f_xql = k.sb("f_xql", [128, WTS0], BF16, es=es_4a)
    xq_th = k.sb("xq_th", [128, NDC * WTS0], BF16, es=es_4a)
    xq_tl = k.sb("xq_tl", [128, NDC * WTS0], BF16, es=es_4a)
    es_4f = contextlib.ExitStack()
    f_xn1 = k.sb("f_xn1", [128, NDC * TC], es=es_4f)
    es_4b = contextlib.ExitStack()
    a_x = k.sb("a_x4", [128, D], es=es_4b)
    a_t = k.sb("a_t4", [128, D], es=es_4b)
    a_sq = k.sb("a_sq4", [128, D], es=es_4b)
    a_am = k.sb("a_am", [128, 2], es=es_4b)
    for tt in range(NTC):
        xsl = slice(D * tt, D * (tt + 1))
        k.v("tensor_copy", dict(out=a_x[:], in_=q_x1[:, xsl]),
            r=[q_x1], w=[a_x])
        emit_ln(k, a_x, a_t, a_sq, s_st8, None, None)
        k.v("tensor_reduce", dict(out=a_am[:, 0:1], in_=a_x[:], axis=AXX,
                                  op=ALU.max, apply_absolute_value=True),
            r=[a_x], w=[a_am])
        if tt == 0:
            k.v("tensor_copy", dict(out=a_am[:, 1:2], in_=a_am[:, 0:1]),
                r=[a_am], w=[a_am])
        else:
            k.v("tensor_tensor", dict(out=a_am[:, 1:2], in0=a_am[:, 1:2],
                                      in1=a_am[:, 0:1], op=ALU.max),
                r=[a_am], w=[a_am])
        for dc in range(NDC):
            bank, bk = PSB[dc % 2], PK[dc % 2]
            k.t("transpose", dict(out=bank[:, 0:128],
                                  in_=a_x[:, 128 * dc:128 * (dc + 1)],
                                  identity=sid[:]), r=[a_x, sid], w=[bk])
            k.s("copy", dict(out=f_xn1[:, dc * TC + 128 * tt:
                                       dc * TC + 128 * (tt + 1)],
                             in_=bank[:, 0:128]), r=[bk], w=[f_xn1])
    if cfg.get("STOP_AFTER") == 31:
        k.phase_stacks.append(es_3)
        k.phase_stacks.append(es_4a)
        k.phase_stacks.append(es_4f)
        k.phase_stacks.append(es_4b)
        return k, locals()
    k.t("transpose", dict(out=PSB[2][0:1, 0:128], in_=a_am[:, 1:2],
                          identity=sid[:]), r=[a_am, sid], w=[PK[2]])
    k.v("tensor_reduce", dict(out=s_sc[:, 0:1], in_=PSB[2][0:1, 0:128],
                              axis=AXX, op=ALU.max), r=[PK[2]], w=[s_sc])
    k.v("memset", dict(ap=s_sc[:, 1:8], constant=0.0), w=[s_sc])
    p.dma("sync", cc3_in[0, :], s_sc[0:1, 0:8], sem="d_ccx")
    k.emit_cc("AllGather", ALU.bypass, ALL8, cc3_in, cc3_out)

    if cfg.get("STOP_AFTER") == 32:
        k.phase_stacks.append(es_3)
        k.phase_stacks.append(es_4a)
        k.phase_stacks.append(es_4f)
        k.phase_stacks.append(es_4b)
        return k, locals()
    s_sptmp2 = k.sb("s_sptmp2", [128, 512], es=es_4b)
    b_car = k.sb("b_car", [128, 16], es=es_4b)
    s_odd = k.sb("s_odd", [128, 1], es=es_4b)
    s_odd1 = k.sb("s_odd1", [1, 1], es=es_4b)
    p.dma("sync", s_odd1[:], is_odd, sem="d_io")
    k.t("matmul", dict(out=PSB[7][:, 0:1], lhsT=s_ones[:], rhs=s_odd1[:],
                       start=True, stop=True), r=[s_ones, s_odd1], w=[PK[7]])
    k.s("copy", dict(out=s_odd[:], in_=PSB[7][:, 0:1]), r=[PK[7]], w=[s_odd])
    BGA = [k.sb(f"b_ga{i}", [128, TC], es=es_4b) for i in range(2)]
    BBB = [k.sb(f"b_bb{i}", [128, TC], es=es_4b) for i in range(2)]
    BWB = [k.sb(f"b_wblk{i}", [128, 256], es=es_4b) for i in range(2)]
    b_spa = k.sb("b_spa", [128, NDC], es=es_4b)
    b_spt = k.sb("b_spt", [128, NDC], es=es_4b)
    p.dma("sync", b_spa[:], a_param.rearrange("(n p) -> p n", p=128),
          sem="d_ap", loose=True)
    softplus_inplace(k, b_spa[:], b_spt[:], b_spa, b_spt)
    k.v("tensor_scalar_mul", dict(out=b_spa[:], in0=b_spa[:], scalar1=-8.0),
        r=[b_spa], w=[b_spa])
    for dt in range(NDC):
        b_ga, b_bb, b_wblk = BGA[dt % 2], BBB[dt % 2], BWB[dt % 2]
        p.dma("sync", b_wblk[:, 0:128], w_in[dt], sem="d_wb" + str(dt % 2))
        p.dma("sync", b_wblk[:, 128:256], w_a[dt], sem="d_wb2" + str(dt % 2))
        for ts in range(NTS):
            bank, bk = PSB[2 + (ts % 2)], PK[2 + (ts % 2)]
            bank2, bk2 = PSB[4 + (ts % 2)], PK[4 + (ts % 2)]
            k.t("matmul", dict(out=bank[:, 0:WTS], lhsT=b_wblk[:, 0:128],
                               rhs=f_xn1[:, dt * TC + WTS * ts:
                                         dt * TC + WTS * (ts + 1)],
                               start=True, stop=True),
                r=[b_wblk, f_xn1], w=[bk])
            k.t("matmul", dict(out=bank2[:, 0:WTS], lhsT=b_wblk[:, 128:256],
                               rhs=f_xn1[:, dt * TC + WTS * ts:
                                          dt * TC + WTS * (ts + 1)],
                               start=True, stop=True),
                r=[b_wblk, f_xn1], w=[bk2])
            k.s("activation", dict(out=b_ga[:, WTS * ts:WTS * (ts + 1)],
                                   in_=bank2[:, 0:WTS], func=AF.Sigmoid),
                r=[bk2], w=[b_ga])
            k.v("tensor_scalar", dict(
                out=b_ga[:, WTS * ts:WTS * (ts + 1)],
                in0=b_ga[:, WTS * ts:WTS * (ts + 1)],
                scalar1=b_spa[:, dt:dt + 1], scalar2=None, op0=ALU.mult),
                r=[b_ga, b_spa], w=[b_ga])
            k.s("activation", dict(out=b_bb[:, WTS * ts:WTS * (ts + 1)],
                                   in_=b_ga[:, WTS * ts:WTS * (ts + 1)],
                                   func=AF.Exp, scale=2.0), r=[b_ga], w=[b_bb])
            k.v("tensor_scalar", dict(
                out=b_bb[:, WTS * ts:WTS * (ts + 1)],
                in0=b_bb[:, WTS * ts:WTS * (ts + 1)],
                scalar1=-1.0, scalar2=1.0, op0=ALU.mult, op1=ALU.add),
                r=[b_bb], w=[b_bb])
            k.s("activation", dict(out=b_bb[:, WTS * ts:WTS * (ts + 1)],
                                   in_=b_bb[:, WTS * ts:WTS * (ts + 1)],
                                   func=AF.Sqrt), r=[b_bb], w=[b_bb])
            k.s("activation", dict(out=s_sptmp2[:, 0:WTS], in_=bank[:, 0:WTS],
                                   func=AF.Sigmoid), r=[bk], w=[s_sptmp2])
            k.v("tensor_tensor", dict(out=s_sptmp2[:, 0:WTS],
                                      in0=s_sptmp2[:, 0:WTS],
                                      in1=f_xn1[:, dt * TC + WTS * ts:
                                                dt * TC + WTS * (ts + 1)],
                                      op=ALU.mult), r=[s_sptmp2, f_xn1],
                w=[s_sptmp2])
            k.v("tensor_tensor", dict(out=b_bb[:, WTS * ts:WTS * (ts + 1)],
                                      in0=b_bb[:, WTS * ts:WTS * (ts + 1)],
                                      in1=s_sptmp2[:, 0:WTS], op=ALU.mult),
                r=[b_bb, s_sptmp2], w=[b_bb])
            k.s("activation", dict(out=b_ga[:, WTS * ts:WTS * (ts + 1)],
                                   in_=b_ga[:, WTS * ts:WTS * (ts + 1)],
                                   func=AF.Exp), r=[b_ga], w=[b_ga])
        p.dma("sync", ab_spill[128 * dt:128 * (dt + 1), :], b_ga[:],
              sem="d_ab")
        p.dma("sync", ab_spill[D + 128 * dt:D + 128 * (dt + 1), :], b_bb[:],
              sem="d_ab")
        k.v("tensor_tensor_scan", dict(out=b_bb[:], data0=b_ga[:],
                                       data1=b_bb[:], initial=0.0,
                                       op0=ALU.mult, op1=ALU.add),
            r=[b_ga, b_bb], w=[b_bb])
        k.v("tensor_copy", dict(out=b_car[:, dt:dt + 1],
                                in_=b_bb[:, TC - 1:TC]), r=[b_bb], w=[b_car])
    if cfg.get("STOP_AFTER") == 33:
        k.phase_stacks.append(es_3)
        k.phase_stacks.append(es_4a)
        k.phase_stacks.append(es_4f)
        k.phase_stacks.append(es_4b)
        return k, locals()
    k.v("tensor_scalar", dict(out=b_car[:, 8:16], in0=b_car[:, 0:8],
                              scalar1=s_odd[:, 0:1], scalar2=None,
                              op0=ALU.mult), r=[b_car, s_odd], w=[b_car])
    k.v("tensor_sub", dict(out=b_car[:, 8:16], in0=b_car[:, 0:8],
                           in1=b_car[:, 8:16]), r=[b_car], w=[b_car])
    p.dma("sync", cc4_in.rearrange("a (n p) -> p (a n)", p=128),
          b_car[:, 8:16], sem="d_cc4", loose=True)
    k.emit_cc("AllReduce", ALU.add, PAIRS, cc4_in, cc4_out)
    p.dma("sync", b_car[:, 8:16], cc4_out.rearrange("a (n p) -> p (a n)", p=128),
          sem="d_cc4b", loose=True)
    k.v("tensor_scalar", dict(out=b_car[:, 8:16], in0=b_car[:, 8:16],
                              scalar1=s_odd[:, 0:1], scalar2=None,
                              op0=ALU.mult), r=[b_car, s_odd], w=[b_car])
    for dt in range(NDC):
        b_ga, b_bb = BGA[dt % 2], BBB[dt % 2]
        p.dma("sync", b_ga[:], ab_spill[128 * dt:128 * (dt + 1), :],
              sem="d_ab3" + str(dt % 2))
        p.dma("sync", b_bb[:], ab_spill[D + 128 * dt:D + 128 * (dt + 1), :],
              sem="d_ab4" + str(dt % 2))
        k.v("tensor_tensor_scan", dict(out=b_bb[:], data0=b_ga[:],
                                       data1=b_bb[:],
                                       initial=b_car[:, 8 + dt:9 + dt],
                                       op0=ALU.mult, op1=ALU.add),
            r=[b_ga, b_bb, b_car], w=[b_bb])
        for tt in range(NTC):
            bank, bk = PSB[6], PK[6]
            k.t("transpose", dict(out=bank[:, 0:128],
                                  in_=b_bb[:, 128 * tt:128 * (tt + 1)],
                                  identity=sid[:]), r=[b_bb, sid], w=[bk])
            k.v("tensor_tensor", dict(
                out=q_x1[:, D * tt + 128 * dt:D * tt + 128 * (dt + 1)],
                in0=q_x1[:, D * tt + 128 * dt:D * tt + 128 * (dt + 1)],
                in1=bank[:, 0:128], op=ALU.add), r=[q_x1, bk], w=[q_x1])
    k.debug("y1x", q_x1[:], [128, NTC * D])
    es_4b.close()
    p.barrier()
    if cfg.get("STOP_AFTER") == 4:
        k.phase_stacks.append(es_3)
        k.phase_stacks.append(es_4a)
        k.phase_stacks.append(es_4f)
        return k, locals()
    es_4c = contextlib.ExitStack()
    wq_up_h = k.dram("wq_up_h", [D, I3], BF16)
    wq_up_l = k.dram("wq_up_l", [D, I3], BF16)
    wq_g_h = k.dram("wq_g_h", [D, I3], BF16)
    wq_g_l = k.dram("wq_g_l", [D, I3], BF16)
    wq_d_h = k.dram("wq_d_h", [I3, D], BF16)
    wq_d_l = k.dram("wq_d_l", [I3, D], BF16)
    DWT = [k.sb(f"d_wt{i}", [128, I3], es=es_4c) for i in range(2)]
    DBM = [k.sb(f"d_bm{i}", [128, I3 // 32], es=es_4c) for i in range(2)]
    DBI = [k.sb(f"d_bi{i}", [128, I3 // 32], es=es_4c) for i in range(2)]
    DIT = [k.sb("d_it0", [128, I3], I32, es=es_4c)] * 2
    DTH = [k.sb(f"d_th{i}", [128, 128], BF16, es=es_4c) for i in range(2)]
    DTL = [k.sb(f"d_tl{i}", [128, 128], BF16, es=es_4c) for i in range(2)]

    def dequant_one(src_ap, n_rows, n_cols, out_h, out_l, semtag):
        nb = n_cols // 32
        for ot in range(n_rows // 128):
            d_wt, d_bm, d_bi = DWT[ot % 2], DBM[ot % 2], DBI[ot % 2]
            d_it, d_th, d_tl = DIT[ot % 2], DTH[ot % 2], DTL[ot % 2]
            p.dma("sync", d_wt[:, 0:n_cols],
                  src_ap[128 * ot:128 * (ot + 1), :], sem="d_dqw" + str(ot % 2))
            v3 = d_wt[:, 0:n_cols].rearrange("p (b e) -> p b e", e=32)
            k.v("tensor_reduce", dict(out=d_bm[:, 0:nb], in_=v3, axis=AXX,
                                      op=ALU.max, apply_absolute_value=True),
                r=[d_wt], w=[d_bm])
            k.v("tensor_scalar", dict(out=d_bm[:, 0:nb], in0=d_bm[:, 0:nb],
                                      scalar1=1.0 / 7.0, scalar2=1e-12,
                                      op0=ALU.mult, op1=ALU.max),
                r=[d_bm], w=[d_bm])
            bm3 = d_bm[:, 0:nb].unsqueeze(2).broadcast_to([128, nb, 32])
            k.v("reciprocal", dict(out=d_bi[:, 0:nb], in_=d_bm[:, 0:nb]),
                r=[d_bm], w=[d_bi])
            bi3 = d_bi[:, 0:nb].unsqueeze(2).broadcast_to([128, nb, 32])
            k.v("tensor_tensor", dict(out=v3, in0=v3, in1=bi3, op=ALU.mult),
                r=[d_wt, d_bi], w=[d_wt])
            emit_rne(k, d_wt[:, 0:n_cols], d_wt, d_it[:, 0:n_cols], d_it)
            k.v("tensor_tensor", dict(out=v3, in0=v3, in1=bm3, op=ALU.mult),
                r=[d_wt, d_bm], w=[d_wt])
            for cc_ in range(n_cols // 128):
                bank, bk = PSB[cc_ % 2], PK[cc_ % 2]
                k.t("transpose", dict(out=bank[:, 0:128],
                                      in_=d_wt[:, 128 * cc_:128 * (cc_ + 1)],
                                      identity=sid[:]), r=[d_wt, sid], w=[bk])
                k.s("copy", dict(out=d_th[:], in_=bank[:, 0:128]),
                    r=[bk], w=[d_th])
                k.v("scalar_tensor_tensor", dict(
                    out=d_tl[:], in0=bank[:, 0:128], scalar=1.0, in1=d_th[:],
                    op0=ALU.mult, op1=ALU.subtract), r=[bk, d_th], w=[d_tl])
                p.dma("sync", out_h[128 * cc_:128 * (cc_ + 1),
                                    128 * ot:128 * (ot + 1)], d_th[:],
                      sem="d_dqh" + str(ot % 2))
                p.dma("sync", out_l[128 * cc_:128 * (cc_ + 1),
                                    128 * ot:128 * (ot + 1)], d_tl[:],
                      sem="d_dql" + str(ot % 2))

    dequant_one(wug[0], I3, D, wq_up_h, wq_up_l, "u")
    dequant_one(wug[1], I3, D, wq_g_h, wq_g_l, "g")
    dequant_one(wd, D, I3, wq_d_h, wq_d_l, "d")
    es_4c.close()
    p.barrier()

    if cfg.get("STOP_AFTER") == 5:
        k.phase_stacks.append(es_3)
        k.phase_stacks.append(es_4a)
        k.phase_stacks.append(es_4f)
        return k, locals()
    es_4d = contextlib.ExitStack()

    def cc_scales(cc_out_ap, cols, dst_col0, qmax, semtag):
        for j in range(cols):
            p.dma("sync", g_cc[0:1, 8 * j:8 * (j + 1)],
                  cc_out_ap[:, j][None, :], sem="d_gcc", loose=True)
        for j in range(cols):
            k.v("tensor_reduce", dict(out=s_sc[:, 8 + j:9 + j],
                                      in_=g_cc[0:1, 8 * j:8 * (j + 1)],
                                      axis=AXX,
                                      op=ALU.max), r=[g_cc], w=[s_sc])
            k.v("tensor_scalar", dict(out=s_sc[:, 8 + j:9 + j],
                                      in0=s_sc[:, 8 + j:9 + j],
                                      scalar1=1.0 / qmax, scalar2=1e-12,
                                      op0=ALU.mult, op1=ALU.max),
                r=[s_sc], w=[s_sc])
            k.v("reciprocal", dict(out=s_sc[:, 12 + j:13 + j],
                                   in_=s_sc[:, 8 + j:9 + j]),
                r=[s_sc], w=[s_sc])
            for (src_c, dst_c) in ((12 + j, dst_col0 + 2 * j),
                                   (8 + j, dst_col0 + 2 * j + 1)):
                k.t("matmul", dict(out=PSB[7][:, 0:1], lhsT=s_ones[:],
                                   rhs=s_sc[0:1, src_c:src_c + 1],
                                   start=True, stop=True),
                    r=[s_ones, s_sc], w=[PK[7]])
                k.s("copy", dict(out=s_scr[:, dst_c:dst_c + 1],
                                 in_=PSB[7][:, 0:1]), r=[PK[7]], w=["s_scr"])

    cc_scales(cc3_out, 1, 0, 32767.0, "3")
    for dc in range(NDC):
        for sc_ in range(TC // WTS0):
            sl3 = slice(TC * dc + WTS0 * sc_, TC * dc + WTS0 * (sc_ + 1))
            emit_qdq(k, f_xn1[:, sl3], f_xn1, u_it[:], u_it,
                     s_scr[:, 0:1], s_scr[:, 1:2])
            split16(k, f_xqh[:], f_xql[:], f_xn1[:, sl3],
                    r=[f_xn1], w=[f_xqh, f_xql])
            p.dma("sync", xq_sp_h[128 * dc:128 * (dc + 1),
                                  WTS0 * sc_:WTS0 * (sc_ + 1)], f_xqh[:],
                  sem="d_xqh")
            p.dma("sync", xq_sp_l[128 * dc:128 * (dc + 1),
                                  WTS0 * sc_:WTS0 * (sc_ + 1)], f_xql[:],
                  sem="d_xql")
    es_4f.close()
    UWH = [k.sb(f"u_wh{i}", [128, NDC * 128], BF16, es=es_4d) for i in range(2)]
    UWL = [k.sb(f"u_wl{i}", [128, NDC * 128], BF16, es=es_4d) for i in range(2)]
    k.v("memset", dict(ap=u_mx[:], constant=0.0), w=[u_mx])
    for w_i, (wh_sp, wl_sp, spill, mxc) in enumerate(
            ((wq_up_h, wq_up_l, up_spill, 0), (wq_g_h, wq_g_l, gate_spill, 1))):
        for ts in range(NTS):
            for ic in range(NDC):
                p.dma("sync", xq_th[:, WTS * ic:WTS * (ic + 1)],
                      xq_sp_h[128 * ic:128 * (ic + 1),
                              WTS * ts:WTS * (ts + 1)], sem="d_xqth")
                p.dma("sync", xq_tl[:, WTS * ic:WTS * (ic + 1)],
                      xq_sp_l[128 * ic:128 * (ic + 1),
                              WTS * ts:WTS * (ts + 1)], sem="d_xqtl")
            for oc in range(NI3):
                u_wh, u_wl = UWH[oc % 2], UWL[oc % 2]
                p.dma("sync", u_wh[:].rearrange("p (c m) -> p c m", m=128),
                      wh_sp[:, 128 * oc:128 * (oc + 1)].rearrange(
                          "(c p) m -> p c m", p=128), sem="d_uw" + str(oc % 2))
                p.dma("sync", u_wl[:].rearrange("p (c m) -> p c m", m=128),
                      wl_sp[:, 128 * oc:128 * (oc + 1)].rearrange(
                          "(c p) m -> p c m", p=128), sem="d_uwl" + str(oc % 2))
                bank, bk = PSB[2 + (oc % 2)], PK[2 + (oc % 2)]
                for ic in range(NDC):
                    mm3(k, bank[:, 0:WTS],
                        u_wh[:, 128 * ic:128 * (ic + 1)],
                        u_wl[:, 128 * ic:128 * (ic + 1)],
                        xq_th[:, WTS * ic:WTS * (ic + 1)],
                        xq_tl[:, WTS * ic:WTS * (ic + 1)],
                        start=(ic == 0), stop=(ic == NDC - 1),
                        r=[u_wh, u_wl, xq_th, xq_tl], w=[bk])
                u_t = UTB[oc % 2]
                k.s("copy", dict(out=u_t[:], in_=bank[:, 0:WTS]),
                    r=[bk], w=[u_t])
                k.v("tensor_reduce", dict(out=u_mx[:, 2:3], in_=u_t[:],
                                          axis=AXX, op=ALU.max,
                                          apply_absolute_value=True),
                    r=[u_t], w=[u_mx])
                k.v("tensor_tensor", dict(out=u_mx[:, mxc:mxc + 1],
                                          in0=u_mx[:, mxc:mxc + 1],
                                          in1=u_mx[:, 2:3], op=ALU.max),
                    r=[u_mx], w=[u_mx])
                p.dma("sync", spill[128 * oc:128 * (oc + 1),
                                    WTS * ts:WTS * (ts + 1)], u_t[:],
                      sem="d_sp" + str(oc % 2))
    for j in range(2):
        k.t("transpose", dict(out=PSB[6][0:1, 0:128], in_=u_mx[:, j:j + 1],
                              identity=sid[:]), r=[u_mx, sid], w=[PK[6]])
        k.v("tensor_reduce", dict(out=s_sc[:, j:j + 1],
                                  in_=PSB[6][0:1, 0:128], axis=AXX,
                                  op=ALU.max), r=[PK[6]], w=[s_sc])
    k.v("memset", dict(ap=s_sc[:, 2:8], constant=0.0), w=[s_sc])
    p.dma("sync", cc5_in[0, :], s_sc[0:1, 0:8], sem="d_ccx")
    k.emit_cc("AllGather", ALU.bypass, ALL8, cc5_in, cc5_out)
    cc_scales(cc5_out, 2, 2, 32767.0, "5")

    if cfg.get("STOP_AFTER") == 6:
        k.phase_stacks.append(es_3)
        k.phase_stacks.append(es_4a)
        k.phase_stacks.append(es_4d)
        return k, locals()
    C2_16 = float(2.0 ** -16)
    k.v("memset", dict(ap=u_mx[:, 3:4], constant=0.0), w=[u_mx])
    for oc in range(NI3):
        for ts in range(NTS):
            cpar = (oc * NTS + ts) % 2
            u_t, u_sg, u_it = UTB[cpar], USG[cpar], UIT[cpar]
            p.dma("sync", u_t[:], gate_spill[128 * oc:128 * (oc + 1),
                                             WTS * ts:WTS * (ts + 1)],
                  sem="d_ga" + str(cpar))
            emit_qdq(k, u_t[:], u_t, u_it[:], u_it, s_scr[:, 4:5],
                     s_scr[:, 5:6])
            k.s("activation", dict(out=u_sg[:], in_=u_t[:], func=AF.Sigmoid),
                r=[u_t], w=[u_sg])
            k.v("tensor_scalar", dict(out=u_sg[:], in0=u_sg[:],
                                      scalar1=65536.0, scalar2=C_RNE,
                                      op0=ALU.mult, op1=ALU.add),
                r=[u_sg], w=[u_sg])
            k.v("tensor_scalar", dict(out=u_sg[:], in0=u_sg[:],
                                      scalar1=C_RNE, scalar2=65535.0,
                                      op0=ALU.subtract, op1=ALU.min),
                r=[u_sg], w=[u_sg])
            k.v("tensor_copy", dict(out=u_it[:], in_=u_sg[:]),
                r=[u_sg], w=[u_it])
            k.v("tensor_copy", dict(out=u_sg[:], in_=u_it[:]),
                r=[u_it], w=[u_sg])
            k.v("scalar_tensor_tensor", dict(
                out=u_t[:], in0=u_sg[:], scalar=C2_16, in1=u_t[:],
                op0=ALU.mult, op1=ALU.mult), r=[u_sg, u_t], w=[u_t])
            k.v("tensor_reduce", dict(out=u_mx[:, 2:3], in_=u_t[:], axis=AXX,
                                      op=ALU.max, apply_absolute_value=True),
                r=[u_t], w=[u_mx])
            k.v("tensor_tensor", dict(out=u_mx[:, 3:4], in0=u_mx[:, 3:4],
                                      in1=u_mx[:, 2:3], op=ALU.max),
                r=[u_mx], w=[u_mx])
            p.dma("sync", g2_spill[128 * oc:128 * (oc + 1),
                                   WTS * ts:WTS * (ts + 1)], u_t[:],
                  sem="d_g2")
    k.t("transpose", dict(out=PSB[6][0:1, 0:128], in_=u_mx[:, 3:4],
                          identity=sid[:]), r=[u_mx, sid], w=[PK[6]])
    k.v("tensor_reduce", dict(out=s_sc[:, 0:1], in_=PSB[6][0:1, 0:128],
                              axis=AXX, op=ALU.max), r=[PK[6]], w=[s_sc])
    k.v("memset", dict(ap=s_sc[:, 1:8], constant=0.0), w=[s_sc])
    p.dma("sync", cc6_in[0, :], s_sc[0:1, 0:8], sem="d_ccx")
    k.emit_cc("AllGather", ALU.bypass, ALL8, cc6_in, cc6_out)
    cc_scales(cc6_out, 1, 6, 32767.0, "6")

    k.v("memset", dict(ap=u_mx[:, 3:4], constant=0.0), w=[u_mx])
    for oc in range(NI3):
        for ts in range(NTS):
            cpar = (oc * NTS + ts) % 2
            u_t, u_sg, u_it = UTB[cpar], USG[cpar], UIT[cpar]
            p.dma("sync", u_t[:], g2_spill[128 * oc:128 * (oc + 1),
                                           WTS * ts:WTS * (ts + 1)],
                  sem="d_g2l" + str(cpar))
            emit_qdq(k, u_t[:], u_t, u_it[:], u_it, s_scr[:, 6:7],
                     s_scr[:, 7:8])
            p.dma("sync", u_sg[:], up_spill[128 * oc:128 * (oc + 1),
                                            WTS * ts:WTS * (ts + 1)],
                  sem="d_upl" + str(cpar))
            emit_qdq(k, u_sg[:], u_sg, u_it[:], u_it, s_scr[:, 2:3],
                     s_scr[:, 3:4])
            k.v("tensor_tensor", dict(out=u_t[:], in0=u_t[:], in1=u_sg[:],
                                      op=ALU.mult), r=[u_t, u_sg], w=[u_t])
            k.v("tensor_reduce", dict(out=u_mx[:, 2:3], in_=u_t[:], axis=AXX,
                                      op=ALU.max, apply_absolute_value=True),
                r=[u_t], w=[u_mx])
            k.v("tensor_tensor", dict(out=u_mx[:, 3:4], in0=u_mx[:, 3:4],
                                      in1=u_mx[:, 2:3], op=ALU.max),
                r=[u_mx], w=[u_mx])
            p.dma("sync", op_spill[128 * oc:128 * (oc + 1),
                                   WTS * ts:WTS * (ts + 1)], u_t[:],
                  sem="d_op")
    k.t("transpose", dict(out=PSB[6][0:1, 0:128], in_=u_mx[:, 3:4],
                          identity=sid[:]), r=[u_mx, sid], w=[PK[6]])
    k.v("tensor_reduce", dict(out=s_sc[:, 0:1], in_=PSB[6][0:1, 0:128],
                              axis=AXX, op=ALU.max), r=[PK[6]], w=[s_sc])
    k.v("memset", dict(ap=s_sc[:, 1:8], constant=0.0), w=[s_sc])
    p.dma("sync", cc7_in[0, :], s_sc[0:1, 0:8], sem="d_ccx")
    k.emit_cc("AllGather", ALU.bypass, ALL8, cc7_in, cc7_out)
    cc_scales(cc7_out, 1, 8, 32767.0, "7")

    o_qh = k.sb("o_qh", [128, NI3 * WTS], BF16, es=es_4d)
    o_ql = k.sb("o_ql", [128, NI3 * WTS], BF16, es=es_4d)
    DWH = [k.sb(f"dw_h{i}", [128, NI3 * 128], BF16, es=es_4d) for i in range(2)]
    DWL = [k.sb(f"dw_l{i}", [128, NI3 * 128], BF16, es=es_4d) for i in range(2)]
    for ts in range(NTS):
        for ic3 in range(NI3):
            cpar = ic3 % 2
            u_t, u_it = UTB[cpar], UIT[cpar]
            p.dma("sync", u_t[:], op_spill[128 * ic3:128 * (ic3 + 1),
                                           WTS * ts:WTS * (ts + 1)],
                  sem="d_oq" + str(cpar))
            emit_qdq(k, u_t[:], u_t, u_it[:], u_it, s_scr[:, 8:9],
                     s_scr[:, 9:10])
            split16(k, o_qh[:, WTS * ic3:WTS * (ic3 + 1)],
                    o_ql[:, WTS * ic3:WTS * (ic3 + 1)], u_t[:],
                    r=[u_t], w=[o_qh, o_ql])
        for dcol in range(NDC):
            dpar = dcol % 2
            dw_h, dw_l = DWH[dpar], DWL[dpar]
            p.dma("sync", dw_h[:].rearrange("p (c m) -> p c m", m=128),
                  wq_d_h[:, 128 * dcol:128 * (dcol + 1)].rearrange(
                      "(c p) m -> p c m", p=128), sem="d_dwh" + str(dpar))
            p.dma("sync", dw_l[:].rearrange("p (c m) -> p c m", m=128),
                  wq_d_l[:, 128 * dcol:128 * (dcol + 1)].rearrange(
                      "(c p) m -> p c m", p=128), sem="d_dwl" + str(dpar))
            bank, bk = PSB[2 + (dcol % 2)], PK[2 + (dcol % 2)]
            for ic3 in range(NI3):
                mm3(k, bank[:, 0:WTS],
                    dw_h[:, 128 * ic3:128 * (ic3 + 1)],
                    dw_l[:, 128 * ic3:128 * (ic3 + 1)],
                    o_qh[:, WTS * ic3:WTS * (ic3 + 1)],
                    o_ql[:, WTS * ic3:WTS * (ic3 + 1)],
                    start=(ic3 == 0), stop=(ic3 == NI3 - 1),
                    r=[dw_h, dw_l, o_qh, o_ql], w=[bk])
            u_t2 = USG[dcol % 2]
            k.s("copy", dict(out=u_t2[:], in_=bank[:, 0:WTS]), r=[bk], w=[u_t2])
            for sub in range(WTS // 128):
                bank2, bk2 = PSB[6], PK[6]
                k.t("transpose", dict(out=bank2[:, 0:128],
                                      in_=u_t2[:, 128 * sub:128 * (sub + 1)],
                                      identity=sid[:]), r=[u_t2, sid], w=[bk2])
                tt2 = (WTS * ts) // 128 + sub
                k.v("tensor_tensor", dict(
                    out=q_x1[:, D * tt2 + 128 * dcol:D * tt2 + 128 * (dcol + 1)],
                    in0=q_x1[:, D * tt2 + 128 * dcol:D * tt2 + 128 * (dcol + 1)],
                    in1=bank2[:, 0:128], op=ALU.add), r=[q_x1, bk2], w=[q_x1])
    es_4d.close()

    out = k.dram("out", [TC, D], kind="ExternalOutput")
    for tt in range(NTC):
        p.dma("gpsimd", out[128 * tt:128 * (tt + 1), :],
              q_x1[:, D * tt:D * (tt + 1)], sem="d_out")
    k.phase_stacks.append(es_3)
    k.phase_stacks.append(es_4a)
    return k, locals()


def finish(k, dummy_out=False):
    p, nc = k.p, k.nc
    if dummy_out:
        o = k.dram("out", [k.TC, k.D], kind="ExternalOutput")
        es_z = contextlib.ExitStack()
        z = k.sb("z_dummy", [128, k.D], es=es_z)
        k.v("memset", dict(ap=z[:], constant=0.0), w=[z])
        for tt in range(k.TC // 128):
            p.dma("gpsimd", o[128 * tt:128 * (tt + 1), :], z[:], sem="d_out")
        es_z.close()
    for st_ in reversed(k.phase_stacks):
        st_.close()
    p.final_wait("gpsimd")
    p.raw("gpsimd", lambda g: g.nop())
    ctx = p.make_sems(k.es)
    with nc.Block() as block:
        p.emit(block, ctx)
    k.es.close()
    return nc


def host_inputs_for_core(ins, core, cfg):
    T, D = cfg["T"], cfg["D"]
    DI = 2 * D
    DIH = DI // 2
    b, ph = core // 2, core % 2
    f32 = np.float32
    cont = np.ascontiguousarray

    def t_(a):
        return cont(np.asarray(a, f32).T)

    xs_rows = slice(ph * DIH, (ph + 1) * DIH)
    z_rows = slice(DI + ph * DIH, DI + (ph + 1) * DIH)
    ipT = np.concatenate([np.asarray(ins["L0_attn_in_proj_w"], f32)[xs_rows].T,
                          np.asarray(ins["L0_attn_in_proj_w"], f32)[z_rows].T],
                         axis=1)
    d = {
        "x0": cont(np.asarray(ins["L0_attn_x"], f32)[b]),
        "w_ipT": cont(ipT),
        "conv_w": cont(np.asarray(ins["L0_attn_conv_w"], f32)[xs_rows, 0, :]),
        "conv_b": cont(np.asarray(ins["L0_attn_conv_b"], f32)[xs_rows]),
        "w_xpT": cont(np.asarray(ins["L0_attn_x_proj_w"], f32)[:, xs_rows].T),
        "w_dtT": cont(np.asarray(ins["L0_attn_dt_proj_w"], f32)[xs_rows].T),
        "dt_b": cont(np.asarray(ins["L0_attn_dt_proj_b"], f32)[xs_rows]),
        "A_log": cont(np.asarray(ins["L0_attn_A_log"], f32)[xs_rows]),
        "D_par": cont(np.asarray(ins["L0_attn_D_param"], f32)[xs_rows]),
        "w_opT": cont(np.asarray(ins["L0_attn_out_proj_w"], f32)
                      [:, xs_rows].T),
        "norm_w": cont(np.asarray(ins["L0_attn_norm_w"], f32)),
    }
    wo_loc = np.asarray(ins["L0_ffn_Wo"], f32)[512 * ph:512 * (ph + 1)]
    wo_r = np.zeros((4, 128, D), f32)
    for g in range(4):
        for r in range(128):
            wo_r[g, r] = wo_loc[(2 * g + r // 64) * 64 + (r % 64)]
    d.update({
        "ln1_w": cont(np.asarray(ins["L0_ffn_ln1_w"], f32)),
        "wq": cont(np.asarray(ins["L0_ffn_Wq"], f32)[:, 512 * ph:512 * (ph + 1)]),
        "wk": cont(np.asarray(ins["L0_ffn_Wk"], f32)[:, 128 * ph:128 * (ph + 1)]),
        "wv": cont(np.asarray(ins["L0_ffn_Wv"], f32)[:, 128 * ph:128 * (ph + 1)]),
        "qn_w": cont(np.asarray(ins["L0_ffn_qn_w"], f32)),
        "kn_w": cont(np.asarray(ins["L0_ffn_kn_w"], f32)),
        "wg": cont(np.asarray(ins["L0_ffn_Wg"], f32)),
        "wo_r": cont(wo_r),
    })
    TC = T // 2
    d.update({
        "x0o": cont(np.asarray(ins["L0_attn_x"], f32)[b][ph * TC:(ph + 1) * TC]),
        "ln2_w": cont(np.asarray(ins["L0_ffn_ln2_w"], f32)),
        "wr": cont(np.asarray(ins["L0_ffn_Wr"], f32)),
        "w1": np.ascontiguousarray(np.asarray(ins["L0_ffn_W1"], f32)),
        "w2": np.ascontiguousarray(np.asarray(ins["L0_ffn_W2"], f32)),
        "a_param": cont(np.asarray(ins["L1_attn_a_param"], f32)),
        "w_in": cont(np.asarray(ins["L1_attn_w_in"], f32)),
        "w_a": cont(np.asarray(ins["L1_attn_w_a"], f32)),
        "is_odd": np.full((1, 1), float(ph), f32),
        "wug": cont(np.stack([np.asarray(ins["L1_ffn_w_up"], f32),
                              np.asarray(ins["L1_ffn_w_gate"], f32)])),
        "wd": cont(np.asarray(ins["L1_ffn_w_down"], f32)),
    })
    return d


def kernel(**inputs):
    cfg = dict(CFG)
    k, _env = build(cfg)
    nc = finish(k)
    in_maps = [host_inputs_for_core(inputs, c, cfg) for c in range(8)]
    res = run_bass_kernel_spmd(nc, in_maps, core_ids=list(range(8)))
    T, D = cfg["T"], cfg["D"]
    TC = T // 2
    out = np.empty((4, T, D), dtype=np.float32)
    for c in range(8):
        b, ph = c // 2, c % 2
        out[b, ph * TC:(ph + 1) * TC] = np.asarray(res.results[c]["out"])
    return out


# version 08
# speedup vs baseline: 1.2301x; 1.2301x over previous
import contextlib
import numpy as np
import concourse.bass as bass
import concourse.mybir as mybir
from concourse.bass_utils import run_bass_kernel_spmd
from concourse.bass_core_rust import IndirectOffsetOnAxis

F32 = mybir.dt.float32
BF16 = mybir.dt.bfloat16
I32 = mybir.dt.int32
U32 = mybir.dt.uint32
AF = mybir.ActivationFunctionType
ALU = mybir.AluOpType
AXX = mybir.AxisListType.X

CFG = {
    "T": 4096,
    "D": 1024,
    "MOE_CAP": 384,
    "DEBUG_OUT": [],
}
C_RNE = float(np.float32(1.5 * 2 ** 23))

ENGINES = ("tensor", "vector", "scalar", "gpsimd", "sync")
SELF_SERIAL = {"vector", "scalar", "gpsimd"}


def _key(x):
    if isinstance(x, str):
        return x
    t = getattr(x, "tensor", None)
    if t is not None:
        return t.name
    return x.name


class Prog:

    def __init__(self, nc):
        self.nc = nc
        self.items = {e: [] for e in ENGINES}
        self.count = {e: 0 for e in ENGINES}
        self.dma_count = {}
        self.watermark = {}
        self.last_write = {}
        self.readers = {}
        self.bar = []

    def _emit_wait(self, eng_name, tok):
        if tok is None:
            return
        sem_name, val = tok
        if val <= 0:
            return
        if sem_name == "c_" + eng_name:
            return
        key = (eng_name, sem_name)
        if self.watermark.get(key, -1) >= val:
            return
        self.watermark[key] = val
        self.items[eng_name].append(("wait", sem_name, val))

    def _deps_for(self, reads, writes, deps):
        toks = list(deps) + list(self.bar)
        for r in reads:
            toks.append(self.last_write.get(_key(r)))
        for w in writes:
            kw = _key(w)
            toks.append(self.last_write.get(kw))
            toks.extend(self.readers.get(kw, ()))
        return toks

    def _record(self, tok, reads, writes):
        for r in reads:
            self.readers.setdefault(_key(r), []).append(tok)
        for w in writes:
            kw = _key(w)
            self.last_write[kw] = tok
            self.readers[kw] = []

    def op(self, eng_name, calls, r=(), w=(), deps=()):
        if isinstance(calls, tuple):
            calls = [calls]
        for t in self._deps_for(r, w, deps):
            self._emit_wait(eng_name, t)
        if eng_name in SELF_SERIAL:
            kk = self.count[eng_name]
            if kk > 0:
                self.items[eng_name].append(("selfwait", "c_" + eng_name, kk))
        self.count[eng_name] += 1
        tok = ("c_" + eng_name, self.count[eng_name])
        self.items[eng_name].append(("op", list(calls), "c_" + eng_name))
        self._record(tok, r, w)
        return tok

    def dma(self, queue, out, in_, sem=None, r=(), w=(), deps=(), loose=False,
            serial=False):
        sem = sem or ("d_" + queue)
        reads = list(r) + [in_]
        writes = list(w) + [out]
        deps = list(deps)
        if serial and sem in self.dma_count:
            deps.append((sem, self.dma_count[sem]))
        for t in self._deps_for(reads, writes, deps):
            self._emit_wait(queue, t)
        self.dma_count[sem] = self.dma_count.get(sem, 0) + 16
        tok = (sem, self.dma_count[sem])
        self.items[queue].append(("dma", out, in_, sem, loose))
        self._record(tok, reads, writes)
        return tok

    def idma(self, queue, out, out_off, in_, in_off, sem, r=(), w=(),
             deps=()):
        reads = list(r) + [in_]
        writes = list(w) + [out]
        if out_off is not None:
            reads.append(out_off[0])
        if in_off is not None:
            reads.append(in_off[0])
        for t in self._deps_for(reads, writes, deps):
            self._emit_wait(queue, t)
        self.dma_count[sem] = self.dma_count.get(sem, 0) + 16
        tok = (sem, self.dma_count[sem])
        oo = None if out_off is None else IndirectOffsetOnAxis(*out_off)
        io = None if in_off is None else IndirectOffsetOnAxis(*in_off)
        self.items[queue].append(("op16", ("indirect_dma_start", dict(
            out=out, out_offset=oo, in_=in_, in_offset=io)), sem))
        self._record(tok, reads, writes)
        return tok

    def cc(self, kind, op_, groups, cc_in, cc_out, deps=()):
        reads = [cc_in]
        writes = [cc_out]
        for t in self._deps_for(reads, writes, deps):
            self._emit_wait("gpsimd", t)
        if self.count["gpsimd"] > 0:
            self.items["gpsimd"].append(
                ("selfwait", "c_gpsimd", self.count["gpsimd"]))
        self.count["gpsimd"] += 1
        tok = ("c_gpsimd", self.count["gpsimd"])
        self.items["gpsimd"].append(("cc", kind, op_, groups, cc_in, cc_out,
                                     "c_gpsimd"))
        self._record(tok, reads, writes)
        return tok

    def barrier(self):
        toks = [("c_" + e, self.count[e]) for e in ENGINES if self.count[e] > 0]
        toks += [(s, v) for s, v in self.dma_count.items()]
        self.bar = toks
        self.last_write = {}
        self.readers = {}

    def raw(self, eng_name, fn):
        self.items[eng_name].append(("raw", fn))

    def final_wait(self, eng_name):
        for sem, v in self.dma_count.items():
            self._emit_wait(eng_name, (sem, v))
        for e in ENGINES:
            if e != eng_name and self.count[e] > 0:
                self._emit_wait(eng_name, ("c_" + e, self.count[e]))

    def make_sems(self, es):
        names = set("c_" + e for e in ENGINES if self.count[e] > 0)
        names |= set(self.dma_count.keys())
        for e in ENGINES:
            for it in self.items[e]:
                if it[0] in ("wait", "selfwait"):
                    names.add(it[1])
        return {n: es.enter_context(self.nc.semaphore(n)) for n in sorted(names)}

    def emit(self, block, ctx):
        def make_body(eng_name):
            def body(eng):
                for it in self.items[eng_name]:
                    kind = it[0]
                    if kind in ("wait", "selfwait"):
                        eng.wait_ge(ctx[it[1]], it[2])
                    elif kind == "op":
                        _, calls, sem_name = it
                        for mth, kw in calls[:-1]:
                            getattr(eng, mth)(**kw)
                        mth, kw = calls[-1]
                        getattr(eng, mth)(**kw).then_inc(ctx[sem_name], 1)
                    elif kind == "op16":
                        _, (mth, kw), sem_name = it
                        getattr(eng, mth)(**kw).then_inc(ctx[sem_name], 16)
                    elif kind == "dma":
                        _, out, in_, sem_name, loose = it
                        eng.dma_start(
                            out=out, in_=in_,
                            allow_slow_non_contiguous=loose,
                        ).then_inc(ctx[sem_name], 16)
                    elif kind == "cc":
                        _, k_, op_, groups, ci, co, sem_name = it
                        eng.collective_compute(
                            k_, op_, replica_groups=groups, ins=[ci], outs=[co],
                        ).then_inc(ctx[sem_name], 1)
                    elif kind == "raw":
                        it[1](eng)
            return body

        for eng_name in ENGINES:
            if not self.items[eng_name]:
                continue
            getattr(block, eng_name)(make_body(eng_name))


class K:
    def __init__(self, cfg):
        self.cfg = cfg
        self.T = cfg["T"]
        self.TC = self.T // 2
        self.D = cfg["D"]
        self.DI = 2 * self.D
        self.DIH = self.DI // 2
        self.NIT = self.DIH // 128
        self.DS = 16
        self.DTR = self.D // 16
        self.HD = 64
        self.E = 8
        self.HID = 2 * self.D
        self.I3 = 3 * self.D
        self.CAP = cfg["MOE_CAP"]
        self.nc = bass.Bass("TRN2", target_bir_lowering=False)
        self.p = Prog(self.nc)
        self.es = contextlib.ExitStack()
        self.phase_stacks = []
        self.dbg_outs = {}

    def sb(self, name, shape, dt=F32, es=None):
        return (es or self.es).enter_context(
            self.nc.sbuf_tensor(name, shape, dt))

    def dram(self, name, shape, dt=F32, kind="Internal"):
        return self.nc.dram_tensor(name, shape, dt, kind=kind).ap()

    def inline(self, name, arr):
        return self.nc.inline_tensor(np.ascontiguousarray(arr), name).ap()

    def v(self, meth, kw, r=(), w=(), deps=()):
        return self.p.op("vector", (meth, kw), r, w, deps)

    def s(self, meth, kw, r=(), w=(), deps=()):
        return self.p.op("scalar", (meth, kw), r, w, deps)

    def g(self, meth, kw, r=(), w=(), deps=()):
        return self.p.op("gpsimd", (meth, kw), r, w, deps)

    def t(self, meth, kw, r=(), w=(), deps=()):
        return self.p.op("tensor", (meth, kw), r, w, deps)

    def tm(self, calls, r=(), w=(), deps=()):
        return self.p.op("tensor", calls, r, w, deps)

    def emit_cc(self, kind, op_, groups, cc_in, cc_out):
        if self.cfg.get("LOCAL_CC"):
            n = cc_out.shape[0]
            for j in range(n):
                self.p.dma("gpsimd", cc_out[j], cc_in[0], sem="d_lcc")
            return
        self.p.cc(kind, op_, groups, cc_in, cc_out)

    def ones_colT_tile(self):
        if not hasattr(self, "_onec"):
            self._onec = self.sb("k_onec", [128, 1])
            self.v("memset", dict(ap=self._onec[:], constant=1.0),
                   w=["k_onec"])
        return self._onec[:]

    def const_col(self, val):
        if not hasattr(self, "_cc_tile"):
            self._cc_tile = self.sb("k_consts", [128, 32])
            self._cc_map = {}
        v = float(val)
        if v not in self._cc_map:
            i = len(self._cc_map)
            assert i < 32
            self._cc_map[v] = i
            ap = self._cc_tile[:, i:i + 1]
            self.v("memset", dict(ap=ap, constant=v), w=["k_consts"])
        i = self._cc_map[v]
        return self._cc_tile[:, i:i + 1]

    def debug(self, name, src_ap, shape):
        if name not in self.cfg["DEBUG_OUT"]:
            return
        o = self.dram("dbg_" + name, shape, kind="ExternalOutput")
        self.p.dma("gpsimd", o, src_ap, sem="d_dbg")
        self.dbg_outs[name] = shape

    def debug_dram(self, name, shape):
        if name in self.cfg["DEBUG_OUT"]:
            o = self.dram("dbg_" + name, shape, kind="ExternalOutput")
            self.dbg_outs[name] = shape
            return o
        return self.dram("scr_" + name, shape, kind="Internal")


def ones_colT(k):
    return k.ones_colT_tile()


def mm3(k, ps_ap, lhi, llo, rhi, rlo, start, stop, r=(), w=(), deps=()):
    calls = [
        ("matmul", dict(out=ps_ap, lhsT=lhi, rhs=rhi, start=start, stop=False)),
        ("matmul", dict(out=ps_ap, lhsT=lhi, rhs=rlo, start=False, stop=False)),
        ("matmul", dict(out=ps_ap, lhsT=llo, rhs=rhi, start=False, stop=stop)),
    ]
    return k.tm(calls, r=r, w=w, deps=deps)


def split16(k, hi_ap, lo_ap, src_ap, r=(), w=(), deps=()):
    k.s("copy", dict(out=hi_ap, in_=src_ap),
        r=list(r) + [src_ap], w=[hi_ap], deps=deps)
    return k.v("tensor_sub", dict(out=lo_ap, in0=src_ap, in1=hi_ap),
               r=list(r) + [src_ap, hi_ap], w=list(w) + [lo_ap])


def softplus_inplace(k, x_ap, tmp_ap, xkey, tkey):
    k.s("activation", dict(out=tmp_ap, in_=x_ap, func=AF.Abs),
        r=[xkey], w=[tkey])
    k.s("activation", dict(out=tmp_ap, in_=tmp_ap, func=AF.Exp, scale=-1.0),
        r=[tkey], w=[tkey])
    k.s("activation", dict(out=tmp_ap, in_=tmp_ap, func=AF.Ln, bias=1.0),
        r=[tkey], w=[tkey])
    k.v("scalar_tensor_tensor", dict(
        out=x_ap, in0=x_ap, scalar=0.0, in1=tmp_ap,
        op0=ALU.max, op1=ALU.add), r=[xkey, tkey], w=[xkey])


def emit_rne(k, ap, key, itmp, ikey):
    k.v("tensor_scalar", dict(out=ap, in0=ap, scalar1=C_RNE, scalar2=C_RNE,
                              op0=ALU.add, op1=ALU.subtract), r=[key], w=[key])
    k.v("tensor_copy", dict(out=itmp, in_=ap), r=[key], w=[ikey])
    k.v("tensor_copy", dict(out=ap, in_=itmp), r=[ikey], w=[key])


def emit_qdq(k, ap, key, itmp, ikey, sinv_rep, s_rep):
    k.v("tensor_scalar", dict(out=ap, in0=ap, scalar1=sinv_rep, scalar2=None,
                              op0=ALU.mult), r=[key, "s_scr"], w=[key])
    emit_rne(k, ap, key, itmp, ikey)
    k.v("tensor_scalar", dict(out=ap, in0=ap, scalar1=s_rep, scalar2=None,
                              op0=ALU.mult), r=[key, "s_scr"], w=[key])


EPS_LN = 1e-5
EPS_RMS = float(np.finfo(np.float32).eps)
EPS_A = 1e-6


def emit_ln(k, a_x, a_t, a_sq, a_st, eps2, w_rep):
    D = k.D
    k.v("reduce_sum", dict(out=a_st[:, 0:1], in_=a_x[:], axis=AXX),
        r=[a_x], w=["a_st"])
    k.v("tensor_scalar_mul", dict(out=a_st[:, 1:2], in0=a_st[:, 0:1],
                                  scalar1=1.0 / D), r=["a_st"], w=["a_st"])
    k.v("tensor_scalar", dict(out=a_t[:], in0=a_x[:], scalar1=a_st[:, 1:2],
                              scalar2=None, op0=ALU.subtract),
        r=[a_x, "a_st"], w=[a_t])
    k.s("activation", dict(out=a_sq[:], in_=a_t[:], func=AF.Square,
                           accum_out=a_st[:, 2:3]), r=[a_t], w=[a_sq, "a_st"])
    c_invd, c_eps = k.const_col(1.0 / D), k.const_col(EPS_LN)
    k.s("activation", dict(out=a_st[:, 3:4], in_=a_st[:, 2:3], func=AF.Sqrt,
                           scale=c_invd, bias=c_eps),
        r=["a_st", "k_consts"], w=["a_st"])
    k.v("reciprocal", dict(out=a_st[:, 4:5], in_=a_st[:, 3:4]),
        r=["a_st"], w=["a_st"])
    k.v("tensor_scalar", dict(out=a_x[:], in0=a_t[:], scalar1=a_st[:, 4:5],
                              scalar2=None, op0=ALU.mult),
        r=[a_t, "a_st"], w=[a_x])
    if eps2 is None:
        return
    k.s("activation", dict(out=a_sq[:], in_=a_x[:], func=AF.Square,
                           accum_out=a_st[:, 5:6]), r=[a_x], w=[a_sq, "a_st"])
    c_eps2 = k.const_col(eps2)
    k.s("activation", dict(out=a_st[:, 6:7], in_=a_st[:, 5:6], func=AF.Sqrt,
                           scale=c_invd, bias=c_eps2),
        r=["a_st", "k_consts"], w=["a_st"])
    k.v("reciprocal", dict(out=a_st[:, 7:8], in_=a_st[:, 6:7]),
        r=["a_st"], w=["a_st"])
    if w_rep is not None:
        k.v("scalar_tensor_tensor", dict(
            out=a_t[:], in0=a_x[:], scalar=a_st[:, 7:8], in1=w_rep[:],
            op0=ALU.mult, op1=ALU.mult), r=[a_x, "a_st", w_rep], w=[a_t])
    else:
        k.v("tensor_scalar", dict(out=a_t[:], in0=a_x[:], scalar1=a_st[:, 7:8],
                                  scalar2=None, op0=ALU.mult),
            r=[a_x, "a_st"], w=[a_t])


def build(cfg):
    k = K(cfg)
    nc, p = k.nc, k.p
    T, TC, D = k.T, k.TC, k.D
    DIH, NIT, DS, DTR = k.DIH, k.NIT, k.DS, k.DTR
    NDC = D // 128
    NTT = T // 128
    SS = 128 if T >= 2048 else max(64, T // 4)
    NST = T // SS
    SF = DS * SS
    NS512 = T // 512 if T >= 512 else 1
    W512 = min(512, T)

    x0 = k.dram("x0", [T, D], kind="ExternalInput")
    w_ipT = k.dram("w_ipT", [D, 2 * DIH], kind="ExternalInput")
    conv_w = k.dram("conv_w", [DIH, 4], kind="ExternalInput")
    conv_b = k.dram("conv_b", [DIH], kind="ExternalInput")
    w_xpT = k.dram("w_xpT", [DIH, 96], kind="ExternalInput")
    w_dtT = k.dram("w_dtT", [DTR, DIH], kind="ExternalInput")
    dt_b = k.dram("dt_b", [DIH], kind="ExternalInput")
    A_log = k.dram("A_log", [DIH, DS], kind="ExternalInput")
    D_par = k.dram("D_par", [DIH], kind="ExternalInput")
    w_opT = k.dram("w_opT", [DIH, D], kind="ExternalInput")
    norm_w = k.dram("norm_w", [D], kind="ExternalInput")

    ident = k.inline("ident", np.eye(128, dtype=np.float32))
    ones_row = k.inline("ones_row", np.ones((1, 128), np.float32))
    selbc = np.zeros((96, 32), np.float32)
    for n in range(32):
        selbc[64 + n, n] = 1.0
    sel_bc = k.inline("sel_bc", selbc)

    u_spill = k.dram("u_spill", [DIH, T])
    z_spill = k.dram("z_spill", [DIH, T])
    cc0_in = k.dram("cc0_in", [1, 96 * T])
    cc0_out = k.dram("cc0_out", [1, 96 * T])
    cc1_in = k.dram("cc1_in", [1, T * D])
    cc1_out = k.dram("cc1_out", [1, TC * D])
    PAIRS = [[0, 1], [2, 3], [4, 5], [6, 7]]
    ALL8 = [list(range(8))]

    k.const_col(0.0)
    k.ones_colT_tile()
    sid = k.sb("sid", [128, 128])
    s_ones = k.sb("s_ones", [1, 128])
    s_st8 = k.sb("a_st", [128, 8])

    PSB = [k.es.enter_context(nc.psum_tensor(f"psb{i}", [128, 512], F32))
           for i in range(8)]
    PK = [f"psb{i}" for i in range(8)]

    p.dma("sync", sid[:], ident, sem="d_ini", serial=True)
    p.dma("sync", s_ones[:], ones_row, sem="d_ini", serial=True)

    es_m = contextlib.ExitStack()
    m_xdbc = k.sb("m_xdbc", [128, T], es=es_m)
    s_selbc = k.sb("s_selbc", [96, 32], es=es_m)
    s_convw = k.sb("s_convw", [128, 4 * NIT], es=es_m)
    s_convb = k.sb("s_convb", [128, NIT], es=es_m)
    s_dtb = k.sb("s_dtb", [128, NIT], es=es_m)
    s_A = k.sb("s_A", [128, DS * NIT], es=es_m)
    s_D = k.sb("s_D", [128, NIT], es=es_m)
    s_dtw = k.sb("s_dtw", [DTR, DIH], es=es_m)
    s_xpw = k.sb("s_xpw", [128, NIT * 96], es=es_m)
    s_nw1 = k.sb("s_nw1", [1, D], es=es_m)
    p.dma("sync", s_selbc[:], sel_bc, sem="d_ini", serial=True)
    p.dma("sync", s_convw[:].rearrange("p (n c) -> p n c", c=4),
          conv_w.rearrange("(n p) c -> p n c", p=128), sem="d_ini", serial=True)
    p.dma("sync", s_convb[:], conv_b.rearrange("(n p) -> p n", p=128),
          sem="d_ini", serial=True, loose=True)
    p.dma("sync", s_dtb[:], dt_b.rearrange("(n p) -> p n", p=128),
          sem="d_ini", serial=True, loose=True)
    p.dma("sync", s_A[:].rearrange("p (n s) -> p n s", s=DS),
          A_log.rearrange("(n p) s -> p n s", p=128), sem="d_ini", serial=True)
    p.dma("sync", s_D[:], D_par.rearrange("(n p) -> p n", p=128),
          sem="d_ini", serial=True, loose=True)
    p.dma("sync", s_dtw[:], w_dtT, sem="d_ini", serial=True)
    p.dma("sync", s_xpw[:].rearrange("p (n o) -> p n o", o=96),
          w_xpT.rearrange("(n p) o -> p n o", p=128), sem="d_ini", serial=True)
    k.s("activation", dict(out=s_A[:], in_=s_A[:], func=AF.Exp),
        r=[s_A], w=[s_A])
    k.v("tensor_scalar_mul", dict(out=s_A[:], in0=s_A[:], scalar1=-1.0),
        r=[s_A], w=[s_A])
    p.dma("sync", s_nw1[:], norm_w[None, :], sem="d_ini", serial=True)

    es_1a = contextlib.ExitStack()
    m_xmTh = k.sb("m_xmTh", [128, NDC * T], BF16, es=es_1a)
    m_xmTl = k.sb("m_xmTl", [128, NDC * T], BF16, es=es_1a)

    es_ln = contextlib.ExitStack()
    s_nw = k.sb("s_nw", [128, D], es=es_ln)
    a_x = k.sb("a_x", [128, D], es=es_ln)
    a_t = k.sb("a_t", [128, D], es=es_ln)
    a_sq = k.sb("a_sq", [128, D], es=es_ln)
    for dc in range(NDC):
        k.t("matmul", dict(out=PSB[0][:, 0:128], lhsT=s_ones[:],
                           rhs=s_nw1[:, 128 * dc:128 * (dc + 1)],
                           start=True, stop=True),
            r=[s_ones, s_nw1], w=[PK[0]])
        k.s("copy", dict(out=s_nw[:, 128 * dc:128 * (dc + 1)],
                         in_=PSB[0][:, 0:128]), r=[PK[0]], w=[s_nw])
    for tt in range(NTT):
        p.dma("sync", a_x[:], x0[128 * tt:128 * (tt + 1), :], sem="d_x")
        emit_ln(k, a_x, a_t, a_sq, s_st8, EPS_RMS, s_nw)
        for dc in range(NDC):
            bank, bk = PSB[dc % 2], PK[dc % 2]
            k.t("transpose", dict(out=bank[:, 0:128],
                                  in_=a_t[:, 128 * dc:128 * (dc + 1)],
                                  identity=sid[:]),
                r=[a_t, sid], w=[bk])
            fo = dc * T + 128 * tt
            split16(k, m_xmTh[:, fo:fo + 128], m_xmTl[:, fo:fo + 128],
                    bank[:, 0:128], r=[bk], w=[m_xmTh, m_xmTl])
    es_ln.close()
    p.barrier()

    es_ip = contextlib.ExitStack()
    TMPW = [k.sb(f"s_tmpw{i}", [128, DIH], es=es_ip) for i in range(2)]
    WIH = [k.sb("m_wih0", [128, 2 * DIH], BF16, es=es_ip)] * 2
    WIL = [k.sb("m_wil0", [128, 2 * DIH], BF16, es=es_ip)] * 2
    m_xs = k.sb("m_xs", [128, T + 4], es=es_ip)
    m_u = k.sb("m_u", [128, T], es=es_ip)
    m_z = k.sb("m_z", [128, W512], es=es_ip)
    k.v("memset", dict(ap=m_xs[:, 0:4], constant=0.0), w=[m_xs])

    for it in range(NIT):
        s_tmpw, m_wih, m_wil = TMPW[it % 2], WIH[it % 2], WIL[it % 2]
        p.dma("sync", s_tmpw[:, 0:128 * NDC].rearrange("p (c m) -> p c m", m=128),
              w_ipT[:, 128 * it:128 * (it + 1)].rearrange(
                  "(c p) m -> p c m", p=128), sem="d_wip" + str(it % 2))
        split16(k, m_wih[:, 0:128 * NDC], m_wil[:, 0:128 * NDC],
                s_tmpw[:, 0:128 * NDC], r=[s_tmpw], w=[m_wih, m_wil])
        p.dma("sync", s_tmpw[:, 0:128 * NDC].rearrange("p (c m) -> p c m", m=128),
              w_ipT[:, DIH + 128 * it:DIH + 128 * (it + 1)].rearrange(
                  "(c p) m -> p c m", p=128), sem="d_wip" + str(it % 2))
        split16(k, m_wih[:, 128 * NDC:256 * NDC], m_wil[:, 128 * NDC:256 * NDC],
                s_tmpw[:, 0:128 * NDC], r=[s_tmpw], w=[m_wih, m_wil])
        for st in range(NS512):
            bank, bk = PSB[2 + (st % 2)], PK[2 + (st % 2)]
            for dc in range(NDC):
                mm3(k, bank[:, 0:W512],
                    m_wih[:, 128 * dc:128 * (dc + 1)],
                    m_wil[:, 128 * dc:128 * (dc + 1)],
                    m_xmTh[:, dc * T + W512 * st: dc * T + W512 * (st + 1)],
                    m_xmTl[:, dc * T + W512 * st: dc * T + W512 * (st + 1)],
                    start=(dc == 0), stop=(dc == NDC - 1),
                    r=[m_wih, m_wil, m_xmTh, m_xmTl], w=[bk])
            k.s("copy", dict(out=m_xs[:, 4 + W512 * st:4 + W512 * (st + 1)],
                             in_=bank[:, 0:W512]), r=[bk], w=[m_xs])
            bankz, bkz = PSB[4 + (st % 2)], PK[4 + (st % 2)]
            for dc in range(NDC):
                mm3(k, bankz[:, 0:W512],
                    m_wih[:, 128 * NDC + 128 * dc:128 * NDC + 128 * (dc + 1)],
                    m_wil[:, 128 * NDC + 128 * dc:128 * NDC + 128 * (dc + 1)],
                    m_xmTh[:, dc * T + W512 * st: dc * T + W512 * (st + 1)],
                    m_xmTl[:, dc * T + W512 * st: dc * T + W512 * (st + 1)],
                    start=(dc == 0), stop=(dc == NDC - 1),
                    r=[m_wih, m_wil, m_xmTh, m_xmTl], w=[bkz])
            k.s("activation", dict(out=m_z[:, 0:W512], in_=bankz[:, 0:W512],
                                   func=AF.Silu), r=[bkz], w=[m_z])
            p.dma("sync", z_spill[128 * it:128 * (it + 1),
                                  W512 * st:W512 * (st + 1)],
                  m_z[:, 0:W512], sem="d_zsp")
        cw = [s_convw[:, 4 * it + kk:4 * it + kk + 1] for kk in range(4)]
        k.v("tensor_scalar", dict(out=m_u[:], in0=m_xs[:, 1:1 + T],
                                  scalar1=cw[0], scalar2=None, op0=ALU.mult),
            r=[m_xs, s_convw], w=[m_u])
        for kk in range(1, 4):
            k.v("scalar_tensor_tensor", dict(
                out=m_u[:], in0=m_xs[:, 1 + kk:1 + kk + T], scalar=cw[kk],
                in1=m_u[:], op0=ALU.mult, op1=ALU.add),
                r=[m_xs, s_convw, m_u], w=[m_u])
        k.s("activation", dict(out=m_u[:], in_=m_u[:], func=AF.Silu,
                               bias=s_convb[:, it:it + 1]),
            r=[m_u, s_convb], w=[m_u])
        p.dma("sync", u_spill[128 * it:128 * (it + 1), :], m_u[:], sem="d_usp")
        for st in range(NS512):
            bank, bk = PSB[6 + (st % 2)], PK[6 + (st % 2)]
            k.t("matmul", dict(out=bank[0:96, 0:W512],
                               lhsT=s_xpw[:, 96 * it:96 * (it + 1)],
                               rhs=m_u[:, W512 * st:W512 * (st + 1)],
                               start=True, stop=True),
                r=[s_xpw, m_u], w=[bk])
            if it == 0:
                k.s("copy", dict(out=m_xdbc[0:96, W512 * st:W512 * (st + 1)],
                                 in_=bank[0:96, 0:W512]), r=[bk], w=[m_xdbc])
            else:
                k.v("tensor_add", dict(
                    out=m_xdbc[0:96, W512 * st:W512 * (st + 1)],
                    in0=m_xdbc[0:96, W512 * st:W512 * (st + 1)],
                    in1=bank[0:96, 0:W512]), r=[bk, m_xdbc], w=[m_xdbc])
    es_ip.close()
    es_1a.close()
    p.barrier()

    p.dma("sync", cc0_in[0, :].rearrange("(p f) -> p f", p=96),
          m_xdbc[0:96, :], sem="d_cc0")
    k.emit_cc("AllReduce", ALU.add, PAIRS, cc0_in, cc0_out)
    es_1b = contextlib.ExitStack()
    p.dma("sync", m_xdbc[0:96, :], cc0_out[0, :].rearrange("(p f) -> p f", p=96),
          sem="d_cc0c")
    k.debug("xdbc", m_xdbc[0:96, :], [96, T])

    BREP = [k.sb(f"s_brep{i}", [128, SF], es=es_1b) for i in range(2)]
    CREP = [k.sb(f"s_crep{i}", [128, SF], es=es_1b) for i in range(2)]
    DA = [k.sb(f"s_dA{i}", [128, SF], es=es_1b) for i in range(2)]
    DBU = [k.sb(f"s_dBu{i}", [128, SF], es=es_1b) for i in range(2)]
    HH = [k.sb(f"s_h{i}", [128, SF], es=es_1b) for i in range(2)]
    HC = [k.sb(f"s_hC{i}", [128, SF], es=es_1b) for i in range(2)]
    DLY = [k.sb(f"s_dly{i}", [128, SS], es=es_1b) for i in range(2)]
    DU = [k.sb(f"s_du{i}", [128, SS], es=es_1b) for i in range(2)]
    UT = [k.sb(f"s_ut{i}", [128, SS], es=es_1b) for i in range(2)]
    ZT = [k.sb(f"s_zt{i}", [128, SS], es=es_1b) for i in range(2)]
    SPT = [k.sb(f"s_spt{i}", [128, SS], es=es_1b) for i in range(2)]
    YG = [k.sb(f"s_yg{i}", [128, NIT * SS], es=es_1b) for i in range(2)]
    YGH = [k.sb(f"s_ygh{i}", [128, NIT * SS], BF16, es=es_1b) for i in range(2)]
    YGL = [k.sb(f"s_ygl{i}", [128, NIT * SS], BF16, es=es_1b) for i in range(2)]
    MO = [k.sb(f"s_mo{i}", [128, SS], es=es_1b) for i in range(2)]
    MO2 = [k.sb(f"s_mo2{i}", [128, 128], es=es_1b) for i in range(2)]
    s_hcar = k.sb("s_hcar", [128, DS * NIT], es=es_1b)
    s_opwh = k.sb("s_opwh", [128, NDC * NIT * 128], BF16, es=es_1b)
    s_opwl = k.sb("s_opwl", [128, NDC * NIT * 128], BF16, es=es_1b)
    s_tw2 = k.sb("s_tw2", [128, NIT * 128], es=es_1b)

    k.v("memset", dict(ap=s_hcar[:], constant=0.0), w=[s_hcar])
    for dc in range(NDC):
        p.dma("sync", s_tw2[:].rearrange("p (n m) -> p n m", m=128),
              w_opT[:, 128 * dc:128 * (dc + 1)].rearrange(
                  "(n p) m -> p n m", p=128), sem="d_wop")
        split16(k, s_opwh[:, 1024 * dc:1024 * (dc + 1)],
                s_opwl[:, 1024 * dc:1024 * (dc + 1)],
                s_tw2[:], r=[s_tw2], w=[s_opwh, s_opwl])

    NGI = NST * NIT

    def stage_A(gi):
        st, it = gi // NIT, gi % NIT
        ip = gi % 2
        sp = st % 2
        s_brep, s_crep = BREP[sp], CREP[sp]
        if it == 0:
            NPB = max(1, 512 // SS)
            for half, dst in ((0, s_brep), (1, s_crep)):
                for n0 in range(0, DS, NPB):
                    bank = PSB[(n0 // NPB) % 2]
                    bk = PK[(n0 // NPB) % 2]
                    nn = min(NPB, DS - n0)
                    calls = []
                    for j in range(nn):
                        n = n0 + j
                        calls.append(("matmul", dict(
                            out=bank[:, SS * j:SS * (j + 1)],
                            lhsT=s_selbc[:, 16 * half + n:16 * half + n + 1
                                         ].broadcast_to([96, 128]),
                            rhs=m_xdbc[0:96, SS * st:SS * (st + 1)],
                            start=True, stop=True)))
                    k.tm(calls, r=[s_selbc, m_xdbc], w=[bk])
                    k.s("copy", dict(out=dst[:, SS * n0:SS * (n0 + nn)],
                                     in_=bank[:, 0:SS * nn]), r=[bk], w=[dst])
        s_dly, s_du, s_ut, s_zt = DLY[ip], DU[ip], UT[ip], ZT[ip]
        s_dA, s_dBu, s_sptmp = DA[ip], DBU[ip], SPT[ip]
        dtb_, dtk = PSB[2 + ip], PK[2 + ip]
        k.t("matmul", dict(out=dtb_[:, 0:SS],
                           lhsT=s_dtw[:, 128 * it:128 * (it + 1)],
                           rhs=m_xdbc[0:DTR, SS * st:SS * (st + 1)],
                           start=True, stop=True),
            r=[s_dtw, m_xdbc], w=[dtk])
        k.v("tensor_scalar", dict(out=s_dly[:], in0=dtb_[:, 0:SS],
                                  scalar1=s_dtb[:, it:it + 1], scalar2=None,
                                  op0=ALU.add), r=[dtk, s_dtb], w=[s_dly])
        softplus_inplace(k, s_dly[:], s_sptmp[:], s_dly, s_sptmp)
        p.dma("sync", s_ut[:], u_spill[128 * it:128 * (it + 1),
                                       SS * st:SS * (st + 1)],
              sem="d_uld" + str(ip))
        p.dma("sync", s_zt[:], z_spill[128 * it:128 * (it + 1),
                                       SS * st:SS * (st + 1)],
              sem="d_zld" + str(ip))
        k.v("tensor_mul", dict(out=s_du[:], in0=s_dly[:], in1=s_ut[:]),
            r=[s_dly, s_ut], w=[s_du])
        dA3 = s_dA[:].rearrange("p (n f) -> p n f", n=DS)
        dly3 = s_dly[:].unsqueeze(1).broadcast_to([128, DS, SS])
        A3 = s_A[:, DS * it:DS * (it + 1)].unsqueeze(2).broadcast_to(
            [128, DS, SS])
        k.g("tensor_tensor", dict(out=dA3, in0=dly3, in1=A3, op=ALU.mult),
            r=[s_dly, s_A], w=[s_dA])
        k.s("activation", dict(out=s_dA[:], in_=s_dA[:], func=AF.Exp),
            r=[s_dA], w=[s_dA])
        du3 = s_du[:].unsqueeze(1).broadcast_to([128, DS, SS])
        dBu3 = s_dBu[:].rearrange("p (n f) -> p n f", n=DS)
        br3 = s_brep[:].rearrange("p (n f) -> p n f", n=DS)
        k.g("tensor_tensor", dict(out=dBu3, in0=du3, in1=br3, op=ALU.mult),
            r=[s_du, s_brep], w=[s_dBu])

    def stage_B(gi):
        st, it = gi // NIT, gi % NIT
        ip = gi % 2
        sp = st % 2
        s_crep = CREP[sp]
        s_yg = YG[sp]
        s_ut, s_zt = UT[ip], ZT[ip]
        s_dA, s_dBu, s_h, s_hC, s_sptmp = (DA[ip], DBU[ip], HH[ip],
                                           HC[ip], SPT[ip])
        car = s_hcar[:, DS * it:DS * (it + 1)]
        car3 = car.rearrange("p (n f) -> p n f", f=1)
        dA0 = s_dA[:].rearrange("p (n f) -> p n f", n=DS)[:, :, 0:1]
        dBu0 = s_dBu[:].rearrange("p (n f) -> p n f", n=DS)[:, :, 0:1]
        tmpc3 = s_sptmp[:, 0:DS].rearrange("p (n f) -> p n f", f=1)
        k.v("tensor_tensor", dict(out=tmpc3, in0=dA0, in1=car3, op=ALU.mult),
            r=[s_dA, s_hcar], w=[s_sptmp])
        k.v("tensor_tensor", dict(out=dBu0, in0=dBu0, in1=tmpc3, op=ALU.add),
            r=[s_sptmp, s_dBu], w=[s_dBu])
        k.v("memset", dict(ap=dA0, constant=0.0), r=[], w=[s_dA])
        k.v("tensor_tensor_scan", dict(
            out=s_h[:], data0=s_dA[:], data1=s_dBu[:], initial=0.0,
            op0=ALU.mult, op1=ALU.add), r=[s_dA, s_dBu], w=[s_h])
        hlast = s_h[:].rearrange("p (n f) -> p n f", n=DS)[:, :, SS - 1:SS]
        k.v("tensor_copy", dict(out=car3, in_=hlast), r=[s_h], w=[s_hcar])
        h3 = s_h[:].rearrange("p (n f) -> p n f", n=DS)
        cr3 = s_crep[:].rearrange("p (n f) -> p n f", n=DS)
        hc3t = s_hC[:].rearrange("p (f n) -> p n f", n=DS)
        k.v("tensor_tensor", dict(out=hc3t, in0=h3, in1=cr3, op=ALU.mult),
            r=[s_h, s_crep], w=[s_hC])
        yg = s_yg[:, SS * it:SS * (it + 1)]
        k.v("tensor_reduce", dict(
            out=yg, in_=s_hC[:].rearrange("p (f n) -> p f n", n=DS),
            axis=AXX, op=ALU.add), r=[s_hC], w=[s_yg])
        k.v("scalar_tensor_tensor", dict(
            out=yg, in0=s_ut[:], scalar=s_D[:, it:it + 1], in1=yg,
            op0=ALU.mult, op1=ALU.add), r=[s_ut, s_D, s_yg], w=[s_yg])
        k.v("tensor_mul", dict(out=yg, in0=yg, in1=s_zt[:]),
            r=[s_yg, s_zt], w=[s_yg])

    def stripe_tail(st):
        sp = st % 2
        s_yg, s_ygh, s_ygl = YG[sp], YGH[sp], YGL[sp]
        split16(k, s_ygh[:], s_ygl[:], s_yg[:], r=[s_yg], w=[s_ygh, s_ygl])
        for dc in range(NDC):
            bank, bkk = PSB[4 + (dc % 2)], PK[4 + (dc % 2)]
            for it in range(NIT):
                mm3(k, bank[:, 0:SS],
                    s_opwh[:, 1024 * dc + 128 * it:1024 * dc + 128 * (it + 1)],
                    s_opwl[:, 1024 * dc + 128 * it:1024 * dc + 128 * (it + 1)],
                    s_ygh[:, SS * it:SS * (it + 1)],
                    s_ygl[:, SS * it:SS * (it + 1)],
                    start=(it == 0), stop=(it == NIT - 1),
                    r=[s_opwh, s_opwl, s_ygh, s_ygl], w=[bkk])
            s_mo = MO[dc % 2]
            s_mo2 = MO2[dc % 2]
            k.s("copy", dict(out=s_mo[:], in_=bank[:, 0:SS]), r=[bkk], w=[s_mo])
            for sub in range(SS // 128):
                tb, tbk = PSB[6 + (dc % 2)], PK[6 + (dc % 2)]
                k.t("transpose", dict(out=tb[:, 0:128],
                                      in_=s_mo[:, 128 * sub:128 * (sub + 1)],
                                      identity=sid[:]), r=[s_mo, sid], w=[tbk])
                k.s("copy", dict(out=s_mo2[:, 0:128], in_=tb[:, 0:128]),
                    r=[tbk], w=[s_mo2])
                p.dma("sync",
                      cc1_in[0, :].rearrange("(t d) -> t d", d=D)[
                          SS * st + 128 * sub:SS * st + 128 * (sub + 1),
                          128 * dc:128 * (dc + 1)],
                      s_mo2[:, 0:128], sem="d_cc1" + str(dc % 2))

    stage_A(0)
    for gi in range(NGI):
        if gi + 1 < NGI:
            stage_A(gi + 1)
        stage_B(gi)
        if gi % NIT == NIT - 1:
            stripe_tail(gi // NIT)
    k.emit_cc("ReduceScatter", ALU.add, PAIRS, cc1_in, cc1_out)
    if "mamba_mm" in k.cfg["DEBUG_OUT"]:
        o = k.dram("dbg_mamba_mm", [TC, D], kind="ExternalOutput")
        k.dbg_outs["mamba_mm"] = [TC, D]
        s_m1 = k.sb("s_m1", [128, D], es=es_1b)
        for tt in range(TC // 128):
            p.dma("sync", s_m1[:], cc1_out[0, :].rearrange(
                "(t d) -> t d", d=D)[128 * tt:128 * (tt + 1), :], sem="d_dbg1")
            p.dma("gpsimd", o[128 * tt:128 * (tt + 1), :], s_m1[:], sem="d_dbg")
    es_1b.close()
    es_m.close()
    p.barrier()

    if cfg.get("STOP_AFTER") == 1:
        return k, locals()
    HB = 8
    ln1_w = k.dram("ln1_w", [D], kind="ExternalInput")
    wq = k.dram("wq", [D, HB * 64], kind="ExternalInput")
    wk = k.dram("wk", [D, 128], kind="ExternalInput")
    wv = k.dram("wv", [D, 128], kind="ExternalInput")
    qn_w = k.dram("qn_w", [64], kind="ExternalInput")
    kn_w = k.dram("kn_w", [64], kind="ExternalInput")
    wg = k.dram("wg", [64, 64], kind="ExternalInput")
    wo_r = k.dram("wo_r", [4, 128, D], kind="ExternalInput")
    cc2_in = k.dram("cc2_in", [1, T * D])
    cc2_out = k.dram("cc2_out", [1, TC * D])
    _inv = 1.0 / 10000.0 ** (np.arange(0, 64, 2) / 64.0)
    _fr = np.arange(T)[:, None] * _inv[None, :]
    rope_cs = k.inline("rope_cs", np.concatenate(
        [np.cos(_fr), np.sin(_fr)], axis=1).astype(np.float32))
    triuT = k.inline("triuT", np.triu(np.ones((128, 128), np.float32)))

    NQT = T // 128
    es_2 = contextlib.ExitStack()
    a_gt = k.sb("a_gt", [128, 4 * T], es=es_2)
    s_triu = k.sb("s_triu", [128, 128], es=es_2)
    qt_sp_h = k.dram("qt_sp_h", [8, 64 * T], BF16)
    qt_sp_l = k.dram("qt_sp_l", [8, 64 * T], BF16)
    es_2q = contextlib.ExitStack()
    a_qsth = k.sb("a_qsth", [64, 128], BF16, es=es_2q)
    a_qstl = k.sb("a_qstl", [64, 128], BF16, es=es_2q)
    a_kth = k.sb("a_kth", [128, 2 * T], BF16, es=es_2q)
    a_ktl = k.sb("a_ktl", [128, 2 * T], BF16, es=es_2q)
    a_v = k.sb("a_v", [128, NQT * 130], es=es_2q)
    p.dma("sync", s_triu[:], triuT, sem="d_ini", serial=True)

    es_2a = contextlib.ExitStack()
    h_hnh = k.sb("h_hnh", [128, NDC * (T // 4)], BF16, es=es_2a)
    h_hnl = k.sb("h_hnl", [128, NDC * (T // 4)], BF16, es=es_2a)
    a_wqh = k.sb("a_wqh", [128, NDC * HB * 64], BF16, es=es_2a)
    a_wql = k.sb("a_wql", [128, NDC * HB * 64], BF16, es=es_2a)
    a_wkvh = k.sb("a_wkvh", [128, NDC * 256], BF16, es=es_2a)
    a_wkvl = k.sb("a_wkvl", [128, NDC * 256], BF16, es=es_2a)
    a_wtmp = k.sb("a_wtmp", [128, HB * 64], es=es_2a)
    a_x = k.sb("a_x2", [128, D], es=es_2a)
    a_t = k.sb("a_t2", [128, D], es=es_2a)
    a_sq = k.sb("a_sq2", [128, D], es=es_2a)
    a_qtile = k.sb("a_qtile", [128, HB * 64], es=es_2a)
    a_kvt = k.sb("a_kvt", [128, 256], es=es_2a)
    a_cs = k.sb("a_cs", [128, 64], es=es_2a)
    a_rr = k.sb("a_rr", [128, HB * 64], es=es_2a)
    a_ms = k.sb("a_ms", [128, 16], es=es_2a)
    s_ln1r = k.sb("s_ln1r", [128, D], es=es_2a)
    s_qnr = k.sb("s_qnr", [128, 64], es=es_2a)
    s_knr = k.sb("s_knr", [128, 64], es=es_2a)
    s_r1 = k.sb("s_r1", [1, D], es=es_2a)

    for (src, dst, ww) in ((ln1_w, s_ln1r, D), (qn_w, s_qnr, 64),
                           (kn_w, s_knr, 64)):
        p.dma("sync", s_r1[:, 0:ww], src[None, :], sem="d_ini", serial=True)
        for dc in range((ww + 127) // 128):
            wN = min(128, ww - 128 * dc)
            k.t("matmul", dict(out=PSB[0][:, 0:wN], lhsT=s_ones[:],
                               rhs=s_r1[:, 128 * dc:128 * dc + wN],
                               start=True, stop=True),
                r=[s_ones, s_r1], w=[PK[0]])
            k.s("copy", dict(out=dst[:, 128 * dc:128 * dc + wN],
                             in_=PSB[0][:, 0:wN]), r=[PK[0]], w=[dst])

    for (wsrc, hdst, ldst, ncols) in ((wq, a_wqh, a_wql, HB * 64),):
        for dc in range(NDC):
            p.dma("sync", a_wtmp[:, 0:ncols],
                  wsrc[128 * dc:128 * (dc + 1), :], sem="d_wq")
            split16(k, hdst[:, ncols * dc:ncols * (dc + 1)],
                    ldst[:, ncols * dc:ncols * (dc + 1)],
                    a_wtmp[:, 0:ncols], r=[a_wtmp], w=[hdst, ldst])
    for dc in range(NDC):
        p.dma("sync", a_wtmp[:, 0:128], wk[128 * dc:128 * (dc + 1), :],
              sem="d_wq")
        p.dma("sync", a_wtmp[:, 128:256], wv[128 * dc:128 * (dc + 1), :],
              sem="d_wq")
        split16(k, a_wkvh[:, 256 * dc:256 * (dc + 1)],
                a_wkvl[:, 256 * dc:256 * (dc + 1)],
                a_wtmp[:, 0:256], r=[a_wtmp], w=[a_wkvh, a_wkvl])

    for tt in range(NQT):
        k.v("memset", dict(ap=a_v[:, 130 * tt + 64:130 * tt + 65],
                           constant=1.0), w=[a_v])
        k.v("memset", dict(ap=a_v[:, 130 * tt + 129:130 * tt + 130],
                           constant=1.0), w=[a_v])

    TH = T // 4
    for hh in range(4):
        for ttl in range(TH // 128):
            tt = hh * (TH // 128) + ttl
            p.dma("sync", a_x[:], x0[128 * tt:128 * (tt + 1), :], sem="d_x2")
            emit_ln(k, a_x, a_t, a_sq, s_st8, EPS_A, s_ln1r)
            for dc in range(NDC):
                bank, bk = PSB[dc % 2], PK[dc % 2]
                k.t("transpose", dict(out=bank[:, 0:128],
                                      in_=a_t[:, 128 * dc:128 * (dc + 1)],
                                      identity=sid[:]), r=[a_t, sid], w=[bk])
                fo = dc * TH + 128 * ttl
                split16(k, h_hnh[:, fo:fo + 128], h_hnl[:, fo:fo + 128],
                        bank[:, 0:128], r=[bk], w=[h_hnh, h_hnl])
        for ttl in range(TH // 128):
            tt = hh * (TH // 128) + ttl
            bq, bqk = PSB[2], PK[2]
            bkv, bkvk = PSB[3], PK[3]
            for dc in range(NDC):
                mm3(k, bq[:, 0:HB * 64],
                    h_hnh[:, dc * TH + 128 * ttl:dc * TH + 128 * (ttl + 1)],
                    h_hnl[:, dc * TH + 128 * ttl:dc * TH + 128 * (ttl + 1)],
                    a_wqh[:, HB * 64 * dc:HB * 64 * (dc + 1)],
                    a_wql[:, HB * 64 * dc:HB * 64 * (dc + 1)],
                    start=(dc == 0), stop=(dc == NDC - 1),
                    r=[h_hnh, h_hnl, a_wqh, a_wql], w=[bqk])
                mm3(k, bkv[:, 0:256],
                    h_hnh[:, dc * TH + 128 * ttl:dc * TH + 128 * (ttl + 1)],
                    h_hnl[:, dc * TH + 128 * ttl:dc * TH + 128 * (ttl + 1)],
                    a_wkvh[:, 256 * dc:256 * (dc + 1)],
                    a_wkvl[:, 256 * dc:256 * (dc + 1)],
                    start=(dc == 0), stop=(dc == NDC - 1),
                    r=[h_hnh, h_hnl, a_wkvh, a_wkvl], w=[bkvk])
            for g in range(2):
                k.s("copy", dict(out=a_v[:, 130 * tt + 65 * g:
                                        130 * tt + 65 * g + 64],
                                 in_=bkv[:, 128 + 64 * g:128 + 64 * (g + 1)]),
                    r=[bkvk], w=[a_v])
            p.dma("sync", a_cs[:], rope_cs[128 * tt:128 * (tt + 1), :],
                  sem="d_cs")
            for (src_bank, srck, nh, dst, nrm) in (
                    (bq, bqk, HB, a_qtile, s_qnr), (bkv, bkvk, 2, a_kvt, s_knr)):
                cosb = a_cs[:, 0:32].unsqueeze(1).broadcast_to([128, nh, 32])
                sinb = a_cs[:, 32:64].unsqueeze(1).broadcast_to([128, nh, 32])
                x1v = src_bank[:, 0:64 * nh].rearrange(
                    "p (h d) -> p h d", d=64)[:, :, 0:32]
                x2v = src_bank[:, 0:64 * nh].rearrange(
                    "p (h d) -> p h d", d=64)[:, :, 32:64]
                r1v = dst[:, 0:64 * nh].rearrange(
                    "p (h d) -> p h d", d=64)[:, :, 0:32]
                r2v = dst[:, 0:64 * nh].rearrange(
                    "p (h d) -> p h d", d=64)[:, :, 32:64]
                tv = a_rr[:, 0:64 * nh].rearrange(
                    "p (h d) -> p h d", d=64)
                t1v, t2v = tv[:, :, 0:32], tv[:, :, 32:64]
                k.v("tensor_tensor", dict(out=t1v, in0=x1v, in1=cosb,
                                          op=ALU.mult),
                    r=[srck, a_cs], w=[a_rr])
                k.v("tensor_tensor", dict(out=t2v, in0=x2v, in1=sinb,
                                          op=ALU.mult),
                    r=[srck, a_cs], w=[a_rr])
                k.v("tensor_tensor", dict(out=r1v, in0=t1v, in1=t2v,
                                          op=ALU.subtract),
                    r=[a_rr], w=[dst])
                k.v("tensor_tensor", dict(out=t1v, in0=x1v, in1=sinb,
                                          op=ALU.mult),
                    r=[srck, a_cs], w=[a_rr])
                k.v("tensor_tensor", dict(out=t2v, in0=x2v, in1=cosb,
                                          op=ALU.mult),
                    r=[srck, a_cs], w=[a_rr])
                k.v("tensor_tensor", dict(out=r2v, in0=t1v, in1=t2v,
                                          op=ALU.add),
                    r=[a_rr], w=[dst])
                dv = dst[:, 0:64 * nh].rearrange("p (h d) -> p h d", d=64)
                k.s("activation", dict(out=a_rr[:, 0:64 * nh],
                                       in_=dst[:, 0:64 * nh], func=AF.Square),
                    r=[dst], w=[a_rr])
                k.v("tensor_reduce", dict(
                    out=a_ms[:, 0:nh],
                    in_=a_rr[:, 0:64 * nh].rearrange("p (h d) -> p h d", d=64),
                    axis=AXX, op=ALU.add), r=[a_rr], w=[a_ms])
                c_i64 = k.const_col(1.0 / 64.0)
                c_ea = k.const_col(EPS_A)
                k.s("activation", dict(out=a_ms[:, 8:8 + nh],
                                       in_=a_ms[:, 0:nh], func=AF.Sqrt,
                                       scale=c_i64, bias=c_ea),
                    r=[a_ms, "k_consts"], w=[a_ms])
                k.v("reciprocal", dict(out=a_ms[:, 0:nh], in_=a_ms[:, 8:8 + nh]),
                    r=[a_ms], w=[a_ms])
                rstdb = a_ms[:, 0:nh].unsqueeze(2).broadcast_to([128, nh, 64])
                k.v("tensor_tensor", dict(out=dv, in0=dv, in1=rstdb,
                                          op=ALU.mult), r=[dst, a_ms], w=[dst])
                nrmb = nrm[:].unsqueeze(1).broadcast_to([128, nh, 64])
                k.v("tensor_tensor", dict(out=dv, in0=dv, in1=nrmb,
                                          op=ALU.mult), r=[dst, nrm], w=[dst])
            for h in range(HB):
                bank, bk = PSB[4 + (h % 2)], PK[4 + (h % 2)]
                k.t("transpose", dict(out=bank[0:64, 0:128],
                                      in_=a_qtile[:, 64 * h:64 * (h + 1)],
                                      identity=sid[:]),
                    r=[a_qtile, sid], w=[bk])
                split16(k, a_qsth[:], a_qstl[:],
                        bank[0:64, 0:128], r=[bk], w=[a_qsth, a_qstl])
                qv_h = qt_sp_h[h, :].rearrange("(d t) -> d t", d=64)
                qv_l = qt_sp_l[h, :].rearrange("(d t) -> d t", d=64)
                p.dma("sync", qv_h[:, 128 * tt:128 * (tt + 1)], a_qsth[:],
                      sem="d_qsh")
                p.dma("sync", qv_l[:, 128 * tt:128 * (tt + 1)], a_qstl[:],
                      sem="d_qsl")
            for g in range(2):
                bank, bk = PSB[6], PK[6]
                k.t("transpose", dict(out=bank[0:64, 0:128],
                                      in_=a_kvt[:, 64 * g:64 * (g + 1)],
                                      identity=sid[:]),
                    r=[a_kvt, sid], w=[bk])
                fo = g * T + 128 * tt
                split16(k, a_kth[0:64, fo:fo + 128],
                        a_ktl[0:64, fo:fo + 128],
                        bank[0:64, 0:128], r=[bk], w=[a_kth, a_ktl])
    for g in range(2):
        p.dma("sync", a_kth[64:128, g * T:(g + 1) * T],
              a_kth[0:64, g * T:(g + 1) * T], sem=f"d_kdh{g}")
        p.dma("sync", a_ktl[64:128, g * T:(g + 1) * T],
              a_ktl[0:64, g * T:(g + 1) * T], sem=f"d_kdl{g}")
    es_2a.close()
    p.barrier()

    es_2b = contextlib.ExitStack()
    a_pt = k.sb("a_pt", [128, 512], es=es_2b)
    a_on = k.sb("a_on", [128, 512], es=es_2b)
    a_l1 = k.sb("a_l1", [1, 512], es=es_2b)
    a_gg = k.sb("a_gg", [64, 512], es=es_2b)
    s_wg2 = k.sb("s_wg2", [64, 64], es=es_2b)
    c_s8 = k.const_col(0.125)
    p.dma("sync", s_wg2[:], wg, sem="d_ini", serial=True)

    NQM = T // 512
    QW_H = [k.sb(f"a_qwh{i}", [128, 512], BF16, es=es_2b) for i in range(2)]
    QW_L = [k.sb(f"a_qwl{i}", [128, 512], BF16, es=es_2b) for i in range(2)]
    PT = [a_pt, k.sb("a_pt2", [128, 512], es=es_2b)]
    for h in range(HB):
        g = h // 4
        pr_h = (h % 2) * 64
        for qm in range(NQM):
            qpar = (h * NQM + qm) % 2
            a_qh, a_ql = QW_H[qpar], QW_L[qpar]
            nkc = 4 * (qm + 1)
            bo, bok = PSB[7], PK[7]
            qv_h = qt_sp_h[h, :].rearrange("(d t) -> d t", d=64)
            qv_l = qt_sp_l[h, :].rearrange("(d t) -> d t", d=64)
            p.dma("sync", a_qh[pr_h:pr_h + 64, :],
                  qv_h[:, 512 * qm:512 * (qm + 1)], sem="d_qlh" + str(qpar))
            p.dma("sync", a_ql[pr_h:pr_h + 64, :],
                  qv_l[:, 512 * qm:512 * (qm + 1)], sem="d_qll" + str(qpar))

            def emit_qk(kc):
                bs, bsk = PSB[4 + (kc % 2)], PK[4 + (kc % 2)]
                mm3(k, bs[:, 0:512],
                    a_kth[pr_h:pr_h + 64,
                          g * T + 128 * kc:g * T + 128 * (kc + 1)],
                    a_ktl[pr_h:pr_h + 64,
                          g * T + 128 * kc:g * T + 128 * (kc + 1)],
                    a_qh[pr_h:pr_h + 64, :],
                    a_ql[pr_h:pr_h + 64, :],
                    start=True, stop=True,
                    r=[a_kth, a_ktl, a_qh, a_ql], w=[bsk])

            emit_qk(0)
            for kc in range(nkc):
                bs, bsk = PSB[4 + (kc % 2)], PK[4 + (kc % 2)]
                pt = PT[kc % 2]
                ptk = _key(pt[:])
                k.s("activation", dict(out=pt[:], in_=bs[:, 0:512],
                                       func=AF.Exp, scale=c_s8),
                    r=[bsk, "k_consts"], w=[ptk])
                if kc + 1 < nkc:
                    emit_qk(kc + 1)
                j = kc - 4 * qm
                if j >= 0:
                    if j > 0:
                        k.v("memset", dict(ap=pt[:, 0:128 * j], constant=0.0),
                            r=[], w=[ptk])
                    k.v("tensor_mul", dict(out=pt[:, 128 * j:128 * (j + 1)],
                                           in0=pt[:, 128 * j:128 * (j + 1)],
                                           in1=s_triu[:]),
                        r=[s_triu, ptk], w=[ptk])
                k.t("matmul", dict(out=bo[0:65, 0:512],
                                   lhsT=a_v[:, 130 * kc + 65 * g:
                                            130 * kc + 65 * (g + 1)],
                                   rhs=pt[:], start=(kc == 0),
                                   stop=(kc == nkc - 1)),
                    r=[a_v, ptk], w=[bok])
            k.v("tensor_copy", dict(out=a_on[0:65, :], in_=bo[0:65, 0:512]),
                r=[bok], w=[a_on])
            k.v("reciprocal", dict(out=a_l1[:], in_=a_on[64:65, :]),
                r=[a_on], w=[a_l1])
            k.t("matmul", dict(out=PSB[3][0:64, 0:512], lhsT=s_ones[:, 0:64],
                               rhs=a_l1[:], start=True, stop=True),
                r=[s_ones, a_l1], w=[PK[3]])
            k.v("tensor_tensor", dict(out=a_on[0:64, :], in0=a_on[0:64, :],
                                      in1=PSB[3][0:64, 0:512], op=ALU.mult),
                r=[a_on, PK[3]], w=[a_on])
            k.t("matmul", dict(out=PSB[2][0:64, 0:512],
                               lhsT=s_wg2[:],
                               rhs=a_on[0:64, :], start=True, stop=True),
                r=[s_wg2, a_on], w=[PK[2]])
            k.s("activation", dict(out=a_gg[:], in_=PSB[2][0:64, 0:512],
                                   func=AF.Sigmoid), r=[PK[2]], w=[a_gg])
            fo_h = (h // 2) * T
            k.v("tensor_tensor", dict(
                out=a_gt[pr_h:pr_h + 64, fo_h + 512 * qm:fo_h + 512 * (qm + 1)],
                in0=a_on[0:64, :], in1=a_gg[:], op=ALU.mult),
                r=[a_on, a_gg], w=[a_gt])
    es_2b.close()
    es_2q.close()
    p.barrier()
    es_2b = contextlib.ExitStack()
    a_gth = k.sb("a_gth", [128, 4 * T], BF16, es=es_2b)
    a_gtl = k.sb("a_gtl", [128, 4 * T], BF16, es=es_2b)
    a_woh = k.sb("a_woh", [128, 4 * 128], BF16, es=es_2b)
    a_wol = k.sb("a_wol", [128, 4 * 128], BF16, es=es_2b)
    a_wot = k.sb("a_wot", [128, 128], es=es_2b)
    a_mo2 = k.sb("a_mo2", [128, 512], es=es_2b)
    a_mo3 = k.sb("a_mo3", [128, 128], es=es_2b)
    split16(k, a_gth[:], a_gtl[:], a_gt[:], r=[a_gt], w=[a_gth, a_gtl])
    for dc in range(NDC):
        for gg in range(4):
            p.dma("sync", a_wot[:], wo_r[gg, :, 128 * dc:128 * (dc + 1)],
                  sem="d_wo")
            split16(k, a_woh[:, 128 * gg:128 * (gg + 1)],
                    a_wol[:, 128 * gg:128 * (gg + 1)], a_wot[:],
                    r=[a_wot], w=[a_woh, a_wol])
        for ts in range(T // 512):
            bank, bk = PSB[0 + (ts % 2)], PK[0 + (ts % 2)]
            for gg in range(4):
                mm3(k, bank[:, 0:512],
                    a_woh[:, 128 * gg:128 * (gg + 1)],
                    a_wol[:, 128 * gg:128 * (gg + 1)],
                    a_gth[:, gg * T + 512 * ts:gg * T + 512 * (ts + 1)],
                    a_gtl[:, gg * T + 512 * ts:gg * T + 512 * (ts + 1)],
                    start=(gg == 0), stop=(gg == 3),
                    r=[a_woh, a_wol, a_gth, a_gtl], w=[bk])
            k.s("copy", dict(out=a_mo2[:], in_=bank[:, 0:512]), r=[bk],
                w=[a_mo2])
            for sub in range(4):
                tb2, tbk2 = PSB[6], PK[6]
                k.t("transpose", dict(out=tb2[:, 0:128],
                                      in_=a_mo2[:, 128 * sub:128 * (sub + 1)],
                                      identity=sid[:]), r=[a_mo2, sid], w=[tbk2])
                k.s("copy", dict(out=a_mo3[:, 0:128], in_=tb2[:, 0:128]),
                    r=[tbk2], w=[a_mo3])
                p.dma("sync",
                      cc2_in[0, :].rearrange("(t d) -> t d", d=D)[
                          512 * ts + 128 * sub:512 * ts + 128 * (sub + 1),
                          128 * dc:128 * (dc + 1)],
                      a_mo3[:, 0:128], sem="d_cc2")
    k.emit_cc("ReduceScatter", ALU.add, PAIRS, cc2_in, cc2_out)
    if "attn_mm" in k.cfg["DEBUG_OUT"]:
        o = k.dram("dbg_attn_mm", [TC, D], kind="ExternalOutput")
        k.dbg_outs["attn_mm"] = [TC, D]
        s_m1 = k.sb("s_m1b", [128, D], es=es_2b)
        for tt in range(TC // 128):
            p.dma("sync", s_m1[:], cc2_out[0, :].rearrange(
                "(t d) -> t d", d=D)[128 * tt:128 * (tt + 1), :], sem="d_dbg1")
            p.dma("gpsimd", o[128 * tt:128 * (tt + 1), :], s_m1[:], sem="d_dbg")
    es_2b.close()
    es_2.close()
    p.barrier()

    if cfg.get("STOP_AFTER") == 2:
        return k, locals()
    NTC = TC // 128
    CAP = k.CAP
    NSLOT = 8 * CAP
    NSL = CAP // 128
    HID = k.HID
    NUC = HID // 128
    x0o = k.dram("x0o", [TC, D], kind="ExternalInput")
    ln2_w = k.dram("ln2_w", [D], kind="ExternalInput")
    wr = k.dram("wr", [D, 8], kind="ExternalInput")
    w1 = k.dram("w1", [8, D, 2 * HID], kind="ExternalInput")
    w2 = k.dram("w2", [8, HID, D], kind="ExternalInput")
    moe_in = k.dram("moe_in", [NSLOT, D])
    moe_out = k.dram("moe_out", [NSLOT, D])
    erow8 = k.inline("erow8", np.tile(np.arange(8, dtype=np.float32), (128, 1)))
    lstrict = k.inline("lstrict",
                       np.triu(np.ones((128, 128), np.float32), 1).T.copy())

    es_3 = contextlib.ExitStack()
    q_x1 = k.sb("q_x1", [128, NTC * D], es=es_3)
    q_dst = k.sb("q_dst", [128, NTC], U32, es=es_3)

    es_3a = contextlib.ExitStack()
    b_xpp = k.sb("b_xpp", [128, NTC * D], es=es_3a)
    s_ln2r = k.sb("s_ln2r", [128, D], es=es_3a)
    s_wr = k.sb("s_wr", [128, NDC * 8], es=es_3a)
    s_er8 = k.sb("s_er8", [128, 8], es=es_3a)
    s_ls = k.sb("s_ls", [128, 128], es=es_3a)
    a_x = k.sb("a_x3", [128, D], es=es_3a)
    a_t = k.sb("a_t3", [128, D], es=es_3a)
    a_sq = k.sb("a_sq3", [128, D], es=es_3a)
    a_c1 = k.sb("a_c1", [128, D], es=es_3a)
    a_h2 = k.sb("a_h2", [128, D], es=es_3a)
    a_h2T = k.sb("a_h2T", [128, NDC * 128], es=es_3a)
    a_mx8 = k.sb("a_mx8", [128, 8], es=es_3a)
    a_ix8 = k.sb("a_ix8", [128, 8], U32, es=es_3a)
    a_ef = k.sb("a_ef", [128, 1], es=es_3a)
    a_oh = k.sb("a_oh", [128, 8], es=es_3a)
    a_tc = k.sb("a_tc", [128, 8], es=es_3a)
    a_rk = k.sb("a_rk", [128, 1], es=es_3a)
    a_di = k.sb("a_di", [128, 1], U32, es=es_3a)
    s_base = k.sb("s_base", [1, 8], es=es_3a)
    a_zz = k.sb("a_zz", [128, D], es=es_3a)

    p.dma("sync", s_er8[:], erow8, sem="d_ini", serial=True)
    p.dma("sync", s_ls[:], lstrict, sem="d_ini", serial=True)
    p.dma("sync", s_wr[:].rearrange("p (c m) -> p c m", m=8),
          wr.rearrange("(c p) m -> p c m", p=128), sem="d_ini", serial=True)
    s_r2 = k.sb("s_r2", [1, D], es=es_3a)
    p.dma("sync", s_r2[:], ln2_w[None, :], sem="d_ini", serial=True)
    for dc in range(NDC):
        k.t("matmul", dict(out=PSB[0][:, 0:128], lhsT=s_ones[:],
                           rhs=s_r2[:, 128 * dc:128 * (dc + 1)],
                           start=True, stop=True), r=[s_ones, s_r2], w=[PK[0]])
        k.s("copy", dict(out=s_ln2r[:, 128 * dc:128 * (dc + 1)],
                         in_=PSB[0][:, 0:128]), r=[PK[0]], w=[s_ln2r])
    k.v("memset", dict(ap=s_base[:], constant=0.0), w=[s_base])
    k.v("memset", dict(ap=a_zz[:], constant=0.0), w=[a_zz])
    for sz in range(NSLOT // 128):
        p.dma("sync", moe_in[128 * sz:128 * (sz + 1), :], a_zz[:], sem="d_mz",
              r=[a_zz])

    c1v = cc1_out[0, :].rearrange("(t d) -> t d", d=D)
    c2v = cc2_out[0, :].rearrange("(t d) -> t d", d=D)
    for tt in range(NTC):
        sl = slice(128 * tt, 128 * (tt + 1))
        xsl = slice(D * tt, D * (tt + 1))
        p.dma("sync", a_x[:], x0o[sl, :], sem="d_x3")
        emit_ln(k, a_x, a_t, a_sq, s_st8, None, None)
        p.dma("sync", a_c1[:], c2v[sl, :], sem="d_cl")
        k.v("tensor_add", dict(out=b_xpp[:, xsl], in0=a_x[:], in1=a_c1[:]),
            r=[a_x, a_c1], w=[b_xpp])
        k.v("tensor_add", dict(out=q_x1[:, xsl], in0=b_xpp[:, xsl], in1=a_x[:]),
            r=[b_xpp, a_x], w=[q_x1])
        p.dma("sync", a_c1[:], c1v[sl, :], sem="d_cl")
        k.v("tensor_add", dict(out=q_x1[:, xsl], in0=q_x1[:, xsl], in1=a_c1[:]),
            r=[q_x1, a_c1], w=[q_x1])
        p.dma("sync", a_c1[:], x0o[sl, :], sem="d_x3b")
        k.v("tensor_add", dict(out=q_x1[:, xsl], in0=q_x1[:, xsl], in1=a_c1[:]),
            r=[q_x1, a_c1], w=[q_x1])
        k.s("activation", dict(out=a_sq[:], in_=b_xpp[:, xsl], func=AF.Square,
                               accum_out=s_st8[:, 0:1]),
            r=[b_xpp], w=[a_sq, "a_st"])
        c_invd = k.const_col(1.0 / D)
        c_ea2 = k.const_col(EPS_A)
        k.s("activation", dict(out=s_st8[:, 1:2], in_=s_st8[:, 0:1],
                               func=AF.Sqrt, scale=c_invd, bias=c_ea2),
            r=["a_st", "k_consts"], w=["a_st"])
        k.v("reciprocal", dict(out=s_st8[:, 2:3], in_=s_st8[:, 1:2]),
            r=["a_st"], w=["a_st"])
        k.v("scalar_tensor_tensor", dict(
            out=a_h2[:], in0=b_xpp[:, xsl], scalar=s_st8[:, 2:3],
            in1=s_ln2r[:], op0=ALU.mult, op1=ALU.mult),
            r=[b_xpp, "a_st", s_ln2r], w=[a_h2])
        for dc in range(NDC):
            bank, bk = PSB[1 + (dc % 2)], PK[1 + (dc % 2)]
            k.t("transpose", dict(out=bank[:, 0:128],
                                  in_=a_h2[:, 128 * dc:128 * (dc + 1)],
                                  identity=sid[:]), r=[a_h2, sid], w=[bk])
            k.s("copy", dict(out=a_h2T[:, 128 * dc:128 * (dc + 1)],
                             in_=bank[:, 0:128]), r=[bk], w=[a_h2T])
        for dc in range(NDC):
            k.t("matmul", dict(out=PSB[3][:, 0:8],
                               lhsT=a_h2T[:, 128 * dc:128 * (dc + 1)],
                               rhs=s_wr[:, 8 * dc:8 * (dc + 1)],
                               start=(dc == 0), stop=(dc == NDC - 1)),
                r=[a_h2T, s_wr], w=[PK[3]])
        k.v("tensor_copy", dict(out=a_tc[:], in_=PSB[3][:, 0:8]),
            r=[PK[3]], w=[a_tc])
        k.v("max", dict(out=a_mx8[:], in_=a_tc[:]), r=[a_tc], w=[a_mx8])
        k.v("max_index", dict(out=a_ix8[:], in_max=a_mx8[:], in_values=a_tc[:]),
            r=[a_mx8, a_tc], w=[a_ix8])
        k.v("tensor_copy", dict(out=a_ef[:], in_=a_ix8[:, 0:1]),
            r=[a_ix8], w=[a_ef])
        efb = a_ef[:].broadcast_to([128, 8])
        k.v("tensor_tensor", dict(out=a_oh[:], in0=efb, in1=s_er8[:],
                                  op=ALU.is_equal), r=[a_ef, s_er8], w=[a_oh])
        k.t("matmul", dict(out=PSB[3][:, 0:8], lhsT=s_ls[:], rhs=a_oh[:],
                           start=True, stop=True), r=[s_ls, a_oh], w=[PK[3]])
        k.v("tensor_copy", dict(out=a_tc[:], in_=PSB[3][:, 0:8]),
            r=[PK[3]], w=[a_tc])
        k.t("matmul", dict(out=PSB[4][0:1, 0:8], lhsT=ones_colT(k),
                           rhs=a_oh[:], start=True, stop=True),
            r=["k_onec", a_oh], w=[PK[4]])
        k.t("matmul", dict(out=PSB[5][:, 0:8], lhsT=s_ones[:], rhs=s_base[:],
                           start=True, stop=True), r=[s_ones, s_base],
            w=[PK[5]])
        k.v("tensor_tensor", dict(out=a_tc[:], in0=a_tc[:],
                                  in1=PSB[5][:, 0:8], op=ALU.add),
            r=[a_tc, PK[5]], w=[a_tc])
        k.v("tensor_tensor", dict(out=a_tc[:], in0=a_tc[:], in1=a_oh[:],
                                  op=ALU.mult), r=[a_tc, a_oh], w=[a_tc])
        k.v("tensor_reduce", dict(out=a_rk[:], in_=a_tc[:], axis=AXX,
                                  op=ALU.add), r=[a_tc], w=[a_rk])
        k.v("scalar_tensor_tensor", dict(
            out=a_rk[:], in0=a_ef[:], scalar=float(CAP), in1=a_rk[:],
            op0=ALU.mult, op1=ALU.add), r=[a_ef, a_rk], w=[a_rk])
        k.v("tensor_scalar_min", dict(out=a_rk[:], in0=a_rk[:],
                                      scalar1=float(NSLOT - 1)),
            r=[a_rk], w=[a_rk])
        k.v("tensor_copy", dict(out=a_di[:], in_=a_rk[:]), r=[a_rk], w=[a_di])
        k.v("tensor_copy", dict(out=q_dst[:, tt:tt + 1], in_=a_di[:]),
            r=[a_di], w=[q_dst])
        k.v("tensor_tensor", dict(out=s_base[:], in0=s_base[:],
                                  in1=PSB[4][0:1, 0:8], op=ALU.add),
            r=[s_base, PK[4]], w=[s_base])
        p.idma("gpsimd", moe_in, (a_di[:], 0), a_h2[:], None, sem="d_sc",
               r=[], w=[])
    es_3a.close()
    p.barrier()

    es_3b = contextlib.ExitStack()
    XET = [k.sb(f"e_xeT{i}", [128, NDC * CAP], es=es_3b) for i in range(2)]
    ACT2 = [k.sb("e_act0", [128, NUC * CAP], es=es_3b)] * 2
    W1A = [k.sb(f"e_w1a{i}", [128, NDC * 128], es=es_3b) for i in range(2)]
    W1B = [k.sb(f"e_w1b{i}", [128, NDC * 128], es=es_3b) for i in range(2)]
    W2B = [k.sb(f"e_w2_{i}", [128, NUC * 128], es=es_3b) for i in range(2)]
    SIG2 = [k.sb(f"e_sig{i}", [128, CAP], es=es_3b) for i in range(2)]
    YT2 = [k.sb("e_yT0", [128, NDC * CAP], es=es_3b)] * 2
    YTT = [k.sb(f"e_yt{i}", [128, D], es=es_3b) for i in range(2)]
    YTG = [k.sb(f"e_ytg{i}", [128, D], es=es_3b) for i in range(2)]
    for e in range(8):
        ep = e % 2
        e_xeT, e_act, e_sig = XET[ep], ACT2[ep], SIG2[ep]
        e_yT, e_yt, e_ytg = YT2[ep], YTT[ep], YTG[ep]
        for sl2 in range(NSL):
            p.dma("sync", e_yt[:], moe_in[e * CAP + 128 * sl2:
                                          e * CAP + 128 * (sl2 + 1), :],
                  sem="d_mi" + str(ep))
            for dc in range(NDC):
                bank, bk = PSB[0 + (dc % 2)], PK[0 + (dc % 2)]
                k.t("transpose", dict(out=bank[:, 0:128],
                                      in_=e_yt[:, 128 * dc:128 * (dc + 1)],
                                      identity=sid[:]), r=[e_yt, sid], w=[bk])
                k.s("copy", dict(
                    out=e_xeT[:, CAP * dc + 128 * sl2:CAP * dc + 128 * (sl2 + 1)],
                    in_=bank[:, 0:128]), r=[bk], w=[e_xeT])
        for uc in range(NUC):
            wa, wb = W1A[(e * NUC + uc) % 2], W1B[(e * NUC + uc) % 2]
            p.dma("sync", wa[:].rearrange("p (c m) -> p c m", m=128),
                  w1[e, :, 128 * uc:128 * (uc + 1)].rearrange(
                      "(c p) m -> p c m", p=128), sem="d_w1a" + str((e * NUC + uc) % 2))
            p.dma("sync", wb[:].rearrange("p (c m) -> p c m", m=128),
                  w1[e, :, HID + 128 * uc:HID + 128 * (uc + 1)].rearrange(
                      "(c p) m -> p c m", p=128), sem="d_w1b" + str((e * NUC + uc) % 2))
            bka, bkb = PSB[2 + 2 * (uc % 2)], PSB[3 + 2 * (uc % 2)]
            kka, kkb = PK[2 + 2 * (uc % 2)], PK[3 + 2 * (uc % 2)]
            for dc in range(NDC):
                k.t("matmul", dict(out=bka[:, 0:CAP],
                                   lhsT=wa[:, 128 * dc:128 * (dc + 1)],
                                   rhs=e_xeT[:, CAP * dc:CAP * (dc + 1)],
                                   start=(dc == 0), stop=(dc == NDC - 1)),
                    r=[wa, e_xeT], w=[kka])
                k.t("matmul", dict(out=bkb[:, 0:CAP],
                                   lhsT=wb[:, 128 * dc:128 * (dc + 1)],
                                   rhs=e_xeT[:, CAP * dc:CAP * (dc + 1)],
                                   start=(dc == 0), stop=(dc == NDC - 1)),
                    r=[wb, e_xeT], w=[kkb])
            k.s("activation", dict(out=e_sig[:], in_=bkb[:, 0:CAP],
                                   func=AF.Sigmoid), r=[kkb], w=[e_sig])
            k.v("tensor_tensor", dict(out=e_act[:, CAP * uc:CAP * (uc + 1)],
                                      in0=bka[:, 0:CAP], in1=e_sig[:],
                                      op=ALU.mult), r=[kka, e_sig], w=[e_act])
        for dcol in range(NDC):
            w2b = W2B[(e * NDC + dcol) % 2]
            p.dma("sync", w2b[:].rearrange("p (c m) -> p c m", m=128),
                  w2[e, :, 128 * dcol:128 * (dcol + 1)].rearrange(
                      "(c p) m -> p c m", p=128), sem="d_w2" + str((e * NDC + dcol) % 2))
            bank, bk = PSB[6 + (dcol % 2)], PK[6 + (dcol % 2)]
            for hc in range(NUC):
                k.t("matmul", dict(out=bank[:, 0:CAP],
                                   lhsT=w2b[:, 128 * hc:128 * (hc + 1)],
                                   rhs=e_act[:, CAP * hc:CAP * (hc + 1)],
                                   start=(hc == 0), stop=(hc == NUC - 1)),
                    r=[w2b, e_act], w=[bk])
            k.s("copy", dict(out=e_yT[:, CAP * dcol:CAP * (dcol + 1)],
                             in_=bank[:, 0:CAP]), r=[bk], w=[e_yT])
        for sl2 in range(NSL):
            for dc in range(NDC):
                bank, bk = PSB[0 + (dc % 2)], PK[0 + (dc % 2)]
                k.t("transpose", dict(
                    out=bank[:, 0:128],
                    in_=e_yT[:, CAP * dc + 128 * sl2:CAP * dc + 128 * (sl2 + 1)],
                    identity=sid[:]), r=[e_yT, sid], w=[bk])
                k.s("copy", dict(out=e_ytg[:, 128 * dc:128 * (dc + 1)],
                                 in_=bank[:, 0:128]), r=[bk], w=[e_ytg])
            p.dma("sync", moe_out[e * CAP + 128 * sl2:
                                  e * CAP + 128 * (sl2 + 1), :], e_ytg[:],
                  sem="d_mo" + str(ep))
    es_3b.close()
    p.barrier()

    es_3c = contextlib.ExitStack()
    g_y = k.sb("g_y", [128, D], es=es_3c)
    g_i = k.sb("g_i", [128, 1], U32, es=es_3c)
    for tt in range(NTC):
        xsl = slice(D * tt, D * (tt + 1))
        k.v("tensor_copy", dict(out=g_i[:], in_=q_dst[:, tt:tt + 1]),
            r=[q_dst], w=[g_i])
        p.idma("gpsimd", g_y[:], None, moe_out, (g_i[:], 0), sem="d_gy",
               r=[], w=[])
        k.v("tensor_add", dict(out=q_x1[:, xsl], in0=q_x1[:, xsl], in1=g_y[:]),
            r=[q_x1, g_y], w=[q_x1])
    k.debug("x1", q_x1[:], [128, NTC * D])
    es_3c.close()
    p.barrier()

    if cfg.get("STOP_AFTER") == 3:
        k.phase_stacks.append(es_3)
        return k, locals()
    a_param = k.dram("a_param", [D], kind="ExternalInput")
    w_in = k.dram("w_in", [8, 128, 128], kind="ExternalInput")
    w_a = k.dram("w_a", [8, 128, 128], kind="ExternalInput")
    is_odd = k.dram("is_odd", [1, 1], kind="ExternalInput")
    wug = k.dram("wug", [2, k.I3, D], kind="ExternalInput")
    wd = k.dram("wd", [D, k.I3], kind="ExternalInput")
    I3 = k.I3
    NI3 = I3 // 128
    NTS = TC // 512 if TC >= 512 else 1
    WTS = min(512, TC)
    cc3_in = k.dram("cc3_in", [1, 8])
    cc3_out = k.dram("cc3_out", [8, 8])
    cc4_in = k.dram("cc4_in", [1, D])
    cc4_out = k.dram("cc4_out", [1, D])
    cc5_in = k.dram("cc5_in", [1, 8])
    cc5_out = k.dram("cc5_out", [8, 8])
    cc6_in = k.dram("cc6_in", [1, 8])
    cc6_out = k.dram("cc6_out", [8, 8])
    cc7_in = k.dram("cc7_in", [1, 8])
    cc7_out = k.dram("cc7_out", [8, 8])
    ab_spill = k.dram("ab_spill", [2 * D, TC])
    up_spill = k.dram("up_spill", [I3, TC])
    gate_spill = k.dram("gate_spill", [I3, TC])
    g2_spill = k.dram("g2_spill", [I3, TC])
    op_spill = k.dram("op_spill", [I3, TC])
    wqh_sp = k.dram("wqh_sp", [3 * I3, D], BF16)
    wql_sp = k.dram("wql_sp", [3 * I3, D], BF16)

    WTS0 = min(512, TC)
    es_4a = contextlib.ExitStack()
    s_sc = k.sb("s_sc", [1, 16], es=es_4a)
    s_scr = k.sb("s_scr", [128, 12], es=es_4a)
    g_cc = k.sb("g_cc", [1, 16], es=es_4a)
    UTB = [k.sb(f"u_t{i}", [128, WTS0], es=es_4a) for i in range(2)]
    u_mx = k.sb("u_mx", [128, 4], es=es_4a)
    USG = [k.sb(f"u_sg{i}", [128, WTS0], es=es_4a) for i in range(2)]
    UIT = [k.sb(f"u_it{i}", [128, WTS0], I32, es=es_4a) for i in range(2)]
    u_t, u_sg, u_it = UTB[0], USG[0], UIT[0]
    xq_sp_h = k.dram("xq_sp_h", [D, TC], BF16)
    xq_sp_l = k.dram("xq_sp_l", [D, TC], BF16)
    f_xqh = k.sb("f_xqh", [128, WTS0], BF16, es=es_4a)
    f_xql = k.sb("f_xql", [128, WTS0], BF16, es=es_4a)
    xq_th = k.sb("xq_th", [128, NDC * WTS0], BF16, es=es_4a)
    xq_tl = k.sb("xq_tl", [128, NDC * WTS0], BF16, es=es_4a)
    es_4f = contextlib.ExitStack()
    f_xn1 = k.sb("f_xn1", [128, NDC * TC], es=es_4f)
    es_4b = contextlib.ExitStack()
    a_x = k.sb("a_x4", [128, D], es=es_4b)
    a_t = k.sb("a_t4", [128, D], es=es_4b)
    a_sq = k.sb("a_sq4", [128, D], es=es_4b)
    a_am = k.sb("a_am", [128, 2], es=es_4b)
    for tt in range(NTC):
        xsl = slice(D * tt, D * (tt + 1))
        k.v("tensor_copy", dict(out=a_x[:], in_=q_x1[:, xsl]),
            r=[q_x1], w=[a_x])
        emit_ln(k, a_x, a_t, a_sq, s_st8, None, None)
        k.v("tensor_reduce", dict(out=a_am[:, 0:1], in_=a_x[:], axis=AXX,
                                  op=ALU.max, apply_absolute_value=True),
            r=[a_x], w=[a_am])
        if tt == 0:
            k.v("tensor_copy", dict(out=a_am[:, 1:2], in_=a_am[:, 0:1]),
                r=[a_am], w=[a_am])
        else:
            k.v("tensor_tensor", dict(out=a_am[:, 1:2], in0=a_am[:, 1:2],
                                      in1=a_am[:, 0:1], op=ALU.max),
                r=[a_am], w=[a_am])
        for dc in range(NDC):
            bank, bk = PSB[dc % 2], PK[dc % 2]
            k.t("transpose", dict(out=bank[:, 0:128],
                                  in_=a_x[:, 128 * dc:128 * (dc + 1)],
                                  identity=sid[:]), r=[a_x, sid], w=[bk])
            k.s("copy", dict(out=f_xn1[:, dc * TC + 128 * tt:
                                       dc * TC + 128 * (tt + 1)],
                             in_=bank[:, 0:128]), r=[bk], w=[f_xn1])
    if cfg.get("STOP_AFTER") == 31:
        k.phase_stacks.append(es_3)
        k.phase_stacks.append(es_4a)
        k.phase_stacks.append(es_4f)
        k.phase_stacks.append(es_4b)
        return k, locals()
    k.t("transpose", dict(out=PSB[2][0:1, 0:128], in_=a_am[:, 1:2],
                          identity=sid[:]), r=[a_am, sid], w=[PK[2]])
    k.v("tensor_reduce", dict(out=s_sc[:, 0:1], in_=PSB[2][0:1, 0:128],
                              axis=AXX, op=ALU.max), r=[PK[2]], w=[s_sc])
    k.v("memset", dict(ap=s_sc[:, 1:8], constant=0.0), w=[s_sc])
    p.dma("sync", cc3_in[0, :], s_sc[0:1, 0:8], sem="d_ccx")
    k.emit_cc("AllGather", ALU.bypass, ALL8, cc3_in, cc3_out)

    if cfg.get("STOP_AFTER") == 32:
        k.phase_stacks.append(es_3)
        k.phase_stacks.append(es_4a)
        k.phase_stacks.append(es_4f)
        k.phase_stacks.append(es_4b)
        return k, locals()
    s_sptmp2 = k.sb("s_sptmp2", [128, 512], es=es_4b)
    b_car = k.sb("b_car", [128, 16], es=es_4b)
    s_odd = k.sb("s_odd", [128, 1], es=es_4b)
    s_odd1 = k.sb("s_odd1", [1, 1], es=es_4b)
    p.dma("sync", s_odd1[:], is_odd, sem="d_io")
    k.t("matmul", dict(out=PSB[7][:, 0:1], lhsT=s_ones[:], rhs=s_odd1[:],
                       start=True, stop=True), r=[s_ones, s_odd1], w=[PK[7]])
    k.s("copy", dict(out=s_odd[:], in_=PSB[7][:, 0:1]), r=[PK[7]], w=[s_odd])
    BGA = [k.sb(f"b_ga{i}", [128, TC], es=es_4b) for i in range(2)]
    BBB = [k.sb(f"b_bb{i}", [128, TC], es=es_4b) for i in range(2)]
    BWB = [k.sb(f"b_wblk{i}", [128, 256], es=es_4b) for i in range(2)]
    b_spa = k.sb("b_spa", [128, NDC], es=es_4b)
    b_spt = k.sb("b_spt", [128, NDC], es=es_4b)
    p.dma("sync", b_spa[:], a_param.rearrange("(n p) -> p n", p=128),
          sem="d_ap", loose=True)
    softplus_inplace(k, b_spa[:], b_spt[:], b_spa, b_spt)
    k.v("tensor_scalar_mul", dict(out=b_spa[:], in0=b_spa[:], scalar1=-8.0),
        r=[b_spa], w=[b_spa])
    for dt in range(NDC):
        b_ga, b_bb, b_wblk = BGA[dt % 2], BBB[dt % 2], BWB[dt % 2]
        p.dma("sync", b_wblk[:, 0:128], w_in[dt], sem="d_wb" + str(dt % 2))
        p.dma("sync", b_wblk[:, 128:256], w_a[dt], sem="d_wb2" + str(dt % 2))
        for ts in range(NTS):
            bank, bk = PSB[2 + (ts % 2)], PK[2 + (ts % 2)]
            bank2, bk2 = PSB[4 + (ts % 2)], PK[4 + (ts % 2)]
            k.t("matmul", dict(out=bank[:, 0:WTS], lhsT=b_wblk[:, 0:128],
                               rhs=f_xn1[:, dt * TC + WTS * ts:
                                         dt * TC + WTS * (ts + 1)],
                               start=True, stop=True),
                r=[b_wblk, f_xn1], w=[bk])
            k.t("matmul", dict(out=bank2[:, 0:WTS], lhsT=b_wblk[:, 128:256],
                               rhs=f_xn1[:, dt * TC + WTS * ts:
                                          dt * TC + WTS * (ts + 1)],
                               start=True, stop=True),
                r=[b_wblk, f_xn1], w=[bk2])
            k.s("activation", dict(out=b_ga[:, WTS * ts:WTS * (ts + 1)],
                                   in_=bank2[:, 0:WTS], func=AF.Sigmoid),
                r=[bk2], w=[b_ga])
            k.v("tensor_scalar", dict(
                out=b_ga[:, WTS * ts:WTS * (ts + 1)],
                in0=b_ga[:, WTS * ts:WTS * (ts + 1)],
                scalar1=b_spa[:, dt:dt + 1], scalar2=None, op0=ALU.mult),
                r=[b_ga, b_spa], w=[b_ga])
            k.s("activation", dict(out=b_bb[:, WTS * ts:WTS * (ts + 1)],
                                   in_=b_ga[:, WTS * ts:WTS * (ts + 1)],
                                   func=AF.Exp, scale=2.0), r=[b_ga], w=[b_bb])
            k.v("tensor_scalar", dict(
                out=b_bb[:, WTS * ts:WTS * (ts + 1)],
                in0=b_bb[:, WTS * ts:WTS * (ts + 1)],
                scalar1=-1.0, scalar2=1.0, op0=ALU.mult, op1=ALU.add),
                r=[b_bb], w=[b_bb])
            k.s("activation", dict(out=b_bb[:, WTS * ts:WTS * (ts + 1)],
                                   in_=b_bb[:, WTS * ts:WTS * (ts + 1)],
                                   func=AF.Sqrt), r=[b_bb], w=[b_bb])
            k.s("activation", dict(out=s_sptmp2[:, 0:WTS], in_=bank[:, 0:WTS],
                                   func=AF.Sigmoid), r=[bk], w=[s_sptmp2])
            k.v("tensor_tensor", dict(out=s_sptmp2[:, 0:WTS],
                                      in0=s_sptmp2[:, 0:WTS],
                                      in1=f_xn1[:, dt * TC + WTS * ts:
                                                dt * TC + WTS * (ts + 1)],
                                      op=ALU.mult), r=[s_sptmp2, f_xn1],
                w=[s_sptmp2])
            k.v("tensor_tensor", dict(out=b_bb[:, WTS * ts:WTS * (ts + 1)],
                                      in0=b_bb[:, WTS * ts:WTS * (ts + 1)],
                                      in1=s_sptmp2[:, 0:WTS], op=ALU.mult),
                r=[b_bb, s_sptmp2], w=[b_bb])
            k.s("activation", dict(out=b_ga[:, WTS * ts:WTS * (ts + 1)],
                                   in_=b_ga[:, WTS * ts:WTS * (ts + 1)],
                                   func=AF.Exp), r=[b_ga], w=[b_ga])
        p.dma("sync", ab_spill[128 * dt:128 * (dt + 1), :], b_ga[:],
              sem="d_ab")
        p.dma("sync", ab_spill[D + 128 * dt:D + 128 * (dt + 1), :], b_bb[:],
              sem="d_ab")
        k.v("tensor_tensor_scan", dict(out=b_bb[:], data0=b_ga[:],
                                       data1=b_bb[:], initial=0.0,
                                       op0=ALU.mult, op1=ALU.add),
            r=[b_ga, b_bb], w=[b_bb])
        k.v("tensor_copy", dict(out=b_car[:, dt:dt + 1],
                                in_=b_bb[:, TC - 1:TC]), r=[b_bb], w=[b_car])
    if cfg.get("STOP_AFTER") == 33:
        k.phase_stacks.append(es_3)
        k.phase_stacks.append(es_4a)
        k.phase_stacks.append(es_4f)
        k.phase_stacks.append(es_4b)
        return k, locals()
    k.v("tensor_scalar", dict(out=b_car[:, 8:16], in0=b_car[:, 0:8],
                              scalar1=s_odd[:, 0:1], scalar2=None,
                              op0=ALU.mult), r=[b_car, s_odd], w=[b_car])
    k.v("tensor_sub", dict(out=b_car[:, 8:16], in0=b_car[:, 0:8],
                           in1=b_car[:, 8:16]), r=[b_car], w=[b_car])
    p.dma("sync", cc4_in.rearrange("a (n p) -> p (a n)", p=128),
          b_car[:, 8:16], sem="d_cc4", loose=True)
    k.emit_cc("AllReduce", ALU.add, PAIRS, cc4_in, cc4_out)
    p.dma("sync", b_car[:, 8:16], cc4_out.rearrange("a (n p) -> p (a n)", p=128),
          sem="d_cc4b", loose=True)
    k.v("tensor_scalar", dict(out=b_car[:, 8:16], in0=b_car[:, 8:16],
                              scalar1=s_odd[:, 0:1], scalar2=None,
                              op0=ALU.mult), r=[b_car, s_odd], w=[b_car])
    for dt in range(NDC):
        b_ga, b_bb = BGA[dt % 2], BBB[dt % 2]
        p.dma("sync", b_ga[:], ab_spill[128 * dt:128 * (dt + 1), :],
              sem="d_ab3" + str(dt % 2))
        p.dma("sync", b_bb[:], ab_spill[D + 128 * dt:D + 128 * (dt + 1), :],
              sem="d_ab4" + str(dt % 2))
        k.v("tensor_tensor_scan", dict(out=b_bb[:], data0=b_ga[:],
                                       data1=b_bb[:],
                                       initial=b_car[:, 8 + dt:9 + dt],
                                       op0=ALU.mult, op1=ALU.add),
            r=[b_ga, b_bb, b_car], w=[b_bb])
        for tt in range(NTC):
            bank, bk = PSB[6], PK[6]
            k.t("transpose", dict(out=bank[:, 0:128],
                                  in_=b_bb[:, 128 * tt:128 * (tt + 1)],
                                  identity=sid[:]), r=[b_bb, sid], w=[bk])
            k.v("tensor_tensor", dict(
                out=q_x1[:, D * tt + 128 * dt:D * tt + 128 * (dt + 1)],
                in0=q_x1[:, D * tt + 128 * dt:D * tt + 128 * (dt + 1)],
                in1=bank[:, 0:128], op=ALU.add), r=[q_x1, bk], w=[q_x1])
    k.debug("y1x", q_x1[:], [128, NTC * D])
    es_4b.close()
    p.barrier()
    if cfg.get("STOP_AFTER") == 4:
        k.phase_stacks.append(es_3)
        k.phase_stacks.append(es_4a)
        k.phase_stacks.append(es_4f)
        return k, locals()
    es_4c = contextlib.ExitStack()
    wq_up_h = k.dram("wq_up_h", [D, I3], BF16)
    wq_up_l = k.dram("wq_up_l", [D, I3], BF16)
    wq_g_h = k.dram("wq_g_h", [D, I3], BF16)
    wq_g_l = k.dram("wq_g_l", [D, I3], BF16)
    wq_d_h = k.dram("wq_d_h", [I3, D], BF16)
    wq_d_l = k.dram("wq_d_l", [I3, D], BF16)
    DWT = [k.sb(f"d_wt{i}", [128, I3], es=es_4c) for i in range(2)]
    DBM = [k.sb(f"d_bm{i}", [128, I3 // 32], es=es_4c) for i in range(2)]
    DBI = [k.sb(f"d_bi{i}", [128, I3 // 32], es=es_4c) for i in range(2)]
    DIT = [k.sb("d_it0", [128, I3], I32, es=es_4c)] * 2
    DTH = [k.sb(f"d_th{i}", [128, 128], BF16, es=es_4c) for i in range(2)]
    DTL = [k.sb(f"d_tl{i}", [128, 128], BF16, es=es_4c) for i in range(2)]

    def dequant_one(src_ap, n_rows, n_cols, out_h, out_l, semtag):
        nb = n_cols // 32
        for ot in range(n_rows // 128):
            d_wt, d_bm, d_bi = DWT[ot % 2], DBM[ot % 2], DBI[ot % 2]
            d_it, d_th, d_tl = DIT[ot % 2], DTH[ot % 2], DTL[ot % 2]
            p.dma("gpsimd", d_wt[:, 0:n_cols],
                  src_ap[128 * ot:128 * (ot + 1), :], sem="d_dqw" + str(ot % 2))
            v3 = d_wt[:, 0:n_cols].rearrange("p (b e) -> p b e", e=32)
            k.v("tensor_reduce", dict(out=d_bm[:, 0:nb], in_=v3, axis=AXX,
                                      op=ALU.max, apply_absolute_value=True),
                r=[d_wt], w=[d_bm])
            k.v("tensor_scalar", dict(out=d_bm[:, 0:nb], in0=d_bm[:, 0:nb],
                                      scalar1=1.0 / 7.0, scalar2=1e-12,
                                      op0=ALU.mult, op1=ALU.max),
                r=[d_bm], w=[d_bm])
            bm3 = d_bm[:, 0:nb].unsqueeze(2).broadcast_to([128, nb, 32])
            k.v("reciprocal", dict(out=d_bi[:, 0:nb], in_=d_bm[:, 0:nb]),
                r=[d_bm], w=[d_bi])
            bi3 = d_bi[:, 0:nb].unsqueeze(2).broadcast_to([128, nb, 32])
            k.v("tensor_tensor", dict(out=v3, in0=v3, in1=bi3, op=ALU.mult),
                r=[d_wt, d_bi], w=[d_wt])
            emit_rne(k, d_wt[:, 0:n_cols], d_wt, d_it[:, 0:n_cols], d_it)
            k.v("tensor_tensor", dict(out=v3, in0=v3, in1=bm3, op=ALU.mult),
                r=[d_wt, d_bm], w=[d_wt])
            for cc_ in range(n_cols // 128):
                bank, bk = PSB[cc_ % 2], PK[cc_ % 2]
                k.t("transpose", dict(out=bank[:, 0:128],
                                      in_=d_wt[:, 128 * cc_:128 * (cc_ + 1)],
                                      identity=sid[:]), r=[d_wt, sid], w=[bk])
                k.s("copy", dict(out=d_th[:], in_=bank[:, 0:128]),
                    r=[bk], w=[d_th])
                k.v("scalar_tensor_tensor", dict(
                    out=d_tl[:], in0=bank[:, 0:128], scalar=1.0, in1=d_th[:],
                    op0=ALU.mult, op1=ALU.subtract), r=[bk, d_th], w=[d_tl])
                p.dma("sync", out_h[128 * cc_:128 * (cc_ + 1),
                                    128 * ot:128 * (ot + 1)], d_th[:],
                      sem="d_dqh" + str(ot % 2))
                p.dma("sync", out_l[128 * cc_:128 * (cc_ + 1),
                                    128 * ot:128 * (ot + 1)], d_tl[:],
                      sem="d_dql" + str(ot % 2))

    dequant_one(wug[0], I3, D, wq_up_h, wq_up_l, "u")
    dequant_one(wug[1], I3, D, wq_g_h, wq_g_l, "g")
    dequant_one(wd, D, I3, wq_d_h, wq_d_l, "d")
    es_4c.close()
    p.barrier()

    if cfg.get("STOP_AFTER") == 5:
        k.phase_stacks.append(es_3)
        k.phase_stacks.append(es_4a)
        k.phase_stacks.append(es_4f)
        return k, locals()
    es_4d = contextlib.ExitStack()

    def cc_scales(cc_out_ap, cols, dst_col0, qmax, semtag):
        for j in range(cols):
            p.dma("sync", g_cc[0:1, 8 * j:8 * (j + 1)],
                  cc_out_ap[:, j][None, :], sem="d_gcc", loose=True)
        for j in range(cols):
            k.v("tensor_reduce", dict(out=s_sc[:, 8 + j:9 + j],
                                      in_=g_cc[0:1, 8 * j:8 * (j + 1)],
                                      axis=AXX,
                                      op=ALU.max), r=[g_cc], w=[s_sc])
            k.v("tensor_scalar", dict(out=s_sc[:, 8 + j:9 + j],
                                      in0=s_sc[:, 8 + j:9 + j],
                                      scalar1=1.0 / qmax, scalar2=1e-12,
                                      op0=ALU.mult, op1=ALU.max),
                r=[s_sc], w=[s_sc])
            k.v("reciprocal", dict(out=s_sc[:, 12 + j:13 + j],
                                   in_=s_sc[:, 8 + j:9 + j]),
                r=[s_sc], w=[s_sc])
            for (src_c, dst_c) in ((12 + j, dst_col0 + 2 * j),
                                   (8 + j, dst_col0 + 2 * j + 1)):
                k.t("matmul", dict(out=PSB[7][:, 0:1], lhsT=s_ones[:],
                                   rhs=s_sc[0:1, src_c:src_c + 1],
                                   start=True, stop=True),
                    r=[s_ones, s_sc], w=[PK[7]])
                k.s("copy", dict(out=s_scr[:, dst_c:dst_c + 1],
                                 in_=PSB[7][:, 0:1]), r=[PK[7]], w=["s_scr"])

    cc_scales(cc3_out, 1, 0, 32767.0, "3")
    for dc in range(NDC):
        for sc_ in range(TC // WTS0):
            sl3 = slice(TC * dc + WTS0 * sc_, TC * dc + WTS0 * (sc_ + 1))
            emit_qdq(k, f_xn1[:, sl3], f_xn1, u_it[:], u_it,
                     s_scr[:, 0:1], s_scr[:, 1:2])
            split16(k, f_xqh[:], f_xql[:], f_xn1[:, sl3],
                    r=[f_xn1], w=[f_xqh, f_xql])
            p.dma("sync", xq_sp_h[128 * dc:128 * (dc + 1),
                                  WTS0 * sc_:WTS0 * (sc_ + 1)], f_xqh[:],
                  sem="d_xqh")
            p.dma("sync", xq_sp_l[128 * dc:128 * (dc + 1),
                                  WTS0 * sc_:WTS0 * (sc_ + 1)], f_xql[:],
                  sem="d_xql")
    es_4f.close()
    UWH = [k.sb(f"u_wh{i}", [128, NDC * 128], BF16, es=es_4d) for i in range(2)]
    UWL = [k.sb(f"u_wl{i}", [128, NDC * 128], BF16, es=es_4d) for i in range(2)]
    k.v("memset", dict(ap=u_mx[:], constant=0.0), w=[u_mx])
    for w_i, (wh_sp, wl_sp, spill, mxc) in enumerate(
            ((wq_up_h, wq_up_l, up_spill, 0), (wq_g_h, wq_g_l, gate_spill, 1))):
        for ts in range(NTS):
            for ic in range(NDC):
                p.dma("gpsimd", xq_th[:, WTS * ic:WTS * (ic + 1)],
                      xq_sp_h[128 * ic:128 * (ic + 1),
                              WTS * ts:WTS * (ts + 1)], sem="d_xqth")
                p.dma("gpsimd", xq_tl[:, WTS * ic:WTS * (ic + 1)],
                      xq_sp_l[128 * ic:128 * (ic + 1),
                              WTS * ts:WTS * (ts + 1)], sem="d_xqtl")
            for oc in range(NI3):
                u_wh, u_wl = UWH[oc % 2], UWL[oc % 2]
                p.dma("gpsimd", u_wh[:].rearrange("p (c m) -> p c m", m=128),
                      wh_sp[:, 128 * oc:128 * (oc + 1)].rearrange(
                          "(c p) m -> p c m", p=128), sem="d_uw" + str(oc % 2))
                p.dma("gpsimd", u_wl[:].rearrange("p (c m) -> p c m", m=128),
                      wl_sp[:, 128 * oc:128 * (oc + 1)].rearrange(
                          "(c p) m -> p c m", p=128), sem="d_uwl" + str(oc % 2))
                bank, bk = PSB[2 + (oc % 2)], PK[2 + (oc % 2)]
                for ic in range(NDC):
                    mm3(k, bank[:, 0:WTS],
                        u_wh[:, 128 * ic:128 * (ic + 1)],
                        u_wl[:, 128 * ic:128 * (ic + 1)],
                        xq_th[:, WTS * ic:WTS * (ic + 1)],
                        xq_tl[:, WTS * ic:WTS * (ic + 1)],
                        start=(ic == 0), stop=(ic == NDC - 1),
                        r=[u_wh, u_wl, xq_th, xq_tl], w=[bk])
                u_t = UTB[oc % 2]
                k.s("copy", dict(out=u_t[:], in_=bank[:, 0:WTS]),
                    r=[bk], w=[u_t])
                k.v("tensor_reduce", dict(out=u_mx[:, 2:3], in_=u_t[:],
                                          axis=AXX, op=ALU.max,
                                          apply_absolute_value=True),
                    r=[u_t], w=[u_mx])
                k.v("tensor_tensor", dict(out=u_mx[:, mxc:mxc + 1],
                                          in0=u_mx[:, mxc:mxc + 1],
                                          in1=u_mx[:, 2:3], op=ALU.max),
                    r=[u_mx], w=[u_mx])
                p.dma("sync", spill[128 * oc:128 * (oc + 1),
                                    WTS * ts:WTS * (ts + 1)], u_t[:],
                      sem="d_sp" + str(oc % 2))
    for j in range(2):
        k.t("transpose", dict(out=PSB[6][0:1, 0:128], in_=u_mx[:, j:j + 1],
                              identity=sid[:]), r=[u_mx, sid], w=[PK[6]])
        k.v("tensor_reduce", dict(out=s_sc[:, j:j + 1],
                                  in_=PSB[6][0:1, 0:128], axis=AXX,
                                  op=ALU.max), r=[PK[6]], w=[s_sc])
    k.v("memset", dict(ap=s_sc[:, 2:8], constant=0.0), w=[s_sc])
    p.dma("sync", cc5_in[0, :], s_sc[0:1, 0:8], sem="d_ccx")
    k.emit_cc("AllGather", ALU.bypass, ALL8, cc5_in, cc5_out)
    cc_scales(cc5_out, 2, 2, 32767.0, "5")

    if cfg.get("STOP_AFTER") == 6:
        k.phase_stacks.append(es_3)
        k.phase_stacks.append(es_4a)
        k.phase_stacks.append(es_4d)
        return k, locals()
    C2_16 = float(2.0 ** -16)
    k.v("memset", dict(ap=u_mx[:, 3:4], constant=0.0), w=[u_mx])
    for oc in range(NI3):
        for ts in range(NTS):
            cpar = (oc * NTS + ts) % 2
            u_t, u_sg, u_it = UTB[cpar], USG[cpar], UIT[cpar]
            p.dma("sync", u_t[:], gate_spill[128 * oc:128 * (oc + 1),
                                             WTS * ts:WTS * (ts + 1)],
                  sem="d_ga" + str(cpar))
            emit_qdq(k, u_t[:], u_t, u_it[:], u_it, s_scr[:, 4:5],
                     s_scr[:, 5:6])
            k.s("activation", dict(out=u_sg[:], in_=u_t[:], func=AF.Sigmoid),
                r=[u_t], w=[u_sg])
            k.v("tensor_scalar", dict(out=u_sg[:], in0=u_sg[:],
                                      scalar1=65536.0, scalar2=C_RNE,
                                      op0=ALU.mult, op1=ALU.add),
                r=[u_sg], w=[u_sg])
            k.v("tensor_scalar", dict(out=u_sg[:], in0=u_sg[:],
                                      scalar1=C_RNE, scalar2=65535.0,
                                      op0=ALU.subtract, op1=ALU.min),
                r=[u_sg], w=[u_sg])
            k.v("tensor_copy", dict(out=u_it[:], in_=u_sg[:]),
                r=[u_sg], w=[u_it])
            k.v("tensor_copy", dict(out=u_sg[:], in_=u_it[:]),
                r=[u_it], w=[u_sg])
            k.v("scalar_tensor_tensor", dict(
                out=u_t[:], in0=u_sg[:], scalar=C2_16, in1=u_t[:],
                op0=ALU.mult, op1=ALU.mult), r=[u_sg, u_t], w=[u_t])
            k.v("tensor_reduce", dict(out=u_mx[:, 2:3], in_=u_t[:], axis=AXX,
                                      op=ALU.max, apply_absolute_value=True),
                r=[u_t], w=[u_mx])
            k.v("tensor_tensor", dict(out=u_mx[:, 3:4], in0=u_mx[:, 3:4],
                                      in1=u_mx[:, 2:3], op=ALU.max),
                r=[u_mx], w=[u_mx])
            p.dma("sync", g2_spill[128 * oc:128 * (oc + 1),
                                   WTS * ts:WTS * (ts + 1)], u_t[:],
                  sem="d_g2")
    k.t("transpose", dict(out=PSB[6][0:1, 0:128], in_=u_mx[:, 3:4],
                          identity=sid[:]), r=[u_mx, sid], w=[PK[6]])
    k.v("tensor_reduce", dict(out=s_sc[:, 0:1], in_=PSB[6][0:1, 0:128],
                              axis=AXX, op=ALU.max), r=[PK[6]], w=[s_sc])
    k.v("memset", dict(ap=s_sc[:, 1:8], constant=0.0), w=[s_sc])
    p.dma("sync", cc6_in[0, :], s_sc[0:1, 0:8], sem="d_ccx")
    k.emit_cc("AllGather", ALU.bypass, ALL8, cc6_in, cc6_out)
    cc_scales(cc6_out, 1, 6, 32767.0, "6")

    k.v("memset", dict(ap=u_mx[:, 3:4], constant=0.0), w=[u_mx])
    for oc in range(NI3):
        for ts in range(NTS):
            cpar = (oc * NTS + ts) % 2
            u_t, u_sg, u_it = UTB[cpar], USG[cpar], UIT[cpar]
            p.dma("sync", u_t[:], g2_spill[128 * oc:128 * (oc + 1),
                                           WTS * ts:WTS * (ts + 1)],
                  sem="d_g2l" + str(cpar))
            emit_qdq(k, u_t[:], u_t, u_it[:], u_it, s_scr[:, 6:7],
                     s_scr[:, 7:8])
            p.dma("sync", u_sg[:], up_spill[128 * oc:128 * (oc + 1),
                                            WTS * ts:WTS * (ts + 1)],
                  sem="d_upl" + str(cpar))
            emit_qdq(k, u_sg[:], u_sg, u_it[:], u_it, s_scr[:, 2:3],
                     s_scr[:, 3:4])
            k.v("tensor_tensor", dict(out=u_t[:], in0=u_t[:], in1=u_sg[:],
                                      op=ALU.mult), r=[u_t, u_sg], w=[u_t])
            k.v("tensor_reduce", dict(out=u_mx[:, 2:3], in_=u_t[:], axis=AXX,
                                      op=ALU.max, apply_absolute_value=True),
                r=[u_t], w=[u_mx])
            k.v("tensor_tensor", dict(out=u_mx[:, 3:4], in0=u_mx[:, 3:4],
                                      in1=u_mx[:, 2:3], op=ALU.max),
                r=[u_mx], w=[u_mx])
            p.dma("sync", op_spill[128 * oc:128 * (oc + 1),
                                   WTS * ts:WTS * (ts + 1)], u_t[:],
                  sem="d_op")
    k.t("transpose", dict(out=PSB[6][0:1, 0:128], in_=u_mx[:, 3:4],
                          identity=sid[:]), r=[u_mx, sid], w=[PK[6]])
    k.v("tensor_reduce", dict(out=s_sc[:, 0:1], in_=PSB[6][0:1, 0:128],
                              axis=AXX, op=ALU.max), r=[PK[6]], w=[s_sc])
    k.v("memset", dict(ap=s_sc[:, 1:8], constant=0.0), w=[s_sc])
    p.dma("sync", cc7_in[0, :], s_sc[0:1, 0:8], sem="d_ccx")
    k.emit_cc("AllGather", ALU.bypass, ALL8, cc7_in, cc7_out)
    cc_scales(cc7_out, 1, 8, 32767.0, "7")

    o_qh = k.sb("o_qh", [128, NI3 * WTS], BF16, es=es_4d)
    o_ql = k.sb("o_ql", [128, NI3 * WTS], BF16, es=es_4d)
    DWH = [k.sb(f"dw_h{i}", [128, NI3 * 128], BF16, es=es_4d) for i in range(2)]
    DWL = [k.sb(f"dw_l{i}", [128, NI3 * 128], BF16, es=es_4d) for i in range(2)]
    for ts in range(NTS):
        for ic3 in range(NI3):
            cpar = ic3 % 2
            u_t, u_it = UTB[cpar], UIT[cpar]
            p.dma("sync", u_t[:], op_spill[128 * ic3:128 * (ic3 + 1),
                                           WTS * ts:WTS * (ts + 1)],
                  sem="d_oq" + str(cpar))
            emit_qdq(k, u_t[:], u_t, u_it[:], u_it, s_scr[:, 8:9],
                     s_scr[:, 9:10])
            split16(k, o_qh[:, WTS * ic3:WTS * (ic3 + 1)],
                    o_ql[:, WTS * ic3:WTS * (ic3 + 1)], u_t[:],
                    r=[u_t], w=[o_qh, o_ql])
        for dcol in range(NDC):
            dpar = dcol % 2
            dw_h, dw_l = DWH[dpar], DWL[dpar]
            p.dma("gpsimd", dw_h[:].rearrange("p (c m) -> p c m", m=128),
                  wq_d_h[:, 128 * dcol:128 * (dcol + 1)].rearrange(
                      "(c p) m -> p c m", p=128), sem="d_dwh" + str(dpar))
            p.dma("gpsimd", dw_l[:].rearrange("p (c m) -> p c m", m=128),
                  wq_d_l[:, 128 * dcol:128 * (dcol + 1)].rearrange(
                      "(c p) m -> p c m", p=128), sem="d_dwl" + str(dpar))
            bank, bk = PSB[2 + (dcol % 2)], PK[2 + (dcol % 2)]
            for ic3 in range(NI3):
                mm3(k, bank[:, 0:WTS],
                    dw_h[:, 128 * ic3:128 * (ic3 + 1)],
                    dw_l[:, 128 * ic3:128 * (ic3 + 1)],
                    o_qh[:, WTS * ic3:WTS * (ic3 + 1)],
                    o_ql[:, WTS * ic3:WTS * (ic3 + 1)],
                    start=(ic3 == 0), stop=(ic3 == NI3 - 1),
                    r=[dw_h, dw_l, o_qh, o_ql], w=[bk])
            u_t2 = USG[dcol % 2]
            k.s("copy", dict(out=u_t2[:], in_=bank[:, 0:WTS]), r=[bk], w=[u_t2])
            for sub in range(WTS // 128):
                bank2, bk2 = PSB[6], PK[6]
                k.t("transpose", dict(out=bank2[:, 0:128],
                                      in_=u_t2[:, 128 * sub:128 * (sub + 1)],
                                      identity=sid[:]), r=[u_t2, sid], w=[bk2])
                tt2 = (WTS * ts) // 128 + sub
                k.v("tensor_tensor", dict(
                    out=q_x1[:, D * tt2 + 128 * dcol:D * tt2 + 128 * (dcol + 1)],
                    in0=q_x1[:, D * tt2 + 128 * dcol:D * tt2 + 128 * (dcol + 1)],
                    in1=bank2[:, 0:128], op=ALU.add), r=[q_x1, bk2], w=[q_x1])
    es_4d.close()

    out = k.dram("out", [TC, D], kind="ExternalOutput")
    for tt in range(NTC):
        p.dma("gpsimd", out[128 * tt:128 * (tt + 1), :],
              q_x1[:, D * tt:D * (tt + 1)], sem="d_out")
    k.phase_stacks.append(es_3)
    k.phase_stacks.append(es_4a)
    return k, locals()


def finish(k, dummy_out=False):
    p, nc = k.p, k.nc
    if dummy_out:
        o = k.dram("out", [k.TC, k.D], kind="ExternalOutput")
        es_z = contextlib.ExitStack()
        z = k.sb("z_dummy", [128, k.D], es=es_z)
        k.v("memset", dict(ap=z[:], constant=0.0), w=[z])
        for tt in range(k.TC // 128):
            p.dma("gpsimd", o[128 * tt:128 * (tt + 1), :], z[:], sem="d_out")
        es_z.close()
    for st_ in reversed(k.phase_stacks):
        st_.close()
    p.final_wait("gpsimd")
    p.raw("gpsimd", lambda g: g.nop())
    ctx = p.make_sems(k.es)
    with nc.Block() as block:
        p.emit(block, ctx)
    k.es.close()
    return nc


def host_inputs_for_core(ins, core, cfg):
    T, D = cfg["T"], cfg["D"]
    DI = 2 * D
    DIH = DI // 2
    b, ph = core // 2, core % 2
    f32 = np.float32
    cont = np.ascontiguousarray

    def t_(a):
        return cont(np.asarray(a, f32).T)

    xs_rows = slice(ph * DIH, (ph + 1) * DIH)
    z_rows = slice(DI + ph * DIH, DI + (ph + 1) * DIH)
    ipT = np.concatenate([np.asarray(ins["L0_attn_in_proj_w"], f32)[xs_rows].T,
                          np.asarray(ins["L0_attn_in_proj_w"], f32)[z_rows].T],
                         axis=1)
    d = {
        "x0": cont(np.asarray(ins["L0_attn_x"], f32)[b]),
        "w_ipT": cont(ipT),
        "conv_w": cont(np.asarray(ins["L0_attn_conv_w"], f32)[xs_rows, 0, :]),
        "conv_b": cont(np.asarray(ins["L0_attn_conv_b"], f32)[xs_rows]),
        "w_xpT": cont(np.asarray(ins["L0_attn_x_proj_w"], f32)[:, xs_rows].T),
        "w_dtT": cont(np.asarray(ins["L0_attn_dt_proj_w"], f32)[xs_rows].T),
        "dt_b": cont(np.asarray(ins["L0_attn_dt_proj_b"], f32)[xs_rows]),
        "A_log": cont(np.asarray(ins["L0_attn_A_log"], f32)[xs_rows]),
        "D_par": cont(np.asarray(ins["L0_attn_D_param"], f32)[xs_rows]),
        "w_opT": cont(np.asarray(ins["L0_attn_out_proj_w"], f32)
                      [:, xs_rows].T),
        "norm_w": cont(np.asarray(ins["L0_attn_norm_w"], f32)),
    }
    wo_loc = np.asarray(ins["L0_ffn_Wo"], f32)[512 * ph:512 * (ph + 1)]
    wo_r = np.zeros((4, 128, D), f32)
    for g in range(4):
        for r in range(128):
            wo_r[g, r] = wo_loc[(2 * g + r // 64) * 64 + (r % 64)]
    d.update({
        "ln1_w": cont(np.asarray(ins["L0_ffn_ln1_w"], f32)),
        "wq": cont(np.asarray(ins["L0_ffn_Wq"], f32)[:, 512 * ph:512 * (ph + 1)]),
        "wk": cont(np.asarray(ins["L0_ffn_Wk"], f32)[:, 128 * ph:128 * (ph + 1)]),
        "wv": cont(np.asarray(ins["L0_ffn_Wv"], f32)[:, 128 * ph:128 * (ph + 1)]),
        "qn_w": cont(np.asarray(ins["L0_ffn_qn_w"], f32)),
        "kn_w": cont(np.asarray(ins["L0_ffn_kn_w"], f32)),
        "wg": cont(np.asarray(ins["L0_ffn_Wg"], f32)),
        "wo_r": cont(wo_r),
    })
    TC = T // 2
    d.update({
        "x0o": cont(np.asarray(ins["L0_attn_x"], f32)[b][ph * TC:(ph + 1) * TC]),
        "ln2_w": cont(np.asarray(ins["L0_ffn_ln2_w"], f32)),
        "wr": cont(np.asarray(ins["L0_ffn_Wr"], f32)),
        "w1": np.ascontiguousarray(np.asarray(ins["L0_ffn_W1"], f32)),
        "w2": np.ascontiguousarray(np.asarray(ins["L0_ffn_W2"], f32)),
        "a_param": cont(np.asarray(ins["L1_attn_a_param"], f32)),
        "w_in": cont(np.asarray(ins["L1_attn_w_in"], f32)),
        "w_a": cont(np.asarray(ins["L1_attn_w_a"], f32)),
        "is_odd": np.full((1, 1), float(ph), f32),
        "wug": cont(np.stack([np.asarray(ins["L1_ffn_w_up"], f32),
                              np.asarray(ins["L1_ffn_w_gate"], f32)])),
        "wd": cont(np.asarray(ins["L1_ffn_w_down"], f32)),
    })
    return d


def kernel(**inputs):
    cfg = dict(CFG)
    k, _env = build(cfg)
    nc = finish(k)
    in_maps = [host_inputs_for_core(inputs, c, cfg) for c in range(8)]
    res = run_bass_kernel_spmd(nc, in_maps, core_ids=list(range(8)))
    T, D = cfg["T"], cfg["D"]
    TC = T // 2
    out = np.empty((4, T, D), dtype=np.float32)
    for c in range(8):
        b, ph = c // 2, c % 2
        out[b, ph * TC:(ph + 1) * TC] = np.asarray(res.results[c]["out"])
    return out


# version 09 (final)
# speedup vs baseline: 1.2619x; 1.0258x over previous
import contextlib
import numpy as np
import concourse.bass as bass
import concourse.mybir as mybir
from concourse.bass_utils import run_bass_kernel_spmd
from concourse.bass_core_rust import IndirectOffsetOnAxis

F32 = mybir.dt.float32
BF16 = mybir.dt.bfloat16
I32 = mybir.dt.int32
U32 = mybir.dt.uint32
AF = mybir.ActivationFunctionType
ALU = mybir.AluOpType
AXX = mybir.AxisListType.X

CFG = {
    "T": 4096,
    "D": 1024,
    "MOE_CAP": 384,
    "DEBUG_OUT": [],
}
C_RNE = float(np.float32(1.5 * 2 ** 23))

ENGINES = ("tensor", "vector", "scalar", "gpsimd", "sync")
SELF_SERIAL = {"vector", "scalar", "gpsimd"}


def _key(x):
    if isinstance(x, str):
        return x
    t = getattr(x, "tensor", None)
    if t is not None:
        return t.name
    return x.name


class Prog:

    def __init__(self, nc):
        self.nc = nc
        self.items = {e: [] for e in ENGINES}
        self.count = {e: 0 for e in ENGINES}
        self.dma_count = {}
        self.watermark = {}
        self.last_write = {}
        self.readers = {}
        self.bar = []

    def _emit_wait(self, eng_name, tok):
        if tok is None:
            return
        sem_name, val = tok
        if val <= 0:
            return
        if sem_name == "c_" + eng_name:
            return
        key = (eng_name, sem_name)
        if self.watermark.get(key, -1) >= val:
            return
        self.watermark[key] = val
        self.items[eng_name].append(("wait", sem_name, val))

    def _deps_for(self, reads, writes, deps):
        toks = list(deps) + list(self.bar)
        for r in reads:
            toks.append(self.last_write.get(_key(r)))
        for w in writes:
            kw = _key(w)
            toks.append(self.last_write.get(kw))
            toks.extend(self.readers.get(kw, ()))
        return toks

    def _record(self, tok, reads, writes):
        for r in reads:
            self.readers.setdefault(_key(r), []).append(tok)
        for w in writes:
            kw = _key(w)
            self.last_write[kw] = tok
            self.readers[kw] = []

    def op(self, eng_name, calls, r=(), w=(), deps=()):
        if isinstance(calls, tuple):
            calls = [calls]
        for t in self._deps_for(r, w, deps):
            self._emit_wait(eng_name, t)
        if eng_name in SELF_SERIAL:
            kk = self.count[eng_name]
            if kk > 0:
                self.items[eng_name].append(("selfwait", "c_" + eng_name, kk))
        self.count[eng_name] += 1
        tok = ("c_" + eng_name, self.count[eng_name])
        self.items[eng_name].append(("op", list(calls), "c_" + eng_name))
        self._record(tok, r, w)
        return tok

    def dma(self, queue, out, in_, sem=None, r=(), w=(), deps=(), loose=False,
            serial=False):
        sem = sem or ("d_" + queue)
        reads = list(r) + [in_]
        writes = list(w) + [out]
        deps = list(deps)
        if serial and sem in self.dma_count:
            deps.append((sem, self.dma_count[sem]))
        for t in self._deps_for(reads, writes, deps):
            self._emit_wait(queue, t)
        self.dma_count[sem] = self.dma_count.get(sem, 0) + 16
        tok = (sem, self.dma_count[sem])
        self.items[queue].append(("dma", out, in_, sem, loose))
        self._record(tok, reads, writes)
        return tok

    def idma(self, queue, out, out_off, in_, in_off, sem, r=(), w=(),
             deps=()):
        reads = list(r) + [in_]
        writes = list(w) + [out]
        if out_off is not None:
            reads.append(out_off[0])
        if in_off is not None:
            reads.append(in_off[0])
        for t in self._deps_for(reads, writes, deps):
            self._emit_wait(queue, t)
        self.dma_count[sem] = self.dma_count.get(sem, 0) + 16
        tok = (sem, self.dma_count[sem])
        oo = None if out_off is None else IndirectOffsetOnAxis(*out_off)
        io = None if in_off is None else IndirectOffsetOnAxis(*in_off)
        self.items[queue].append(("op16", ("indirect_dma_start", dict(
            out=out, out_offset=oo, in_=in_, in_offset=io)), sem))
        self._record(tok, reads, writes)
        return tok

    def cc(self, kind, op_, groups, cc_in, cc_out, deps=()):
        reads = [cc_in]
        writes = [cc_out]
        for t in self._deps_for(reads, writes, deps):
            self._emit_wait("gpsimd", t)
        if self.count["gpsimd"] > 0:
            self.items["gpsimd"].append(
                ("selfwait", "c_gpsimd", self.count["gpsimd"]))
        self.count["gpsimd"] += 1
        tok = ("c_gpsimd", self.count["gpsimd"])
        self.items["gpsimd"].append(("cc", kind, op_, groups, cc_in, cc_out,
                                     "c_gpsimd"))
        self._record(tok, reads, writes)
        return tok

    def barrier(self):
        toks = [("c_" + e, self.count[e]) for e in ENGINES if self.count[e] > 0]
        toks += [(s, v) for s, v in self.dma_count.items()]
        self.bar = toks
        self.last_write = {}
        self.readers = {}

    def raw(self, eng_name, fn):
        self.items[eng_name].append(("raw", fn))

    def final_wait(self, eng_name):
        for sem, v in self.dma_count.items():
            self._emit_wait(eng_name, (sem, v))
        for e in ENGINES:
            if e != eng_name and self.count[e] > 0:
                self._emit_wait(eng_name, ("c_" + e, self.count[e]))

    def make_sems(self, es):
        names = set("c_" + e for e in ENGINES if self.count[e] > 0)
        names |= set(self.dma_count.keys())
        for e in ENGINES:
            for it in self.items[e]:
                if it[0] in ("wait", "selfwait"):
                    names.add(it[1])
        return {n: es.enter_context(self.nc.semaphore(n)) for n in sorted(names)}

    def emit(self, block, ctx):
        def make_body(eng_name):
            def body(eng):
                for it in self.items[eng_name]:
                    kind = it[0]
                    if kind in ("wait", "selfwait"):
                        eng.wait_ge(ctx[it[1]], it[2])
                    elif kind == "op":
                        _, calls, sem_name = it
                        for mth, kw in calls[:-1]:
                            getattr(eng, mth)(**kw)
                        mth, kw = calls[-1]
                        getattr(eng, mth)(**kw).then_inc(ctx[sem_name], 1)
                    elif kind == "op16":
                        _, (mth, kw), sem_name = it
                        getattr(eng, mth)(**kw).then_inc(ctx[sem_name], 16)
                    elif kind == "dma":
                        _, out, in_, sem_name, loose = it
                        eng.dma_start(
                            out=out, in_=in_,
                            allow_slow_non_contiguous=loose,
                        ).then_inc(ctx[sem_name], 16)
                    elif kind == "cc":
                        _, k_, op_, groups, ci, co, sem_name = it
                        eng.collective_compute(
                            k_, op_, replica_groups=groups, ins=[ci], outs=[co],
                        ).then_inc(ctx[sem_name], 1)
                    elif kind == "raw":
                        it[1](eng)
            return body

        for eng_name in ENGINES:
            if not self.items[eng_name]:
                continue
            getattr(block, eng_name)(make_body(eng_name))


class K:
    def __init__(self, cfg):
        self.cfg = cfg
        self.T = cfg["T"]
        self.TC = self.T // 2
        self.D = cfg["D"]
        self.DI = 2 * self.D
        self.DIH = self.DI // 2
        self.NIT = self.DIH // 128
        self.DS = 16
        self.DTR = self.D // 16
        self.HD = 64
        self.E = 8
        self.HID = 2 * self.D
        self.I3 = 3 * self.D
        self.CAP = cfg["MOE_CAP"]
        self.nc = bass.Bass("TRN2", target_bir_lowering=False)
        self.p = Prog(self.nc)
        self.es = contextlib.ExitStack()
        self.phase_stacks = []
        self.dbg_outs = {}

    def sb(self, name, shape, dt=F32, es=None):
        return (es or self.es).enter_context(
            self.nc.sbuf_tensor(name, shape, dt))

    def dram(self, name, shape, dt=F32, kind="Internal"):
        return self.nc.dram_tensor(name, shape, dt, kind=kind).ap()

    def inline(self, name, arr):
        return self.nc.inline_tensor(np.ascontiguousarray(arr), name).ap()

    def v(self, meth, kw, r=(), w=(), deps=()):
        return self.p.op("vector", (meth, kw), r, w, deps)

    def s(self, meth, kw, r=(), w=(), deps=()):
        return self.p.op("scalar", (meth, kw), r, w, deps)

    def g(self, meth, kw, r=(), w=(), deps=()):
        return self.p.op("gpsimd", (meth, kw), r, w, deps)

    def t(self, meth, kw, r=(), w=(), deps=()):
        return self.p.op("tensor", (meth, kw), r, w, deps)

    def tm(self, calls, r=(), w=(), deps=()):
        return self.p.op("tensor", calls, r, w, deps)

    def emit_cc(self, kind, op_, groups, cc_in, cc_out):
        if self.cfg.get("LOCAL_CC"):
            n = cc_out.shape[0]
            for j in range(n):
                self.p.dma("gpsimd", cc_out[j], cc_in[0], sem="d_lcc")
            return
        self.p.cc(kind, op_, groups, cc_in, cc_out)

    def ones_colT_tile(self):
        if not hasattr(self, "_onec"):
            self._onec = self.sb("k_onec", [128, 1])
            self.v("memset", dict(ap=self._onec[:], constant=1.0),
                   w=["k_onec"])
        return self._onec[:]

    def const_col(self, val):
        if not hasattr(self, "_cc_tile"):
            self._cc_tile = self.sb("k_consts", [128, 32])
            self._cc_map = {}
        v = float(val)
        if v not in self._cc_map:
            i = len(self._cc_map)
            assert i < 32
            self._cc_map[v] = i
            ap = self._cc_tile[:, i:i + 1]
            self.v("memset", dict(ap=ap, constant=v), w=["k_consts"])
        i = self._cc_map[v]
        return self._cc_tile[:, i:i + 1]

    def debug(self, name, src_ap, shape):
        if name not in self.cfg["DEBUG_OUT"]:
            return
        o = self.dram("dbg_" + name, shape, kind="ExternalOutput")
        self.p.dma("gpsimd", o, src_ap, sem="d_dbg")
        self.dbg_outs[name] = shape

    def debug_dram(self, name, shape):
        if name in self.cfg["DEBUG_OUT"]:
            o = self.dram("dbg_" + name, shape, kind="ExternalOutput")
            self.dbg_outs[name] = shape
            return o
        return self.dram("scr_" + name, shape, kind="Internal")


def ones_colT(k):
    return k.ones_colT_tile()


def mm3(k, ps_ap, lhi, llo, rhi, rlo, start, stop, r=(), w=(), deps=()):
    calls = [
        ("matmul", dict(out=ps_ap, lhsT=lhi, rhs=rhi, start=start, stop=False)),
        ("matmul", dict(out=ps_ap, lhsT=lhi, rhs=rlo, start=False, stop=False)),
        ("matmul", dict(out=ps_ap, lhsT=llo, rhs=rhi, start=False, stop=stop)),
    ]
    return k.tm(calls, r=r, w=w, deps=deps)


def split16(k, hi_ap, lo_ap, src_ap, r=(), w=(), deps=()):
    k.s("copy", dict(out=hi_ap, in_=src_ap),
        r=list(r) + [src_ap], w=[hi_ap], deps=deps)
    return k.v("tensor_sub", dict(out=lo_ap, in0=src_ap, in1=hi_ap),
               r=list(r) + [src_ap, hi_ap], w=list(w) + [lo_ap])


def softplus_inplace(k, x_ap, tmp_ap, xkey, tkey):
    k.s("activation", dict(out=tmp_ap, in_=x_ap, func=AF.Abs),
        r=[xkey], w=[tkey])
    k.s("activation", dict(out=tmp_ap, in_=tmp_ap, func=AF.Exp, scale=-1.0),
        r=[tkey], w=[tkey])
    k.s("activation", dict(out=tmp_ap, in_=tmp_ap, func=AF.Ln, bias=1.0),
        r=[tkey], w=[tkey])
    k.v("scalar_tensor_tensor", dict(
        out=x_ap, in0=x_ap, scalar=0.0, in1=tmp_ap,
        op0=ALU.max, op1=ALU.add), r=[xkey, tkey], w=[xkey])


def emit_rne(k, ap, key, itmp, ikey):
    k.v("tensor_scalar", dict(out=ap, in0=ap, scalar1=C_RNE, scalar2=C_RNE,
                              op0=ALU.add, op1=ALU.subtract), r=[key], w=[key])
    k.v("tensor_copy", dict(out=itmp, in_=ap), r=[key], w=[ikey])
    k.v("tensor_copy", dict(out=ap, in_=itmp), r=[ikey], w=[key])


def emit_qdq(k, ap, key, itmp, ikey, sinv_rep, s_rep):
    k.v("tensor_scalar", dict(out=ap, in0=ap, scalar1=sinv_rep, scalar2=None,
                              op0=ALU.mult), r=[key, "s_scr"], w=[key])
    emit_rne(k, ap, key, itmp, ikey)
    k.v("tensor_scalar", dict(out=ap, in0=ap, scalar1=s_rep, scalar2=None,
                              op0=ALU.mult), r=[key, "s_scr"], w=[key])


EPS_LN = 1e-5
EPS_RMS = float(np.finfo(np.float32).eps)
EPS_A = 1e-6


def emit_ln(k, a_x, a_t, a_sq, a_st, eps2, w_rep):
    D = k.D
    k.v("reduce_sum", dict(out=a_st[:, 0:1], in_=a_x[:], axis=AXX),
        r=[a_x], w=["a_st"])
    k.v("tensor_scalar_mul", dict(out=a_st[:, 1:2], in0=a_st[:, 0:1],
                                  scalar1=1.0 / D), r=["a_st"], w=["a_st"])
    k.v("tensor_scalar", dict(out=a_t[:], in0=a_x[:], scalar1=a_st[:, 1:2],
                              scalar2=None, op0=ALU.subtract),
        r=[a_x, "a_st"], w=[a_t])
    k.s("activation", dict(out=a_sq[:], in_=a_t[:], func=AF.Square,
                           accum_out=a_st[:, 2:3]), r=[a_t], w=[a_sq, "a_st"])
    c_invd, c_eps = k.const_col(1.0 / D), k.const_col(EPS_LN)
    k.s("activation", dict(out=a_st[:, 3:4], in_=a_st[:, 2:3], func=AF.Sqrt,
                           scale=c_invd, bias=c_eps),
        r=["a_st", "k_consts"], w=["a_st"])
    k.v("reciprocal", dict(out=a_st[:, 4:5], in_=a_st[:, 3:4]),
        r=["a_st"], w=["a_st"])
    k.v("tensor_scalar", dict(out=a_x[:], in0=a_t[:], scalar1=a_st[:, 4:5],
                              scalar2=None, op0=ALU.mult),
        r=[a_t, "a_st"], w=[a_x])
    if eps2 is None:
        return
    k.s("activation", dict(out=a_sq[:], in_=a_x[:], func=AF.Square,
                           accum_out=a_st[:, 5:6]), r=[a_x], w=[a_sq, "a_st"])
    c_eps2 = k.const_col(eps2)
    k.s("activation", dict(out=a_st[:, 6:7], in_=a_st[:, 5:6], func=AF.Sqrt,
                           scale=c_invd, bias=c_eps2),
        r=["a_st", "k_consts"], w=["a_st"])
    k.v("reciprocal", dict(out=a_st[:, 7:8], in_=a_st[:, 6:7]),
        r=["a_st"], w=["a_st"])
    if w_rep is not None:
        k.v("scalar_tensor_tensor", dict(
            out=a_t[:], in0=a_x[:], scalar=a_st[:, 7:8], in1=w_rep[:],
            op0=ALU.mult, op1=ALU.mult), r=[a_x, "a_st", w_rep], w=[a_t])
    else:
        k.v("tensor_scalar", dict(out=a_t[:], in0=a_x[:], scalar1=a_st[:, 7:8],
                                  scalar2=None, op0=ALU.mult),
            r=[a_x, "a_st"], w=[a_t])


def build(cfg):
    k = K(cfg)
    nc, p = k.nc, k.p
    T, TC, D = k.T, k.TC, k.D
    DIH, NIT, DS, DTR = k.DIH, k.NIT, k.DS, k.DTR
    NDC = D // 128
    NTT = T // 128
    SS = 128 if T >= 2048 else max(64, T // 4)
    NST = T // SS
    SF = DS * SS
    NS512 = T // 512 if T >= 512 else 1
    W512 = min(512, T)

    x0 = k.dram("x0", [T, D], kind="ExternalInput")
    w_ipT = k.dram("w_ipT", [D, 2 * DIH], kind="ExternalInput")
    conv_w = k.dram("conv_w", [DIH, 4], kind="ExternalInput")
    conv_b = k.dram("conv_b", [DIH], kind="ExternalInput")
    w_xpT = k.dram("w_xpT", [DIH, 96], kind="ExternalInput")
    w_dtT = k.dram("w_dtT", [DTR, DIH], kind="ExternalInput")
    dt_b = k.dram("dt_b", [DIH], kind="ExternalInput")
    A_log = k.dram("A_log", [DIH, DS], kind="ExternalInput")
    D_par = k.dram("D_par", [DIH], kind="ExternalInput")
    w_opT = k.dram("w_opT", [DIH, D], kind="ExternalInput")
    norm_w = k.dram("norm_w", [D], kind="ExternalInput")

    ident = k.inline("ident", np.eye(128, dtype=np.float32))
    ones_row = k.inline("ones_row", np.ones((1, 128), np.float32))
    selbc = np.zeros((96, 32), np.float32)
    for n in range(32):
        selbc[64 + n, n] = 1.0
    sel_bc = k.inline("sel_bc", selbc)

    u_spill = k.dram("u_spill", [DIH, T])
    z_spill = k.dram("z_spill", [DIH, T])
    cc0_in = k.dram("cc0_in", [1, 96 * T])
    cc0_out = k.dram("cc0_out", [1, 96 * T])
    cc1_in = k.dram("cc1_in", [1, T * D])
    cc1_out = k.dram("cc1_out", [1, TC * D])
    PAIRS = [[0, 1], [2, 3], [4, 5], [6, 7]]
    ALL8 = [list(range(8))]

    k.const_col(0.0)
    k.ones_colT_tile()
    sid = k.sb("sid", [128, 128])
    s_ones = k.sb("s_ones", [1, 128])
    s_st8 = k.sb("a_st", [128, 8])

    PSB = [k.es.enter_context(nc.psum_tensor(f"psb{i}", [128, 512], F32))
           for i in range(8)]
    PK = [f"psb{i}" for i in range(8)]

    p.dma("sync", sid[:], ident, sem="d_ini", serial=True)
    p.dma("sync", s_ones[:], ones_row, sem="d_ini", serial=True)

    es_m = contextlib.ExitStack()
    m_xdbc = k.sb("m_xdbc", [128, T], es=es_m)
    s_selbc = k.sb("s_selbc", [96, 32], es=es_m)
    s_convw = k.sb("s_convw", [128, 4 * NIT], es=es_m)
    s_convb = k.sb("s_convb", [128, NIT], es=es_m)
    s_dtb = k.sb("s_dtb", [128, NIT], es=es_m)
    s_A = k.sb("s_A", [128, DS * NIT], es=es_m)
    s_D = k.sb("s_D", [128, NIT], es=es_m)
    s_dtw = k.sb("s_dtw", [DTR, DIH], es=es_m)
    s_xpw = k.sb("s_xpw", [128, NIT * 96], es=es_m)
    s_nw1 = k.sb("s_nw1", [1, D], es=es_m)
    p.dma("sync", s_selbc[:], sel_bc, sem="d_ini", serial=True)
    p.dma("sync", s_convw[:].rearrange("p (n c) -> p n c", c=4),
          conv_w.rearrange("(n p) c -> p n c", p=128), sem="d_ini", serial=True)
    p.dma("sync", s_convb[:], conv_b.rearrange("(n p) -> p n", p=128),
          sem="d_ini", serial=True, loose=True)
    p.dma("sync", s_dtb[:], dt_b.rearrange("(n p) -> p n", p=128),
          sem="d_ini", serial=True, loose=True)
    p.dma("sync", s_A[:].rearrange("p (n s) -> p n s", s=DS),
          A_log.rearrange("(n p) s -> p n s", p=128), sem="d_ini", serial=True)
    p.dma("sync", s_D[:], D_par.rearrange("(n p) -> p n", p=128),
          sem="d_ini", serial=True, loose=True)
    p.dma("sync", s_dtw[:], w_dtT, sem="d_ini", serial=True)
    p.dma("sync", s_xpw[:].rearrange("p (n o) -> p n o", o=96),
          w_xpT.rearrange("(n p) o -> p n o", p=128), sem="d_ini", serial=True)
    k.s("activation", dict(out=s_A[:], in_=s_A[:], func=AF.Exp),
        r=[s_A], w=[s_A])
    k.v("tensor_scalar_mul", dict(out=s_A[:], in0=s_A[:], scalar1=-1.0),
        r=[s_A], w=[s_A])
    p.dma("sync", s_nw1[:], norm_w[None, :], sem="d_ini", serial=True)

    es_1a = contextlib.ExitStack()
    m_xmTh = k.sb("m_xmTh", [128, NDC * T], BF16, es=es_1a)
    m_xmTl = k.sb("m_xmTl", [128, NDC * T], BF16, es=es_1a)

    es_ln = contextlib.ExitStack()
    s_nw = k.sb("s_nw", [128, D], es=es_ln)
    a_x = k.sb("a_x", [128, D], es=es_ln)
    a_t = k.sb("a_t", [128, D], es=es_ln)
    a_sq = k.sb("a_sq", [128, D], es=es_ln)
    for dc in range(NDC):
        k.t("matmul", dict(out=PSB[0][:, 0:128], lhsT=s_ones[:],
                           rhs=s_nw1[:, 128 * dc:128 * (dc + 1)],
                           start=True, stop=True),
            r=[s_ones, s_nw1], w=[PK[0]])
        k.s("copy", dict(out=s_nw[:, 128 * dc:128 * (dc + 1)],
                         in_=PSB[0][:, 0:128]), r=[PK[0]], w=[s_nw])
    for tt in range(NTT):
        p.dma("sync", a_x[:], x0[128 * tt:128 * (tt + 1), :], sem="d_x")
        emit_ln(k, a_x, a_t, a_sq, s_st8, EPS_RMS, s_nw)
        for dc in range(NDC):
            bank, bk = PSB[dc % 2], PK[dc % 2]
            k.t("transpose", dict(out=bank[:, 0:128],
                                  in_=a_t[:, 128 * dc:128 * (dc + 1)],
                                  identity=sid[:]),
                r=[a_t, sid], w=[bk])
            fo = dc * T + 128 * tt
            split16(k, m_xmTh[:, fo:fo + 128], m_xmTl[:, fo:fo + 128],
                    bank[:, 0:128], r=[bk], w=[m_xmTh, m_xmTl])
    es_ln.close()
    p.barrier()

    es_ip = contextlib.ExitStack()
    TMPW = [k.sb(f"s_tmpw{i}", [128, DIH], es=es_ip) for i in range(2)]
    WIH = [k.sb("m_wih0", [128, 2 * DIH], BF16, es=es_ip)] * 2
    WIL = [k.sb("m_wil0", [128, 2 * DIH], BF16, es=es_ip)] * 2
    m_xs = k.sb("m_xs", [128, T + 4], es=es_ip)
    m_u = k.sb("m_u", [128, T], es=es_ip)
    m_z = k.sb("m_z", [128, W512], es=es_ip)
    k.v("memset", dict(ap=m_xs[:, 0:4], constant=0.0), w=[m_xs])

    for it in range(NIT):
        s_tmpw, m_wih, m_wil = TMPW[it % 2], WIH[it % 2], WIL[it % 2]
        p.dma("sync", s_tmpw[:, 0:128 * NDC].rearrange("p (c m) -> p c m", m=128),
              w_ipT[:, 128 * it:128 * (it + 1)].rearrange(
                  "(c p) m -> p c m", p=128), sem="d_wip" + str(it % 2))
        split16(k, m_wih[:, 0:128 * NDC], m_wil[:, 0:128 * NDC],
                s_tmpw[:, 0:128 * NDC], r=[s_tmpw], w=[m_wih, m_wil])
        p.dma("sync", s_tmpw[:, 0:128 * NDC].rearrange("p (c m) -> p c m", m=128),
              w_ipT[:, DIH + 128 * it:DIH + 128 * (it + 1)].rearrange(
                  "(c p) m -> p c m", p=128), sem="d_wip" + str(it % 2))
        split16(k, m_wih[:, 128 * NDC:256 * NDC], m_wil[:, 128 * NDC:256 * NDC],
                s_tmpw[:, 0:128 * NDC], r=[s_tmpw], w=[m_wih, m_wil])
        for st in range(NS512):
            bank, bk = PSB[2 + (st % 2)], PK[2 + (st % 2)]
            for dc in range(NDC):
                mm3(k, bank[:, 0:W512],
                    m_wih[:, 128 * dc:128 * (dc + 1)],
                    m_wil[:, 128 * dc:128 * (dc + 1)],
                    m_xmTh[:, dc * T + W512 * st: dc * T + W512 * (st + 1)],
                    m_xmTl[:, dc * T + W512 * st: dc * T + W512 * (st + 1)],
                    start=(dc == 0), stop=(dc == NDC - 1),
                    r=[m_wih, m_wil, m_xmTh, m_xmTl], w=[bk])
            k.s("copy", dict(out=m_xs[:, 4 + W512 * st:4 + W512 * (st + 1)],
                             in_=bank[:, 0:W512]), r=[bk], w=[m_xs])
            bankz, bkz = PSB[4 + (st % 2)], PK[4 + (st % 2)]
            for dc in range(NDC):
                mm3(k, bankz[:, 0:W512],
                    m_wih[:, 128 * NDC + 128 * dc:128 * NDC + 128 * (dc + 1)],
                    m_wil[:, 128 * NDC + 128 * dc:128 * NDC + 128 * (dc + 1)],
                    m_xmTh[:, dc * T + W512 * st: dc * T + W512 * (st + 1)],
                    m_xmTl[:, dc * T + W512 * st: dc * T + W512 * (st + 1)],
                    start=(dc == 0), stop=(dc == NDC - 1),
                    r=[m_wih, m_wil, m_xmTh, m_xmTl], w=[bkz])
            k.s("activation", dict(out=m_z[:, 0:W512], in_=bankz[:, 0:W512],
                                   func=AF.Silu), r=[bkz], w=[m_z])
            p.dma("sync", z_spill[128 * it:128 * (it + 1),
                                  W512 * st:W512 * (st + 1)],
                  m_z[:, 0:W512], sem="d_zsp")
        cw = [s_convw[:, 4 * it + kk:4 * it + kk + 1] for kk in range(4)]
        k.v("tensor_scalar", dict(out=m_u[:], in0=m_xs[:, 1:1 + T],
                                  scalar1=cw[0], scalar2=None, op0=ALU.mult),
            r=[m_xs, s_convw], w=[m_u])
        for kk in range(1, 4):
            k.v("scalar_tensor_tensor", dict(
                out=m_u[:], in0=m_xs[:, 1 + kk:1 + kk + T], scalar=cw[kk],
                in1=m_u[:], op0=ALU.mult, op1=ALU.add),
                r=[m_xs, s_convw, m_u], w=[m_u])
        k.s("activation", dict(out=m_u[:], in_=m_u[:], func=AF.Silu,
                               bias=s_convb[:, it:it + 1]),
            r=[m_u, s_convb], w=[m_u])
        p.dma("sync", u_spill[128 * it:128 * (it + 1), :], m_u[:], sem="d_usp")
        for st in range(NS512):
            bank, bk = PSB[6 + (st % 2)], PK[6 + (st % 2)]
            k.t("matmul", dict(out=bank[0:96, 0:W512],
                               lhsT=s_xpw[:, 96 * it:96 * (it + 1)],
                               rhs=m_u[:, W512 * st:W512 * (st + 1)],
                               start=True, stop=True),
                r=[s_xpw, m_u], w=[bk])
            if it == 0:
                k.s("copy", dict(out=m_xdbc[0:96, W512 * st:W512 * (st + 1)],
                                 in_=bank[0:96, 0:W512]), r=[bk], w=[m_xdbc])
            else:
                k.v("tensor_add", dict(
                    out=m_xdbc[0:96, W512 * st:W512 * (st + 1)],
                    in0=m_xdbc[0:96, W512 * st:W512 * (st + 1)],
                    in1=bank[0:96, 0:W512]), r=[bk, m_xdbc], w=[m_xdbc])
    es_ip.close()
    es_1a.close()
    p.barrier()

    p.dma("sync", cc0_in[0, :].rearrange("(p f) -> p f", p=96),
          m_xdbc[0:96, :], sem="d_cc0")
    k.emit_cc("AllReduce", ALU.add, PAIRS, cc0_in, cc0_out)
    es_1b = contextlib.ExitStack()
    p.dma("sync", m_xdbc[0:96, :], cc0_out[0, :].rearrange("(p f) -> p f", p=96),
          sem="d_cc0c")
    k.debug("xdbc", m_xdbc[0:96, :], [96, T])

    BREP = [k.sb(f"s_brep{i}", [128, SF], es=es_1b) for i in range(2)]
    CREP = [k.sb(f"s_crep{i}", [128, SF], es=es_1b) for i in range(2)]
    DA = [k.sb(f"s_dA{i}", [128, SF], es=es_1b) for i in range(2)]
    DBU = [k.sb(f"s_dBu{i}", [128, SF], es=es_1b) for i in range(2)]
    HH = [k.sb(f"s_h{i}", [128, SF], es=es_1b) for i in range(2)]
    HC = [k.sb(f"s_hC{i}", [128, SF], es=es_1b) for i in range(2)]
    DLY = [k.sb(f"s_dly{i}", [128, SS], es=es_1b) for i in range(2)]
    DU = [k.sb(f"s_du{i}", [128, SS], es=es_1b) for i in range(2)]
    UT = [k.sb(f"s_ut{i}", [128, SS], es=es_1b) for i in range(2)]
    ZT = [k.sb(f"s_zt{i}", [128, SS], es=es_1b) for i in range(2)]
    SPT = [k.sb(f"s_spt{i}", [128, SS], es=es_1b) for i in range(2)]
    YG = [k.sb(f"s_yg{i}", [128, NIT * SS], es=es_1b) for i in range(2)]
    YGH = [k.sb(f"s_ygh{i}", [128, NIT * SS], BF16, es=es_1b) for i in range(2)]
    YGL = [k.sb(f"s_ygl{i}", [128, NIT * SS], BF16, es=es_1b) for i in range(2)]
    MO = [k.sb(f"s_mo{i}", [128, SS], es=es_1b) for i in range(2)]
    MO2 = [k.sb(f"s_mo2{i}", [128, 128], es=es_1b) for i in range(2)]
    s_hcar = k.sb("s_hcar", [128, DS * NIT], es=es_1b)
    s_opwh = k.sb("s_opwh", [128, NDC * NIT * 128], BF16, es=es_1b)
    s_opwl = k.sb("s_opwl", [128, NDC * NIT * 128], BF16, es=es_1b)
    s_tw2 = k.sb("s_tw2", [128, NIT * 128], es=es_1b)

    k.v("memset", dict(ap=s_hcar[:], constant=0.0), w=[s_hcar])
    for dc in range(NDC):
        p.dma("sync", s_tw2[:].rearrange("p (n m) -> p n m", m=128),
              w_opT[:, 128 * dc:128 * (dc + 1)].rearrange(
                  "(n p) m -> p n m", p=128), sem="d_wop")
        split16(k, s_opwh[:, 1024 * dc:1024 * (dc + 1)],
                s_opwl[:, 1024 * dc:1024 * (dc + 1)],
                s_tw2[:], r=[s_tw2], w=[s_opwh, s_opwl])

    NGI = NST * NIT

    def stage_A(gi):
        st, it = gi // NIT, gi % NIT
        ip = gi % 2
        sp = st % 2
        s_brep, s_crep = BREP[sp], CREP[sp]
        if it == 0:
            NPB = max(1, 512 // SS)
            for half, dst in ((0, s_brep), (1, s_crep)):
                for n0 in range(0, DS, NPB):
                    bank = PSB[(n0 // NPB) % 2]
                    bk = PK[(n0 // NPB) % 2]
                    nn = min(NPB, DS - n0)
                    calls = []
                    for j in range(nn):
                        n = n0 + j
                        calls.append(("matmul", dict(
                            out=bank[:, SS * j:SS * (j + 1)],
                            lhsT=s_selbc[:, 16 * half + n:16 * half + n + 1
                                         ].broadcast_to([96, 128]),
                            rhs=m_xdbc[0:96, SS * st:SS * (st + 1)],
                            start=True, stop=True)))
                    k.tm(calls, r=[s_selbc, m_xdbc], w=[bk])
                    k.s("copy", dict(out=dst[:, SS * n0:SS * (n0 + nn)],
                                     in_=bank[:, 0:SS * nn]), r=[bk], w=[dst])
        s_dly, s_du, s_ut, s_zt = DLY[ip], DU[ip], UT[ip], ZT[ip]
        s_dA, s_dBu, s_sptmp = DA[ip], DBU[ip], SPT[ip]
        dtb_, dtk = PSB[2 + ip], PK[2 + ip]
        k.t("matmul", dict(out=dtb_[:, 0:SS],
                           lhsT=s_dtw[:, 128 * it:128 * (it + 1)],
                           rhs=m_xdbc[0:DTR, SS * st:SS * (st + 1)],
                           start=True, stop=True),
            r=[s_dtw, m_xdbc], w=[dtk])
        k.v("tensor_scalar", dict(out=s_dly[:], in0=dtb_[:, 0:SS],
                                  scalar1=s_dtb[:, it:it + 1], scalar2=None,
                                  op0=ALU.add), r=[dtk, s_dtb], w=[s_dly])
        softplus_inplace(k, s_dly[:], s_sptmp[:], s_dly, s_sptmp)
        p.dma("sync", s_ut[:], u_spill[128 * it:128 * (it + 1),
                                       SS * st:SS * (st + 1)],
              sem="d_uld" + str(ip))
        p.dma("sync", s_zt[:], z_spill[128 * it:128 * (it + 1),
                                       SS * st:SS * (st + 1)],
              sem="d_zld" + str(ip))
        k.v("tensor_mul", dict(out=s_du[:], in0=s_dly[:], in1=s_ut[:]),
            r=[s_dly, s_ut], w=[s_du])
        dA3 = s_dA[:].rearrange("p (n f) -> p n f", n=DS)
        dly3 = s_dly[:].unsqueeze(1).broadcast_to([128, DS, SS])
        A3 = s_A[:, DS * it:DS * (it + 1)].unsqueeze(2).broadcast_to(
            [128, DS, SS])
        k.g("tensor_tensor", dict(out=dA3, in0=dly3, in1=A3, op=ALU.mult),
            r=[s_dly, s_A], w=[s_dA])
        k.s("activation", dict(out=s_dA[:], in_=s_dA[:], func=AF.Exp),
            r=[s_dA], w=[s_dA])
        du3 = s_du[:].unsqueeze(1).broadcast_to([128, DS, SS])
        dBu3 = s_dBu[:].rearrange("p (n f) -> p n f", n=DS)
        br3 = s_brep[:].rearrange("p (n f) -> p n f", n=DS)
        k.g("tensor_tensor", dict(out=dBu3, in0=du3, in1=br3, op=ALU.mult),
            r=[s_du, s_brep], w=[s_dBu])

    def stage_B(gi):
        st, it = gi // NIT, gi % NIT
        ip = gi % 2
        sp = st % 2
        s_crep = CREP[sp]
        s_yg = YG[sp]
        s_ut, s_zt = UT[ip], ZT[ip]
        s_dA, s_dBu, s_h, s_hC, s_sptmp = (DA[ip], DBU[ip], HH[ip],
                                           HC[ip], SPT[ip])
        car = s_hcar[:, DS * it:DS * (it + 1)]
        car3 = car.rearrange("p (n f) -> p n f", f=1)
        dA0 = s_dA[:].rearrange("p (n f) -> p n f", n=DS)[:, :, 0:1]
        dBu0 = s_dBu[:].rearrange("p (n f) -> p n f", n=DS)[:, :, 0:1]
        tmpc3 = s_sptmp[:, 0:DS].rearrange("p (n f) -> p n f", f=1)
        k.v("tensor_tensor", dict(out=tmpc3, in0=dA0, in1=car3, op=ALU.mult),
            r=[s_dA, s_hcar], w=[s_sptmp])
        k.v("tensor_tensor", dict(out=dBu0, in0=dBu0, in1=tmpc3, op=ALU.add),
            r=[s_sptmp, s_dBu], w=[s_dBu])
        k.v("memset", dict(ap=dA0, constant=0.0), r=[], w=[s_dA])
        k.v("tensor_tensor_scan", dict(
            out=s_h[:], data0=s_dA[:], data1=s_dBu[:], initial=0.0,
            op0=ALU.mult, op1=ALU.add), r=[s_dA, s_dBu], w=[s_h])
        hlast = s_h[:].rearrange("p (n f) -> p n f", n=DS)[:, :, SS - 1:SS]
        k.v("tensor_copy", dict(out=car3, in_=hlast), r=[s_h], w=[s_hcar])
        h3 = s_h[:].rearrange("p (n f) -> p n f", n=DS)
        cr3 = s_crep[:].rearrange("p (n f) -> p n f", n=DS)
        hc3t = s_hC[:].rearrange("p (f n) -> p n f", n=DS)
        k.v("tensor_tensor", dict(out=hc3t, in0=h3, in1=cr3, op=ALU.mult),
            r=[s_h, s_crep], w=[s_hC])
        yg = s_yg[:, SS * it:SS * (it + 1)]
        k.v("tensor_reduce", dict(
            out=yg, in_=s_hC[:].rearrange("p (f n) -> p f n", n=DS),
            axis=AXX, op=ALU.add), r=[s_hC], w=[s_yg])
        k.v("scalar_tensor_tensor", dict(
            out=yg, in0=s_ut[:], scalar=s_D[:, it:it + 1], in1=yg,
            op0=ALU.mult, op1=ALU.add), r=[s_ut, s_D, s_yg], w=[s_yg])
        k.v("tensor_mul", dict(out=yg, in0=yg, in1=s_zt[:]),
            r=[s_yg, s_zt], w=[s_yg])

    def stripe_tail(st):
        sp = st % 2
        s_yg, s_ygh, s_ygl = YG[sp], YGH[sp], YGL[sp]
        split16(k, s_ygh[:], s_ygl[:], s_yg[:], r=[s_yg], w=[s_ygh, s_ygl])
        for dc in range(NDC):
            bank, bkk = PSB[4 + (dc % 2)], PK[4 + (dc % 2)]
            for it in range(NIT):
                mm3(k, bank[:, 0:SS],
                    s_opwh[:, 1024 * dc + 128 * it:1024 * dc + 128 * (it + 1)],
                    s_opwl[:, 1024 * dc + 128 * it:1024 * dc + 128 * (it + 1)],
                    s_ygh[:, SS * it:SS * (it + 1)],
                    s_ygl[:, SS * it:SS * (it + 1)],
                    start=(it == 0), stop=(it == NIT - 1),
                    r=[s_opwh, s_opwl, s_ygh, s_ygl], w=[bkk])
            s_mo = MO[dc % 2]
            s_mo2 = MO2[dc % 2]
            k.s("copy", dict(out=s_mo[:], in_=bank[:, 0:SS]), r=[bkk], w=[s_mo])
            for sub in range(SS // 128):
                tb, tbk = PSB[6 + (dc % 2)], PK[6 + (dc % 2)]
                k.t("transpose", dict(out=tb[:, 0:128],
                                      in_=s_mo[:, 128 * sub:128 * (sub + 1)],
                                      identity=sid[:]), r=[s_mo, sid], w=[tbk])
                k.s("copy", dict(out=s_mo2[:, 0:128], in_=tb[:, 0:128]),
                    r=[tbk], w=[s_mo2])
                p.dma("sync",
                      cc1_in[0, :].rearrange("(t d) -> t d", d=D)[
                          SS * st + 128 * sub:SS * st + 128 * (sub + 1),
                          128 * dc:128 * (dc + 1)],
                      s_mo2[:, 0:128], sem="d_cc1" + str(dc % 2))

    stage_A(0)
    for gi in range(NGI):
        if gi + 1 < NGI:
            stage_A(gi + 1)
        stage_B(gi)
        if gi % NIT == NIT - 1:
            stripe_tail(gi // NIT)
    k.emit_cc("ReduceScatter", ALU.add, PAIRS, cc1_in, cc1_out)
    if "mamba_mm" in k.cfg["DEBUG_OUT"]:
        o = k.dram("dbg_mamba_mm", [TC, D], kind="ExternalOutput")
        k.dbg_outs["mamba_mm"] = [TC, D]
        s_m1 = k.sb("s_m1", [128, D], es=es_1b)
        for tt in range(TC // 128):
            p.dma("sync", s_m1[:], cc1_out[0, :].rearrange(
                "(t d) -> t d", d=D)[128 * tt:128 * (tt + 1), :], sem="d_dbg1")
            p.dma("gpsimd", o[128 * tt:128 * (tt + 1), :], s_m1[:], sem="d_dbg")
    es_1b.close()
    es_m.close()
    p.barrier()

    if cfg.get("STOP_AFTER") == 1:
        return k, locals()
    HB = 8
    ln1_w = k.dram("ln1_w", [D], kind="ExternalInput")
    wq = k.dram("wq", [D, HB * 64], kind="ExternalInput")
    wk = k.dram("wk", [D, 128], kind="ExternalInput")
    wv = k.dram("wv", [D, 128], kind="ExternalInput")
    qn_w = k.dram("qn_w", [64], kind="ExternalInput")
    kn_w = k.dram("kn_w", [64], kind="ExternalInput")
    wg = k.dram("wg", [64, 64], kind="ExternalInput")
    wo_r = k.dram("wo_r", [4, 128, D], kind="ExternalInput")
    cc2_in = k.dram("cc2_in", [1, T * D])
    cc2_out = k.dram("cc2_out", [1, TC * D])
    _inv = 1.0 / 10000.0 ** (np.arange(0, 64, 2) / 64.0)
    _fr = np.arange(T)[:, None] * _inv[None, :]
    rope_cs = k.inline("rope_cs", np.concatenate(
        [np.cos(_fr), np.sin(_fr)], axis=1).astype(np.float32))
    triuT = k.inline("triuT", np.triu(np.ones((128, 128), np.float32)))

    NQT = T // 128
    es_2 = contextlib.ExitStack()
    a_gt = k.sb("a_gt", [128, 4 * T], es=es_2)
    s_triu = k.sb("s_triu", [128, 128], es=es_2)
    qt_sp_h = k.dram("qt_sp_h", [8, 64 * T], BF16)
    qt_sp_l = k.dram("qt_sp_l", [8, 64 * T], BF16)
    es_2q = contextlib.ExitStack()
    a_qsth = k.sb("a_qsth", [64, 128], BF16, es=es_2q)
    a_qstl = k.sb("a_qstl", [64, 128], BF16, es=es_2q)
    a_kth = k.sb("a_kth", [128, 2 * T], BF16, es=es_2q)
    a_ktl = k.sb("a_ktl", [128, 2 * T], BF16, es=es_2q)
    a_v = k.sb("a_v", [128, NQT * 130], es=es_2q)
    p.dma("sync", s_triu[:], triuT, sem="d_ini", serial=True)

    es_2a = contextlib.ExitStack()
    h_hnh = k.sb("h_hnh", [128, NDC * (T // 4)], BF16, es=es_2a)
    h_hnl = k.sb("h_hnl", [128, NDC * (T // 4)], BF16, es=es_2a)
    a_wqh = k.sb("a_wqh", [128, NDC * HB * 64], BF16, es=es_2a)
    a_wql = k.sb("a_wql", [128, NDC * HB * 64], BF16, es=es_2a)
    a_wkvh = k.sb("a_wkvh", [128, NDC * 256], BF16, es=es_2a)
    a_wkvl = k.sb("a_wkvl", [128, NDC * 256], BF16, es=es_2a)
    a_wtmp = k.sb("a_wtmp", [128, HB * 64], es=es_2a)
    a_x = k.sb("a_x2", [128, D], es=es_2a)
    a_t = k.sb("a_t2", [128, D], es=es_2a)
    a_sq = k.sb("a_sq2", [128, D], es=es_2a)
    a_qtile = k.sb("a_qtile", [128, HB * 64], es=es_2a)
    a_kvt = k.sb("a_kvt", [128, 256], es=es_2a)
    a_cs = k.sb("a_cs", [128, 64], es=es_2a)
    a_rr = k.sb("a_rr", [128, HB * 64], es=es_2a)
    a_ms = k.sb("a_ms", [128, 16], es=es_2a)
    s_ln1r = k.sb("s_ln1r", [128, D], es=es_2a)
    s_qnr = k.sb("s_qnr", [128, 64], es=es_2a)
    s_knr = k.sb("s_knr", [128, 64], es=es_2a)
    s_r1 = k.sb("s_r1", [1, D], es=es_2a)

    for (src, dst, ww) in ((ln1_w, s_ln1r, D), (qn_w, s_qnr, 64),
                           (kn_w, s_knr, 64)):
        p.dma("sync", s_r1[:, 0:ww], src[None, :], sem="d_ini", serial=True)
        for dc in range((ww + 127) // 128):
            wN = min(128, ww - 128 * dc)
            k.t("matmul", dict(out=PSB[0][:, 0:wN], lhsT=s_ones[:],
                               rhs=s_r1[:, 128 * dc:128 * dc + wN],
                               start=True, stop=True),
                r=[s_ones, s_r1], w=[PK[0]])
            k.s("copy", dict(out=dst[:, 128 * dc:128 * dc + wN],
                             in_=PSB[0][:, 0:wN]), r=[PK[0]], w=[dst])

    for (wsrc, hdst, ldst, ncols) in ((wq, a_wqh, a_wql, HB * 64),):
        for dc in range(NDC):
            p.dma("sync", a_wtmp[:, 0:ncols],
                  wsrc[128 * dc:128 * (dc + 1), :], sem="d_wq")
            split16(k, hdst[:, ncols * dc:ncols * (dc + 1)],
                    ldst[:, ncols * dc:ncols * (dc + 1)],
                    a_wtmp[:, 0:ncols], r=[a_wtmp], w=[hdst, ldst])
    for dc in range(NDC):
        p.dma("sync", a_wtmp[:, 0:128], wk[128 * dc:128 * (dc + 1), :],
              sem="d_wq")
        p.dma("sync", a_wtmp[:, 128:256], wv[128 * dc:128 * (dc + 1), :],
              sem="d_wq")
        split16(k, a_wkvh[:, 256 * dc:256 * (dc + 1)],
                a_wkvl[:, 256 * dc:256 * (dc + 1)],
                a_wtmp[:, 0:256], r=[a_wtmp], w=[a_wkvh, a_wkvl])

    for tt in range(NQT):
        k.v("memset", dict(ap=a_v[:, 130 * tt + 64:130 * tt + 65],
                           constant=1.0), w=[a_v])
        k.v("memset", dict(ap=a_v[:, 130 * tt + 129:130 * tt + 130],
                           constant=1.0), w=[a_v])

    TH = T // 4
    for hh in range(4):
        for ttl in range(TH // 128):
            tt = hh * (TH // 128) + ttl
            p.dma("sync", a_x[:], x0[128 * tt:128 * (tt + 1), :], sem="d_x2")
            emit_ln(k, a_x, a_t, a_sq, s_st8, EPS_A, s_ln1r)
            for dc in range(NDC):
                bank, bk = PSB[dc % 2], PK[dc % 2]
                k.t("transpose", dict(out=bank[:, 0:128],
                                      in_=a_t[:, 128 * dc:128 * (dc + 1)],
                                      identity=sid[:]), r=[a_t, sid], w=[bk])
                fo = dc * TH + 128 * ttl
                split16(k, h_hnh[:, fo:fo + 128], h_hnl[:, fo:fo + 128],
                        bank[:, 0:128], r=[bk], w=[h_hnh, h_hnl])
        for ttl in range(TH // 128):
            tt = hh * (TH // 128) + ttl
            bq, bqk = PSB[2], PK[2]
            bkv, bkvk = PSB[3], PK[3]
            for dc in range(NDC):
                mm3(k, bq[:, 0:HB * 64],
                    h_hnh[:, dc * TH + 128 * ttl:dc * TH + 128 * (ttl + 1)],
                    h_hnl[:, dc * TH + 128 * ttl:dc * TH + 128 * (ttl + 1)],
                    a_wqh[:, HB * 64 * dc:HB * 64 * (dc + 1)],
                    a_wql[:, HB * 64 * dc:HB * 64 * (dc + 1)],
                    start=(dc == 0), stop=(dc == NDC - 1),
                    r=[h_hnh, h_hnl, a_wqh, a_wql], w=[bqk])
                mm3(k, bkv[:, 0:256],
                    h_hnh[:, dc * TH + 128 * ttl:dc * TH + 128 * (ttl + 1)],
                    h_hnl[:, dc * TH + 128 * ttl:dc * TH + 128 * (ttl + 1)],
                    a_wkvh[:, 256 * dc:256 * (dc + 1)],
                    a_wkvl[:, 256 * dc:256 * (dc + 1)],
                    start=(dc == 0), stop=(dc == NDC - 1),
                    r=[h_hnh, h_hnl, a_wkvh, a_wkvl], w=[bkvk])
            for g in range(2):
                k.s("copy", dict(out=a_v[:, 130 * tt + 65 * g:
                                        130 * tt + 65 * g + 64],
                                 in_=bkv[:, 128 + 64 * g:128 + 64 * (g + 1)]),
                    r=[bkvk], w=[a_v])
            p.dma("sync", a_cs[:], rope_cs[128 * tt:128 * (tt + 1), :],
                  sem="d_cs")
            for (src_bank, srck, nh, dst, nrm) in (
                    (bq, bqk, HB, a_qtile, s_qnr), (bkv, bkvk, 2, a_kvt, s_knr)):
                cosb = a_cs[:, 0:32].unsqueeze(1).broadcast_to([128, nh, 32])
                sinb = a_cs[:, 32:64].unsqueeze(1).broadcast_to([128, nh, 32])
                x1v = src_bank[:, 0:64 * nh].rearrange(
                    "p (h d) -> p h d", d=64)[:, :, 0:32]
                x2v = src_bank[:, 0:64 * nh].rearrange(
                    "p (h d) -> p h d", d=64)[:, :, 32:64]
                r1v = dst[:, 0:64 * nh].rearrange(
                    "p (h d) -> p h d", d=64)[:, :, 0:32]
                r2v = dst[:, 0:64 * nh].rearrange(
                    "p (h d) -> p h d", d=64)[:, :, 32:64]
                tv = a_rr[:, 0:64 * nh].rearrange(
                    "p (h d) -> p h d", d=64)
                t1v, t2v = tv[:, :, 0:32], tv[:, :, 32:64]
                k.v("tensor_tensor", dict(out=t1v, in0=x1v, in1=cosb,
                                          op=ALU.mult),
                    r=[srck, a_cs], w=[a_rr])
                k.v("tensor_tensor", dict(out=t2v, in0=x2v, in1=sinb,
                                          op=ALU.mult),
                    r=[srck, a_cs], w=[a_rr])
                k.v("tensor_tensor", dict(out=r1v, in0=t1v, in1=t2v,
                                          op=ALU.subtract),
                    r=[a_rr], w=[dst])
                k.v("tensor_tensor", dict(out=t1v, in0=x1v, in1=sinb,
                                          op=ALU.mult),
                    r=[srck, a_cs], w=[a_rr])
                k.v("tensor_tensor", dict(out=t2v, in0=x2v, in1=cosb,
                                          op=ALU.mult),
                    r=[srck, a_cs], w=[a_rr])
                k.v("tensor_tensor", dict(out=r2v, in0=t1v, in1=t2v,
                                          op=ALU.add),
                    r=[a_rr], w=[dst])
                dv = dst[:, 0:64 * nh].rearrange("p (h d) -> p h d", d=64)
                k.s("activation", dict(out=a_rr[:, 0:64 * nh],
                                       in_=dst[:, 0:64 * nh], func=AF.Square),
                    r=[dst], w=[a_rr])
                k.v("tensor_reduce", dict(
                    out=a_ms[:, 0:nh],
                    in_=a_rr[:, 0:64 * nh].rearrange("p (h d) -> p h d", d=64),
                    axis=AXX, op=ALU.add), r=[a_rr], w=[a_ms])
                c_i64 = k.const_col(1.0 / 64.0)
                c_ea = k.const_col(EPS_A)
                k.s("activation", dict(out=a_ms[:, 8:8 + nh],
                                       in_=a_ms[:, 0:nh], func=AF.Sqrt,
                                       scale=c_i64, bias=c_ea),
                    r=[a_ms, "k_consts"], w=[a_ms])
                k.v("reciprocal", dict(out=a_ms[:, 0:nh], in_=a_ms[:, 8:8 + nh]),
                    r=[a_ms], w=[a_ms])
                rstdb = a_ms[:, 0:nh].unsqueeze(2).broadcast_to([128, nh, 64])
                k.v("tensor_tensor", dict(out=dv, in0=dv, in1=rstdb,
                                          op=ALU.mult), r=[dst, a_ms], w=[dst])
                nrmb = nrm[:].unsqueeze(1).broadcast_to([128, nh, 64])
                k.v("tensor_tensor", dict(out=dv, in0=dv, in1=nrmb,
                                          op=ALU.mult), r=[dst, nrm], w=[dst])
            for h in range(HB):
                bank, bk = PSB[4 + (h % 2)], PK[4 + (h % 2)]
                k.t("transpose", dict(out=bank[0:64, 0:128],
                                      in_=a_qtile[:, 64 * h:64 * (h + 1)],
                                      identity=sid[:]),
                    r=[a_qtile, sid], w=[bk])
                split16(k, a_qsth[:], a_qstl[:],
                        bank[0:64, 0:128], r=[bk], w=[a_qsth, a_qstl])
                qv_h = qt_sp_h[h, :].rearrange("(d t) -> d t", d=64)
                qv_l = qt_sp_l[h, :].rearrange("(d t) -> d t", d=64)
                p.dma("sync", qv_h[:, 128 * tt:128 * (tt + 1)], a_qsth[:],
                      sem="d_qsh")
                p.dma("sync", qv_l[:, 128 * tt:128 * (tt + 1)], a_qstl[:],
                      sem="d_qsl")
            for g in range(2):
                bank, bk = PSB[6], PK[6]
                k.t("transpose", dict(out=bank[0:64, 0:128],
                                      in_=a_kvt[:, 64 * g:64 * (g + 1)],
                                      identity=sid[:]),
                    r=[a_kvt, sid], w=[bk])
                fo = g * T + 128 * tt
                split16(k, a_kth[0:64, fo:fo + 128],
                        a_ktl[0:64, fo:fo + 128],
                        bank[0:64, 0:128], r=[bk], w=[a_kth, a_ktl])
    for g in range(2):
        p.dma("sync", a_kth[64:128, g * T:(g + 1) * T],
              a_kth[0:64, g * T:(g + 1) * T], sem=f"d_kdh{g}")
        p.dma("sync", a_ktl[64:128, g * T:(g + 1) * T],
              a_ktl[0:64, g * T:(g + 1) * T], sem=f"d_kdl{g}")
    es_2a.close()
    p.barrier()

    es_2b = contextlib.ExitStack()
    a_pt = k.sb("a_pt", [128, 512], es=es_2b)
    a_on = k.sb("a_on", [128, 512], es=es_2b)
    a_l1 = k.sb("a_l1", [1, 512], es=es_2b)
    a_gg = k.sb("a_gg", [64, 512], es=es_2b)
    s_wg2 = k.sb("s_wg2", [64, 64], es=es_2b)
    c_s8 = k.const_col(0.125)
    p.dma("sync", s_wg2[:], wg, sem="d_ini", serial=True)

    NQM = T // 512
    QW_H = [k.sb(f"a_qwh{i}", [128, 512], BF16, es=es_2b) for i in range(2)]
    QW_L = [k.sb(f"a_qwl{i}", [128, 512], BF16, es=es_2b) for i in range(2)]
    PT = [a_pt, k.sb("a_pt2", [128, 512], es=es_2b)]
    for h in range(HB):
        g = h // 4
        pr_h = (h % 2) * 64
        for qm in range(NQM):
            qpar = (h * NQM + qm) % 2
            a_qh, a_ql = QW_H[qpar], QW_L[qpar]
            nkc = 4 * (qm + 1)
            bo, bok = PSB[7], PK[7]
            qv_h = qt_sp_h[h, :].rearrange("(d t) -> d t", d=64)
            qv_l = qt_sp_l[h, :].rearrange("(d t) -> d t", d=64)
            p.dma("sync", a_qh[pr_h:pr_h + 64, :],
                  qv_h[:, 512 * qm:512 * (qm + 1)], sem="d_qlh" + str(qpar))
            p.dma("sync", a_ql[pr_h:pr_h + 64, :],
                  qv_l[:, 512 * qm:512 * (qm + 1)], sem="d_qll" + str(qpar))

            def emit_qk(kc):
                bs, bsk = PSB[4 + (kc % 2)], PK[4 + (kc % 2)]
                mm3(k, bs[:, 0:512],
                    a_kth[pr_h:pr_h + 64,
                          g * T + 128 * kc:g * T + 128 * (kc + 1)],
                    a_ktl[pr_h:pr_h + 64,
                          g * T + 128 * kc:g * T + 128 * (kc + 1)],
                    a_qh[pr_h:pr_h + 64, :],
                    a_ql[pr_h:pr_h + 64, :],
                    start=True, stop=True,
                    r=[a_kth, a_ktl, a_qh, a_ql], w=[bsk])

            emit_qk(0)
            for kc in range(nkc):
                bs, bsk = PSB[4 + (kc % 2)], PK[4 + (kc % 2)]
                pt = PT[kc % 2]
                ptk = _key(pt[:])
                k.s("activation", dict(out=pt[:], in_=bs[:, 0:512],
                                       func=AF.Exp, scale=c_s8),
                    r=[bsk, "k_consts"], w=[ptk])
                if kc + 1 < nkc:
                    emit_qk(kc + 1)
                j = kc - 4 * qm
                if j >= 0:
                    if j > 0:
                        k.v("memset", dict(ap=pt[:, 0:128 * j], constant=0.0),
                            r=[], w=[ptk])
                    k.v("tensor_mul", dict(out=pt[:, 128 * j:128 * (j + 1)],
                                           in0=pt[:, 128 * j:128 * (j + 1)],
                                           in1=s_triu[:]),
                        r=[s_triu, ptk], w=[ptk])
                k.t("matmul", dict(out=bo[0:65, 0:512],
                                   lhsT=a_v[:, 130 * kc + 65 * g:
                                            130 * kc + 65 * (g + 1)],
                                   rhs=pt[:], start=(kc == 0),
                                   stop=(kc == nkc - 1)),
                    r=[a_v, ptk], w=[bok])
            k.v("tensor_copy", dict(out=a_on[0:65, :], in_=bo[0:65, 0:512]),
                r=[bok], w=[a_on])
            k.v("reciprocal", dict(out=a_l1[:], in_=a_on[64:65, :]),
                r=[a_on], w=[a_l1])
            k.t("matmul", dict(out=PSB[3][0:64, 0:512], lhsT=s_ones[:, 0:64],
                               rhs=a_l1[:], start=True, stop=True),
                r=[s_ones, a_l1], w=[PK[3]])
            k.v("tensor_tensor", dict(out=a_on[0:64, :], in0=a_on[0:64, :],
                                      in1=PSB[3][0:64, 0:512], op=ALU.mult),
                r=[a_on, PK[3]], w=[a_on])
            k.t("matmul", dict(out=PSB[2][0:64, 0:512],
                               lhsT=s_wg2[:],
                               rhs=a_on[0:64, :], start=True, stop=True),
                r=[s_wg2, a_on], w=[PK[2]])
            k.s("activation", dict(out=a_gg[:], in_=PSB[2][0:64, 0:512],
                                   func=AF.Sigmoid), r=[PK[2]], w=[a_gg])
            fo_h = (h // 2) * T
            k.v("tensor_tensor", dict(
                out=a_gt[pr_h:pr_h + 64, fo_h + 512 * qm:fo_h + 512 * (qm + 1)],
                in0=a_on[0:64, :], in1=a_gg[:], op=ALU.mult),
                r=[a_on, a_gg], w=[a_gt])
    es_2b.close()
    es_2q.close()
    p.barrier()
    es_2b = contextlib.ExitStack()
    a_gth = k.sb("a_gth", [128, 4 * T], BF16, es=es_2b)
    a_gtl = k.sb("a_gtl", [128, 4 * T], BF16, es=es_2b)
    a_woh = k.sb("a_woh", [128, 4 * 128], BF16, es=es_2b)
    a_wol = k.sb("a_wol", [128, 4 * 128], BF16, es=es_2b)
    a_wot = k.sb("a_wot", [128, 128], es=es_2b)
    a_mo2 = k.sb("a_mo2", [128, 512], es=es_2b)
    a_mo3 = k.sb("a_mo3", [128, 128], es=es_2b)
    split16(k, a_gth[:], a_gtl[:], a_gt[:], r=[a_gt], w=[a_gth, a_gtl])
    for dc in range(NDC):
        for gg in range(4):
            p.dma("sync", a_wot[:], wo_r[gg, :, 128 * dc:128 * (dc + 1)],
                  sem="d_wo")
            split16(k, a_woh[:, 128 * gg:128 * (gg + 1)],
                    a_wol[:, 128 * gg:128 * (gg + 1)], a_wot[:],
                    r=[a_wot], w=[a_woh, a_wol])
        for ts in range(T // 512):
            bank, bk = PSB[0 + (ts % 2)], PK[0 + (ts % 2)]
            for gg in range(4):
                mm3(k, bank[:, 0:512],
                    a_woh[:, 128 * gg:128 * (gg + 1)],
                    a_wol[:, 128 * gg:128 * (gg + 1)],
                    a_gth[:, gg * T + 512 * ts:gg * T + 512 * (ts + 1)],
                    a_gtl[:, gg * T + 512 * ts:gg * T + 512 * (ts + 1)],
                    start=(gg == 0), stop=(gg == 3),
                    r=[a_woh, a_wol, a_gth, a_gtl], w=[bk])
            k.s("copy", dict(out=a_mo2[:], in_=bank[:, 0:512]), r=[bk],
                w=[a_mo2])
            for sub in range(4):
                tb2, tbk2 = PSB[6], PK[6]
                k.t("transpose", dict(out=tb2[:, 0:128],
                                      in_=a_mo2[:, 128 * sub:128 * (sub + 1)],
                                      identity=sid[:]), r=[a_mo2, sid], w=[tbk2])
                k.s("copy", dict(out=a_mo3[:, 0:128], in_=tb2[:, 0:128]),
                    r=[tbk2], w=[a_mo3])
                p.dma("sync",
                      cc2_in[0, :].rearrange("(t d) -> t d", d=D)[
                          512 * ts + 128 * sub:512 * ts + 128 * (sub + 1),
                          128 * dc:128 * (dc + 1)],
                      a_mo3[:, 0:128], sem="d_cc2")
    k.emit_cc("ReduceScatter", ALU.add, PAIRS, cc2_in, cc2_out)
    if "attn_mm" in k.cfg["DEBUG_OUT"]:
        o = k.dram("dbg_attn_mm", [TC, D], kind="ExternalOutput")
        k.dbg_outs["attn_mm"] = [TC, D]
        s_m1 = k.sb("s_m1b", [128, D], es=es_2b)
        for tt in range(TC // 128):
            p.dma("sync", s_m1[:], cc2_out[0, :].rearrange(
                "(t d) -> t d", d=D)[128 * tt:128 * (tt + 1), :], sem="d_dbg1")
            p.dma("gpsimd", o[128 * tt:128 * (tt + 1), :], s_m1[:], sem="d_dbg")
    es_2b.close()
    es_2.close()
    p.barrier()

    if cfg.get("STOP_AFTER") == 2:
        return k, locals()
    NTC = TC // 128
    CAP = k.CAP
    NSLOT = 8 * CAP
    NSL = CAP // 128
    HID = k.HID
    NUC = HID // 128
    x0o = k.dram("x0o", [TC, D], kind="ExternalInput")
    ln2_w = k.dram("ln2_w", [D], kind="ExternalInput")
    wr = k.dram("wr", [D, 8], kind="ExternalInput")
    w1 = k.dram("w1", [8, D, 2 * HID], kind="ExternalInput")
    w2 = k.dram("w2", [8, HID, D], kind="ExternalInput")
    moe_in = k.dram("moe_in", [NSLOT, D])
    moe_out = k.dram("moe_out", [NSLOT, D])
    erow8 = k.inline("erow8", np.tile(np.arange(8, dtype=np.float32), (128, 1)))
    lstrict = k.inline("lstrict",
                       np.triu(np.ones((128, 128), np.float32), 1).T.copy())

    es_3 = contextlib.ExitStack()
    q_x1 = k.sb("q_x1", [128, NTC * D], es=es_3)
    q_dst = k.sb("q_dst", [128, NTC], U32, es=es_3)

    es_3a = contextlib.ExitStack()
    b_xpp = k.sb("b_xpp", [128, NTC * D], es=es_3a)
    s_ln2r = k.sb("s_ln2r", [128, D], es=es_3a)
    s_wr = k.sb("s_wr", [128, NDC * 8], es=es_3a)
    s_er8 = k.sb("s_er8", [128, 8], es=es_3a)
    s_ls = k.sb("s_ls", [128, 128], es=es_3a)
    a_x = k.sb("a_x3", [128, D], es=es_3a)
    a_t = k.sb("a_t3", [128, D], es=es_3a)
    a_sq = k.sb("a_sq3", [128, D], es=es_3a)
    a_c1 = k.sb("a_c1", [128, D], es=es_3a)
    a_h2 = k.sb("a_h2", [128, D], es=es_3a)
    a_h2T = k.sb("a_h2T", [128, NDC * 128], es=es_3a)
    a_mx8 = k.sb("a_mx8", [128, 8], es=es_3a)
    a_ix8 = k.sb("a_ix8", [128, 8], U32, es=es_3a)
    a_ef = k.sb("a_ef", [128, 1], es=es_3a)
    a_oh = k.sb("a_oh", [128, 8], es=es_3a)
    a_tc = k.sb("a_tc", [128, 8], es=es_3a)
    a_rk = k.sb("a_rk", [128, 1], es=es_3a)
    a_di = k.sb("a_di", [128, 1], U32, es=es_3a)
    s_base = k.sb("s_base", [1, 8], es=es_3a)
    a_zz = k.sb("a_zz", [128, D], es=es_3a)

    p.dma("sync", s_er8[:], erow8, sem="d_ini", serial=True)
    p.dma("sync", s_ls[:], lstrict, sem="d_ini", serial=True)
    p.dma("sync", s_wr[:].rearrange("p (c m) -> p c m", m=8),
          wr.rearrange("(c p) m -> p c m", p=128), sem="d_ini", serial=True)
    s_r2 = k.sb("s_r2", [1, D], es=es_3a)
    p.dma("sync", s_r2[:], ln2_w[None, :], sem="d_ini", serial=True)
    for dc in range(NDC):
        k.t("matmul", dict(out=PSB[0][:, 0:128], lhsT=s_ones[:],
                           rhs=s_r2[:, 128 * dc:128 * (dc + 1)],
                           start=True, stop=True), r=[s_ones, s_r2], w=[PK[0]])
        k.s("copy", dict(out=s_ln2r[:, 128 * dc:128 * (dc + 1)],
                         in_=PSB[0][:, 0:128]), r=[PK[0]], w=[s_ln2r])
    k.v("memset", dict(ap=s_base[:], constant=0.0), w=[s_base])
    k.v("memset", dict(ap=a_zz[:], constant=0.0), w=[a_zz])
    for sz in range(NSLOT // 128):
        p.dma("sync", moe_in[128 * sz:128 * (sz + 1), :], a_zz[:], sem="d_mz",
              r=[a_zz])

    c1v = cc1_out[0, :].rearrange("(t d) -> t d", d=D)
    c2v = cc2_out[0, :].rearrange("(t d) -> t d", d=D)
    for tt in range(NTC):
        sl = slice(128 * tt, 128 * (tt + 1))
        xsl = slice(D * tt, D * (tt + 1))
        p.dma("sync", a_x[:], x0o[sl, :], sem="d_x3")
        emit_ln(k, a_x, a_t, a_sq, s_st8, None, None)
        p.dma("sync", a_c1[:], c2v[sl, :], sem="d_cl")
        k.v("tensor_add", dict(out=b_xpp[:, xsl], in0=a_x[:], in1=a_c1[:]),
            r=[a_x, a_c1], w=[b_xpp])
        k.v("tensor_add", dict(out=q_x1[:, xsl], in0=b_xpp[:, xsl], in1=a_x[:]),
            r=[b_xpp, a_x], w=[q_x1])
        p.dma("sync", a_c1[:], c1v[sl, :], sem="d_cl")
        k.v("tensor_add", dict(out=q_x1[:, xsl], in0=q_x1[:, xsl], in1=a_c1[:]),
            r=[q_x1, a_c1], w=[q_x1])
        p.dma("sync", a_c1[:], x0o[sl, :], sem="d_x3b")
        k.v("tensor_add", dict(out=q_x1[:, xsl], in0=q_x1[:, xsl], in1=a_c1[:]),
            r=[q_x1, a_c1], w=[q_x1])
        k.s("activation", dict(out=a_sq[:], in_=b_xpp[:, xsl], func=AF.Square,
                               accum_out=s_st8[:, 0:1]),
            r=[b_xpp], w=[a_sq, "a_st"])
        c_invd = k.const_col(1.0 / D)
        c_ea2 = k.const_col(EPS_A)
        k.s("activation", dict(out=s_st8[:, 1:2], in_=s_st8[:, 0:1],
                               func=AF.Sqrt, scale=c_invd, bias=c_ea2),
            r=["a_st", "k_consts"], w=["a_st"])
        k.v("reciprocal", dict(out=s_st8[:, 2:3], in_=s_st8[:, 1:2]),
            r=["a_st"], w=["a_st"])
        k.v("scalar_tensor_tensor", dict(
            out=a_h2[:], in0=b_xpp[:, xsl], scalar=s_st8[:, 2:3],
            in1=s_ln2r[:], op0=ALU.mult, op1=ALU.mult),
            r=[b_xpp, "a_st", s_ln2r], w=[a_h2])
        for dc in range(NDC):
            bank, bk = PSB[1 + (dc % 2)], PK[1 + (dc % 2)]
            k.t("transpose", dict(out=bank[:, 0:128],
                                  in_=a_h2[:, 128 * dc:128 * (dc + 1)],
                                  identity=sid[:]), r=[a_h2, sid], w=[bk])
            k.s("copy", dict(out=a_h2T[:, 128 * dc:128 * (dc + 1)],
                             in_=bank[:, 0:128]), r=[bk], w=[a_h2T])
        for dc in range(NDC):
            k.t("matmul", dict(out=PSB[3][:, 0:8],
                               lhsT=a_h2T[:, 128 * dc:128 * (dc + 1)],
                               rhs=s_wr[:, 8 * dc:8 * (dc + 1)],
                               start=(dc == 0), stop=(dc == NDC - 1)),
                r=[a_h2T, s_wr], w=[PK[3]])
        k.v("tensor_copy", dict(out=a_tc[:], in_=PSB[3][:, 0:8]),
            r=[PK[3]], w=[a_tc])
        k.v("max", dict(out=a_mx8[:], in_=a_tc[:]), r=[a_tc], w=[a_mx8])
        k.v("max_index", dict(out=a_ix8[:], in_max=a_mx8[:], in_values=a_tc[:]),
            r=[a_mx8, a_tc], w=[a_ix8])
        k.v("tensor_copy", dict(out=a_ef[:], in_=a_ix8[:, 0:1]),
            r=[a_ix8], w=[a_ef])
        efb = a_ef[:].broadcast_to([128, 8])
        k.v("tensor_tensor", dict(out=a_oh[:], in0=efb, in1=s_er8[:],
                                  op=ALU.is_equal), r=[a_ef, s_er8], w=[a_oh])
        k.t("matmul", dict(out=PSB[3][:, 0:8], lhsT=s_ls[:], rhs=a_oh[:],
                           start=True, stop=True), r=[s_ls, a_oh], w=[PK[3]])
        k.v("tensor_copy", dict(out=a_tc[:], in_=PSB[3][:, 0:8]),
            r=[PK[3]], w=[a_tc])
        k.t("matmul", dict(out=PSB[4][0:1, 0:8], lhsT=ones_colT(k),
                           rhs=a_oh[:], start=True, stop=True),
            r=["k_onec", a_oh], w=[PK[4]])
        k.t("matmul", dict(out=PSB[5][:, 0:8], lhsT=s_ones[:], rhs=s_base[:],
                           start=True, stop=True), r=[s_ones, s_base],
            w=[PK[5]])
        k.v("tensor_tensor", dict(out=a_tc[:], in0=a_tc[:],
                                  in1=PSB[5][:, 0:8], op=ALU.add),
            r=[a_tc, PK[5]], w=[a_tc])
        k.v("tensor_tensor", dict(out=a_tc[:], in0=a_tc[:], in1=a_oh[:],
                                  op=ALU.mult), r=[a_tc, a_oh], w=[a_tc])
        k.v("tensor_reduce", dict(out=a_rk[:], in_=a_tc[:], axis=AXX,
                                  op=ALU.add), r=[a_tc], w=[a_rk])
        k.v("scalar_tensor_tensor", dict(
            out=a_rk[:], in0=a_ef[:], scalar=float(CAP), in1=a_rk[:],
            op0=ALU.mult, op1=ALU.add), r=[a_ef, a_rk], w=[a_rk])
        k.v("tensor_scalar_min", dict(out=a_rk[:], in0=a_rk[:],
                                      scalar1=float(NSLOT - 1)),
            r=[a_rk], w=[a_rk])
        k.v("tensor_copy", dict(out=a_di[:], in_=a_rk[:]), r=[a_rk], w=[a_di])
        k.v("tensor_copy", dict(out=q_dst[:, tt:tt + 1], in_=a_di[:]),
            r=[a_di], w=[q_dst])
        k.v("tensor_tensor", dict(out=s_base[:], in0=s_base[:],
                                  in1=PSB[4][0:1, 0:8], op=ALU.add),
            r=[s_base, PK[4]], w=[s_base])
        p.idma("gpsimd", moe_in, (a_di[:], 0), a_h2[:], None, sem="d_sc",
               r=[], w=[])
    es_3a.close()
    p.barrier()

    es_3b = contextlib.ExitStack()
    XET = [k.sb(f"e_xeT{i}", [128, NDC * CAP], es=es_3b) for i in range(2)]
    ACT2 = [k.sb("e_act0", [128, NUC * CAP], es=es_3b)] * 2
    W1A = [k.sb(f"e_w1a{i}", [128, NDC * 128], es=es_3b) for i in range(2)]
    W1B = [k.sb(f"e_w1b{i}", [128, NDC * 128], es=es_3b) for i in range(2)]
    W2B = [k.sb(f"e_w2_{i}", [128, NUC * 128], es=es_3b) for i in range(2)]
    SIG2 = [k.sb(f"e_sig{i}", [128, CAP], es=es_3b) for i in range(2)]
    YT2 = [k.sb("e_yT0", [128, NDC * CAP], es=es_3b)] * 2
    YTT = [k.sb(f"e_yt{i}", [128, D], es=es_3b) for i in range(2)]
    YTG = [k.sb(f"e_ytg{i}", [128, D], es=es_3b) for i in range(2)]
    for e in range(8):
        ep = e % 2
        e_xeT, e_act, e_sig = XET[ep], ACT2[ep], SIG2[ep]
        e_yT, e_yt, e_ytg = YT2[ep], YTT[ep], YTG[ep]
        for sl2 in range(NSL):
            p.dma("sync", e_yt[:], moe_in[e * CAP + 128 * sl2:
                                          e * CAP + 128 * (sl2 + 1), :],
                  sem="d_mi" + str(ep))
            for dc in range(NDC):
                bank, bk = PSB[0 + (dc % 2)], PK[0 + (dc % 2)]
                k.t("transpose", dict(out=bank[:, 0:128],
                                      in_=e_yt[:, 128 * dc:128 * (dc + 1)],
                                      identity=sid[:]), r=[e_yt, sid], w=[bk])
                k.s("copy", dict(
                    out=e_xeT[:, CAP * dc + 128 * sl2:CAP * dc + 128 * (sl2 + 1)],
                    in_=bank[:, 0:128]), r=[bk], w=[e_xeT])
        for uc in range(NUC):
            wa, wb = W1A[(e * NUC + uc) % 2], W1B[(e * NUC + uc) % 2]
            p.dma("sync", wa[:].rearrange("p (c m) -> p c m", m=128),
                  w1[e, :, 128 * uc:128 * (uc + 1)].rearrange(
                      "(c p) m -> p c m", p=128), sem="d_w1a" + str((e * NUC + uc) % 2))
            p.dma("sync", wb[:].rearrange("p (c m) -> p c m", m=128),
                  w1[e, :, HID + 128 * uc:HID + 128 * (uc + 1)].rearrange(
                      "(c p) m -> p c m", p=128), sem="d_w1b" + str((e * NUC + uc) % 2))
            bka, bkb = PSB[2 + 2 * (uc % 2)], PSB[3 + 2 * (uc % 2)]
            kka, kkb = PK[2 + 2 * (uc % 2)], PK[3 + 2 * (uc % 2)]
            for dc in range(NDC):
                k.t("matmul", dict(out=bka[:, 0:CAP],
                                   lhsT=wa[:, 128 * dc:128 * (dc + 1)],
                                   rhs=e_xeT[:, CAP * dc:CAP * (dc + 1)],
                                   start=(dc == 0), stop=(dc == NDC - 1)),
                    r=[wa, e_xeT], w=[kka])
                k.t("matmul", dict(out=bkb[:, 0:CAP],
                                   lhsT=wb[:, 128 * dc:128 * (dc + 1)],
                                   rhs=e_xeT[:, CAP * dc:CAP * (dc + 1)],
                                   start=(dc == 0), stop=(dc == NDC - 1)),
                    r=[wb, e_xeT], w=[kkb])
            k.s("activation", dict(out=e_sig[:], in_=bkb[:, 0:CAP],
                                   func=AF.Sigmoid), r=[kkb], w=[e_sig])
            k.v("tensor_tensor", dict(out=e_act[:, CAP * uc:CAP * (uc + 1)],
                                      in0=bka[:, 0:CAP], in1=e_sig[:],
                                      op=ALU.mult), r=[kka, e_sig], w=[e_act])
        for dcol in range(NDC):
            w2b = W2B[(e * NDC + dcol) % 2]
            p.dma("sync", w2b[:].rearrange("p (c m) -> p c m", m=128),
                  w2[e, :, 128 * dcol:128 * (dcol + 1)].rearrange(
                      "(c p) m -> p c m", p=128), sem="d_w2" + str((e * NDC + dcol) % 2))
            bank, bk = PSB[6 + (dcol % 2)], PK[6 + (dcol % 2)]
            for hc in range(NUC):
                k.t("matmul", dict(out=bank[:, 0:CAP],
                                   lhsT=w2b[:, 128 * hc:128 * (hc + 1)],
                                   rhs=e_act[:, CAP * hc:CAP * (hc + 1)],
                                   start=(hc == 0), stop=(hc == NUC - 1)),
                    r=[w2b, e_act], w=[bk])
            k.s("copy", dict(out=e_yT[:, CAP * dcol:CAP * (dcol + 1)],
                             in_=bank[:, 0:CAP]), r=[bk], w=[e_yT])
        for sl2 in range(NSL):
            for dc in range(NDC):
                bank, bk = PSB[0 + (dc % 2)], PK[0 + (dc % 2)]
                k.t("transpose", dict(
                    out=bank[:, 0:128],
                    in_=e_yT[:, CAP * dc + 128 * sl2:CAP * dc + 128 * (sl2 + 1)],
                    identity=sid[:]), r=[e_yT, sid], w=[bk])
                k.s("copy", dict(out=e_ytg[:, 128 * dc:128 * (dc + 1)],
                                 in_=bank[:, 0:128]), r=[bk], w=[e_ytg])
            p.dma("sync", moe_out[e * CAP + 128 * sl2:
                                  e * CAP + 128 * (sl2 + 1), :], e_ytg[:],
                  sem="d_mo" + str(ep))
    es_3b.close()
    p.barrier()

    es_3c = contextlib.ExitStack()
    g_y = k.sb("g_y", [128, D], es=es_3c)
    g_i = k.sb("g_i", [128, 1], U32, es=es_3c)
    for tt in range(NTC):
        xsl = slice(D * tt, D * (tt + 1))
        k.v("tensor_copy", dict(out=g_i[:], in_=q_dst[:, tt:tt + 1]),
            r=[q_dst], w=[g_i])
        p.idma("gpsimd", g_y[:], None, moe_out, (g_i[:], 0), sem="d_gy",
               r=[], w=[])
        k.v("tensor_add", dict(out=q_x1[:, xsl], in0=q_x1[:, xsl], in1=g_y[:]),
            r=[q_x1, g_y], w=[q_x1])
    k.debug("x1", q_x1[:], [128, NTC * D])
    es_3c.close()
    p.barrier()

    if cfg.get("STOP_AFTER") == 3:
        k.phase_stacks.append(es_3)
        return k, locals()
    a_param = k.dram("a_param", [D], kind="ExternalInput")
    w_in = k.dram("w_in", [8, 128, 128], kind="ExternalInput")
    w_a = k.dram("w_a", [8, 128, 128], kind="ExternalInput")
    is_odd = k.dram("is_odd", [1, 1], kind="ExternalInput")
    wug = k.dram("wug", [2, k.I3, D], kind="ExternalInput")
    wd = k.dram("wd", [D, k.I3], kind="ExternalInput")
    I3 = k.I3
    NI3 = I3 // 128
    NTS = TC // 512 if TC >= 512 else 1
    WTS = min(512, TC)
    cc3_in = k.dram("cc3_in", [1, 8])
    cc3_out = k.dram("cc3_out", [8, 8])
    cc4_in = k.dram("cc4_in", [1, D])
    cc4_out = k.dram("cc4_out", [1, D])
    cc5_in = k.dram("cc5_in", [1, 8])
    cc5_out = k.dram("cc5_out", [8, 8])
    cc6_in = k.dram("cc6_in", [1, 8])
    cc6_out = k.dram("cc6_out", [8, 8])
    cc7_in = k.dram("cc7_in", [1, 8])
    cc7_out = k.dram("cc7_out", [8, 8])
    ab_spill = k.dram("ab_spill", [2 * D, TC])
    up_spill = k.dram("up_spill", [I3, TC])
    gate_spill = k.dram("gate_spill", [I3, TC])
    g2_spill = k.dram("g2_spill", [I3, TC])
    op_spill = k.dram("op_spill", [I3, TC])
    wqh_sp = k.dram("wqh_sp", [3 * I3, D], BF16)
    wql_sp = k.dram("wql_sp", [3 * I3, D], BF16)

    WTS0 = min(512, TC)
    es_4a = contextlib.ExitStack()
    s_sc = k.sb("s_sc", [1, 16], es=es_4a)
    s_scr = k.sb("s_scr", [128, 12], es=es_4a)
    g_cc = k.sb("g_cc", [1, 16], es=es_4a)
    UTB = [k.sb(f"u_t{i}", [128, WTS0], es=es_4a) for i in range(2)]
    u_mx = k.sb("u_mx", [128, 4], es=es_4a)
    USG = [k.sb(f"u_sg{i}", [128, WTS0], es=es_4a) for i in range(2)]
    UIT = [k.sb(f"u_it{i}", [128, WTS0], I32, es=es_4a) for i in range(2)]
    u_t, u_sg, u_it = UTB[0], USG[0], UIT[0]
    xq_sp_h = k.dram("xq_sp_h", [D, TC], BF16)
    xq_sp_l = k.dram("xq_sp_l", [D, TC], BF16)
    f_xqh = k.sb("f_xqh", [128, WTS0], BF16, es=es_4a)
    f_xql = k.sb("f_xql", [128, WTS0], BF16, es=es_4a)
    xq_th = k.sb("xq_th", [128, NDC * WTS0], BF16, es=es_4a)
    xq_tl = k.sb("xq_tl", [128, NDC * WTS0], BF16, es=es_4a)
    es_4f = contextlib.ExitStack()
    f_xn1 = k.sb("f_xn1", [128, NDC * TC], es=es_4f)
    es_4b = contextlib.ExitStack()
    a_x = k.sb("a_x4", [128, D], es=es_4b)
    a_t = k.sb("a_t4", [128, D], es=es_4b)
    a_sq = k.sb("a_sq4", [128, D], es=es_4b)
    a_am = k.sb("a_am", [128, 2], es=es_4b)
    for tt in range(NTC):
        xsl = slice(D * tt, D * (tt + 1))
        k.v("tensor_copy", dict(out=a_x[:], in_=q_x1[:, xsl]),
            r=[q_x1], w=[a_x])
        emit_ln(k, a_x, a_t, a_sq, s_st8, None, None)
        k.v("tensor_reduce", dict(out=a_am[:, 0:1], in_=a_x[:], axis=AXX,
                                  op=ALU.max, apply_absolute_value=True),
            r=[a_x], w=[a_am])
        if tt == 0:
            k.v("tensor_copy", dict(out=a_am[:, 1:2], in_=a_am[:, 0:1]),
                r=[a_am], w=[a_am])
        else:
            k.v("tensor_tensor", dict(out=a_am[:, 1:2], in0=a_am[:, 1:2],
                                      in1=a_am[:, 0:1], op=ALU.max),
                r=[a_am], w=[a_am])
        for dc in range(NDC):
            bank, bk = PSB[dc % 2], PK[dc % 2]
            k.t("transpose", dict(out=bank[:, 0:128],
                                  in_=a_x[:, 128 * dc:128 * (dc + 1)],
                                  identity=sid[:]), r=[a_x, sid], w=[bk])
            k.s("copy", dict(out=f_xn1[:, dc * TC + 128 * tt:
                                       dc * TC + 128 * (tt + 1)],
                             in_=bank[:, 0:128]), r=[bk], w=[f_xn1])
    if cfg.get("STOP_AFTER") == 31:
        k.phase_stacks.append(es_3)
        k.phase_stacks.append(es_4a)
        k.phase_stacks.append(es_4f)
        k.phase_stacks.append(es_4b)
        return k, locals()
    k.t("transpose", dict(out=PSB[2][0:1, 0:128], in_=a_am[:, 1:2],
                          identity=sid[:]), r=[a_am, sid], w=[PK[2]])
    k.v("tensor_reduce", dict(out=s_sc[:, 0:1], in_=PSB[2][0:1, 0:128],
                              axis=AXX, op=ALU.max), r=[PK[2]], w=[s_sc])
    k.v("memset", dict(ap=s_sc[:, 1:8], constant=0.0), w=[s_sc])
    p.dma("sync", cc3_in[0, :], s_sc[0:1, 0:8], sem="d_ccx")
    k.emit_cc("AllGather", ALU.bypass, ALL8, cc3_in, cc3_out)

    if cfg.get("STOP_AFTER") == 32:
        k.phase_stacks.append(es_3)
        k.phase_stacks.append(es_4a)
        k.phase_stacks.append(es_4f)
        k.phase_stacks.append(es_4b)
        return k, locals()
    s_sptmp2 = k.sb("s_sptmp2", [128, 512], es=es_4b)
    b_car = k.sb("b_car", [128, 16], es=es_4b)
    s_odd = k.sb("s_odd", [128, 1], es=es_4b)
    s_odd1 = k.sb("s_odd1", [1, 1], es=es_4b)
    p.dma("sync", s_odd1[:], is_odd, sem="d_io")
    k.t("matmul", dict(out=PSB[7][:, 0:1], lhsT=s_ones[:], rhs=s_odd1[:],
                       start=True, stop=True), r=[s_ones, s_odd1], w=[PK[7]])
    k.s("copy", dict(out=s_odd[:], in_=PSB[7][:, 0:1]), r=[PK[7]], w=[s_odd])
    BGA = [k.sb(f"b_ga{i}", [128, TC], es=es_4b) for i in range(2)]
    BBB = [k.sb(f"b_bb{i}", [128, TC], es=es_4b) for i in range(2)]
    BWB = [k.sb(f"b_wblk{i}", [128, 256], es=es_4b) for i in range(2)]
    b_spa = k.sb("b_spa", [128, NDC], es=es_4b)
    b_spt = k.sb("b_spt", [128, NDC], es=es_4b)
    p.dma("sync", b_spa[:], a_param.rearrange("(n p) -> p n", p=128),
          sem="d_ap", loose=True)
    softplus_inplace(k, b_spa[:], b_spt[:], b_spa, b_spt)
    k.v("tensor_scalar_mul", dict(out=b_spa[:], in0=b_spa[:], scalar1=-8.0),
        r=[b_spa], w=[b_spa])
    for dt in range(NDC):
        b_ga, b_bb, b_wblk = BGA[dt % 2], BBB[dt % 2], BWB[dt % 2]
        p.dma("sync", b_wblk[:, 0:128], w_in[dt], sem="d_wb" + str(dt % 2))
        p.dma("sync", b_wblk[:, 128:256], w_a[dt], sem="d_wb2" + str(dt % 2))
        for ts in range(NTS):
            bank, bk = PSB[2 + (ts % 2)], PK[2 + (ts % 2)]
            bank2, bk2 = PSB[4 + (ts % 2)], PK[4 + (ts % 2)]
            k.t("matmul", dict(out=bank[:, 0:WTS], lhsT=b_wblk[:, 0:128],
                               rhs=f_xn1[:, dt * TC + WTS * ts:
                                         dt * TC + WTS * (ts + 1)],
                               start=True, stop=True),
                r=[b_wblk, f_xn1], w=[bk])
            k.t("matmul", dict(out=bank2[:, 0:WTS], lhsT=b_wblk[:, 128:256],
                               rhs=f_xn1[:, dt * TC + WTS * ts:
                                          dt * TC + WTS * (ts + 1)],
                               start=True, stop=True),
                r=[b_wblk, f_xn1], w=[bk2])
            k.s("activation", dict(out=b_ga[:, WTS * ts:WTS * (ts + 1)],
                                   in_=bank2[:, 0:WTS], func=AF.Sigmoid),
                r=[bk2], w=[b_ga])
            k.v("tensor_scalar", dict(
                out=b_ga[:, WTS * ts:WTS * (ts + 1)],
                in0=b_ga[:, WTS * ts:WTS * (ts + 1)],
                scalar1=b_spa[:, dt:dt + 1], scalar2=None, op0=ALU.mult),
                r=[b_ga, b_spa], w=[b_ga])
            k.s("activation", dict(out=b_bb[:, WTS * ts:WTS * (ts + 1)],
                                   in_=b_ga[:, WTS * ts:WTS * (ts + 1)],
                                   func=AF.Exp, scale=2.0), r=[b_ga], w=[b_bb])
            k.v("tensor_scalar", dict(
                out=b_bb[:, WTS * ts:WTS * (ts + 1)],
                in0=b_bb[:, WTS * ts:WTS * (ts + 1)],
                scalar1=-1.0, scalar2=1.0, op0=ALU.mult, op1=ALU.add),
                r=[b_bb], w=[b_bb])
            k.s("activation", dict(out=b_bb[:, WTS * ts:WTS * (ts + 1)],
                                   in_=b_bb[:, WTS * ts:WTS * (ts + 1)],
                                   func=AF.Sqrt), r=[b_bb], w=[b_bb])
            k.s("activation", dict(out=s_sptmp2[:, 0:WTS], in_=bank[:, 0:WTS],
                                   func=AF.Sigmoid), r=[bk], w=[s_sptmp2])
            k.v("tensor_tensor", dict(out=s_sptmp2[:, 0:WTS],
                                      in0=s_sptmp2[:, 0:WTS],
                                      in1=f_xn1[:, dt * TC + WTS * ts:
                                                dt * TC + WTS * (ts + 1)],
                                      op=ALU.mult), r=[s_sptmp2, f_xn1],
                w=[s_sptmp2])
            k.v("tensor_tensor", dict(out=b_bb[:, WTS * ts:WTS * (ts + 1)],
                                      in0=b_bb[:, WTS * ts:WTS * (ts + 1)],
                                      in1=s_sptmp2[:, 0:WTS], op=ALU.mult),
                r=[b_bb, s_sptmp2], w=[b_bb])
            k.s("activation", dict(out=b_ga[:, WTS * ts:WTS * (ts + 1)],
                                   in_=b_ga[:, WTS * ts:WTS * (ts + 1)],
                                   func=AF.Exp), r=[b_ga], w=[b_ga])
        p.dma("sync", ab_spill[128 * dt:128 * (dt + 1), :], b_ga[:],
              sem="d_ab")
        p.dma("sync", ab_spill[D + 128 * dt:D + 128 * (dt + 1), :], b_bb[:],
              sem="d_ab")
        k.v("tensor_tensor_scan", dict(out=b_bb[:], data0=b_ga[:],
                                       data1=b_bb[:], initial=0.0,
                                       op0=ALU.mult, op1=ALU.add),
            r=[b_ga, b_bb], w=[b_bb])
        k.v("tensor_copy", dict(out=b_car[:, dt:dt + 1],
                                in_=b_bb[:, TC - 1:TC]), r=[b_bb], w=[b_car])
    if cfg.get("STOP_AFTER") == 33:
        k.phase_stacks.append(es_3)
        k.phase_stacks.append(es_4a)
        k.phase_stacks.append(es_4f)
        k.phase_stacks.append(es_4b)
        return k, locals()
    k.v("tensor_scalar", dict(out=b_car[:, 8:16], in0=b_car[:, 0:8],
                              scalar1=s_odd[:, 0:1], scalar2=None,
                              op0=ALU.mult), r=[b_car, s_odd], w=[b_car])
    k.v("tensor_sub", dict(out=b_car[:, 8:16], in0=b_car[:, 0:8],
                           in1=b_car[:, 8:16]), r=[b_car], w=[b_car])
    p.dma("sync", cc4_in.rearrange("a (n p) -> p (a n)", p=128),
          b_car[:, 8:16], sem="d_cc4", loose=True)
    k.emit_cc("AllReduce", ALU.add, PAIRS, cc4_in, cc4_out)
    p.dma("sync", b_car[:, 8:16], cc4_out.rearrange("a (n p) -> p (a n)", p=128),
          sem="d_cc4b", loose=True)
    k.v("tensor_scalar", dict(out=b_car[:, 8:16], in0=b_car[:, 8:16],
                              scalar1=s_odd[:, 0:1], scalar2=None,
                              op0=ALU.mult), r=[b_car, s_odd], w=[b_car])
    for dt in range(NDC):
        b_ga, b_bb = BGA[dt % 2], BBB[dt % 2]
        p.dma("sync", b_ga[:], ab_spill[128 * dt:128 * (dt + 1), :],
              sem="d_ab3" + str(dt % 2))
        p.dma("sync", b_bb[:], ab_spill[D + 128 * dt:D + 128 * (dt + 1), :],
              sem="d_ab4" + str(dt % 2))
        k.v("tensor_tensor_scan", dict(out=b_bb[:], data0=b_ga[:],
                                       data1=b_bb[:],
                                       initial=b_car[:, 8 + dt:9 + dt],
                                       op0=ALU.mult, op1=ALU.add),
            r=[b_ga, b_bb, b_car], w=[b_bb])
        for tt in range(NTC):
            bank, bk = PSB[6], PK[6]
            k.t("transpose", dict(out=bank[:, 0:128],
                                  in_=b_bb[:, 128 * tt:128 * (tt + 1)],
                                  identity=sid[:]), r=[b_bb, sid], w=[bk])
            k.v("tensor_tensor", dict(
                out=q_x1[:, D * tt + 128 * dt:D * tt + 128 * (dt + 1)],
                in0=q_x1[:, D * tt + 128 * dt:D * tt + 128 * (dt + 1)],
                in1=bank[:, 0:128], op=ALU.add), r=[q_x1, bk], w=[q_x1])
    k.debug("y1x", q_x1[:], [128, NTC * D])
    es_4b.close()
    p.barrier()
    if cfg.get("STOP_AFTER") == 4:
        k.phase_stacks.append(es_3)
        k.phase_stacks.append(es_4a)
        k.phase_stacks.append(es_4f)
        return k, locals()
    es_4c = contextlib.ExitStack()
    wq_up_h = k.dram("wq_up_h", [D, I3], BF16)
    wq_up_l = k.dram("wq_up_l", [D, I3], BF16)
    wq_g_h = k.dram("wq_g_h", [D, I3], BF16)
    wq_g_l = k.dram("wq_g_l", [D, I3], BF16)
    wq_d_h = k.dram("wq_d_h", [I3, D], BF16)
    wq_d_l = k.dram("wq_d_l", [I3, D], BF16)
    DWT = [k.sb(f"d_wt{i}", [128, I3], es=es_4c) for i in range(2)]
    DBM = [k.sb(f"d_bm{i}", [128, I3 // 32], es=es_4c) for i in range(2)]
    DBI = [k.sb(f"d_bi{i}", [128, I3 // 32], es=es_4c) for i in range(2)]
    DIT = [k.sb("d_it0", [128, I3], I32, es=es_4c)] * 2
    DTH = [k.sb(f"d_th{i}", [128, 128], BF16, es=es_4c) for i in range(2)]
    DTL = [k.sb(f"d_tl{i}", [128, 128], BF16, es=es_4c) for i in range(2)]

    def dequant_one(src_ap, n_rows, n_cols, out_h, out_l, semtag):
        nb = n_cols // 32
        for ot in range(n_rows // 128):
            d_wt, d_bm, d_bi = DWT[ot % 2], DBM[ot % 2], DBI[ot % 2]
            d_it, d_th, d_tl = DIT[ot % 2], DTH[ot % 2], DTL[ot % 2]
            p.dma("gpsimd", d_wt[:, 0:n_cols],
                  src_ap[128 * ot:128 * (ot + 1), :], sem="d_dqw" + str(ot % 2))
            v3 = d_wt[:, 0:n_cols].rearrange("p (b e) -> p b e", e=32)
            k.v("tensor_reduce", dict(out=d_bm[:, 0:nb], in_=v3, axis=AXX,
                                      op=ALU.max, apply_absolute_value=True),
                r=[d_wt], w=[d_bm])
            k.v("tensor_scalar", dict(out=d_bm[:, 0:nb], in0=d_bm[:, 0:nb],
                                      scalar1=1.0 / 7.0, scalar2=1e-12,
                                      op0=ALU.mult, op1=ALU.max),
                r=[d_bm], w=[d_bm])
            bm3 = d_bm[:, 0:nb].unsqueeze(2).broadcast_to([128, nb, 32])
            k.v("reciprocal", dict(out=d_bi[:, 0:nb], in_=d_bm[:, 0:nb]),
                r=[d_bm], w=[d_bi])
            bi3 = d_bi[:, 0:nb].unsqueeze(2).broadcast_to([128, nb, 32])
            k.v("tensor_tensor", dict(out=v3, in0=v3, in1=bi3, op=ALU.mult),
                r=[d_wt, d_bi], w=[d_wt])
            emit_rne(k, d_wt[:, 0:n_cols], d_wt, d_it[:, 0:n_cols], d_it)
            k.v("tensor_tensor", dict(out=v3, in0=v3, in1=bm3, op=ALU.mult),
                r=[d_wt, d_bm], w=[d_wt])
            for cc_ in range(n_cols // 128):
                bank, bk = PSB[cc_ % 2], PK[cc_ % 2]
                k.t("transpose", dict(out=bank[:, 0:128],
                                      in_=d_wt[:, 128 * cc_:128 * (cc_ + 1)],
                                      identity=sid[:]), r=[d_wt, sid], w=[bk])
                k.s("copy", dict(out=d_th[:], in_=bank[:, 0:128]),
                    r=[bk], w=[d_th])
                k.v("scalar_tensor_tensor", dict(
                    out=d_tl[:], in0=bank[:, 0:128], scalar=1.0, in1=d_th[:],
                    op0=ALU.mult, op1=ALU.subtract), r=[bk, d_th], w=[d_tl])
                p.dma("sync", out_h[128 * cc_:128 * (cc_ + 1),
                                    128 * ot:128 * (ot + 1)], d_th[:],
                      sem="d_dqh" + str(ot % 2))
                p.dma("sync", out_l[128 * cc_:128 * (cc_ + 1),
                                    128 * ot:128 * (ot + 1)], d_tl[:],
                      sem="d_dql" + str(ot % 2))

    dequant_one(wug[0], I3, D, wq_up_h, wq_up_l, "u")
    dequant_one(wug[1], I3, D, wq_g_h, wq_g_l, "g")
    dequant_one(wd, D, I3, wq_d_h, wq_d_l, "d")
    es_4c.close()
    p.barrier()

    if cfg.get("STOP_AFTER") == 5:
        k.phase_stacks.append(es_3)
        k.phase_stacks.append(es_4a)
        k.phase_stacks.append(es_4f)
        return k, locals()
    es_4d = contextlib.ExitStack()

    def cc_scales(cc_out_ap, cols, dst_col0, qmax, semtag):
        for j in range(cols):
            p.dma("sync", g_cc[0:1, 8 * j:8 * (j + 1)],
                  cc_out_ap[:, j][None, :], sem="d_gcc", loose=True)
        for j in range(cols):
            k.v("tensor_reduce", dict(out=s_sc[:, 8 + j:9 + j],
                                      in_=g_cc[0:1, 8 * j:8 * (j + 1)],
                                      axis=AXX,
                                      op=ALU.max), r=[g_cc], w=[s_sc])
            k.v("tensor_scalar", dict(out=s_sc[:, 8 + j:9 + j],
                                      in0=s_sc[:, 8 + j:9 + j],
                                      scalar1=1.0 / qmax, scalar2=1e-12,
                                      op0=ALU.mult, op1=ALU.max),
                r=[s_sc], w=[s_sc])
            k.v("reciprocal", dict(out=s_sc[:, 12 + j:13 + j],
                                   in_=s_sc[:, 8 + j:9 + j]),
                r=[s_sc], w=[s_sc])
            for (src_c, dst_c) in ((12 + j, dst_col0 + 2 * j),
                                   (8 + j, dst_col0 + 2 * j + 1)):
                k.t("matmul", dict(out=PSB[7][:, 0:1], lhsT=s_ones[:],
                                   rhs=s_sc[0:1, src_c:src_c + 1],
                                   start=True, stop=True),
                    r=[s_ones, s_sc], w=[PK[7]])
                k.s("copy", dict(out=s_scr[:, dst_c:dst_c + 1],
                                 in_=PSB[7][:, 0:1]), r=[PK[7]], w=["s_scr"])

    cc_scales(cc3_out, 1, 0, 32767.0, "3")
    for dc in range(NDC):
        for sc_ in range(TC // WTS0):
            sl3 = slice(TC * dc + WTS0 * sc_, TC * dc + WTS0 * (sc_ + 1))
            emit_qdq(k, f_xn1[:, sl3], f_xn1, u_it[:], u_it,
                     s_scr[:, 0:1], s_scr[:, 1:2])
            split16(k, f_xqh[:], f_xql[:], f_xn1[:, sl3],
                    r=[f_xn1], w=[f_xqh, f_xql])
            p.dma("sync", xq_sp_h[128 * dc:128 * (dc + 1),
                                  WTS0 * sc_:WTS0 * (sc_ + 1)], f_xqh[:],
                  sem="d_xqh")
            p.dma("sync", xq_sp_l[128 * dc:128 * (dc + 1),
                                  WTS0 * sc_:WTS0 * (sc_ + 1)], f_xql[:],
                  sem="d_xql")
    es_4f.close()
    UWH = [k.sb(f"u_wh{i}", [128, NDC * 128], BF16, es=es_4d) for i in range(2)]
    UWL = [k.sb(f"u_wl{i}", [128, NDC * 128], BF16, es=es_4d) for i in range(2)]
    k.v("memset", dict(ap=u_mx[:], constant=0.0), w=[u_mx])
    for w_i, (wh_sp, wl_sp, spill, mxc) in enumerate(
            ((wq_up_h, wq_up_l, up_spill, 0), (wq_g_h, wq_g_l, gate_spill, 1))):
        for ts in range(NTS):
            for ic in range(NDC):
                p.dma("gpsimd", xq_th[:, WTS * ic:WTS * (ic + 1)],
                      xq_sp_h[128 * ic:128 * (ic + 1),
                              WTS * ts:WTS * (ts + 1)], sem="d_xqth")
                p.dma("gpsimd", xq_tl[:, WTS * ic:WTS * (ic + 1)],
                      xq_sp_l[128 * ic:128 * (ic + 1),
                              WTS * ts:WTS * (ts + 1)], sem="d_xqtl")
            for oc in range(NI3):
                u_wh, u_wl = UWH[oc % 2], UWL[oc % 2]
                p.dma("gpsimd", u_wh[:].rearrange("p (c m) -> p c m", m=128),
                      wh_sp[:, 128 * oc:128 * (oc + 1)].rearrange(
                          "(c p) m -> p c m", p=128), sem="d_uw" + str(oc % 2))
                p.dma("gpsimd", u_wl[:].rearrange("p (c m) -> p c m", m=128),
                      wl_sp[:, 128 * oc:128 * (oc + 1)].rearrange(
                          "(c p) m -> p c m", p=128), sem="d_uwl" + str(oc % 2))
                bank, bk = PSB[2 + (oc % 2)], PK[2 + (oc % 2)]
                for ic in range(NDC):
                    mm3(k, bank[:, 0:WTS],
                        u_wh[:, 128 * ic:128 * (ic + 1)],
                        u_wl[:, 128 * ic:128 * (ic + 1)],
                        xq_th[:, WTS * ic:WTS * (ic + 1)],
                        xq_tl[:, WTS * ic:WTS * (ic + 1)],
                        start=(ic == 0), stop=(ic == NDC - 1),
                        r=[u_wh, u_wl, xq_th, xq_tl], w=[bk])
                u_t = UTB[oc % 2]
                k.s("copy", dict(out=u_t[:], in_=bank[:, 0:WTS]),
                    r=[bk], w=[u_t])
                k.v("tensor_reduce", dict(out=u_mx[:, 2:3], in_=u_t[:],
                                          axis=AXX, op=ALU.max,
                                          apply_absolute_value=True),
                    r=[u_t], w=[u_mx])
                k.v("tensor_tensor", dict(out=u_mx[:, mxc:mxc + 1],
                                          in0=u_mx[:, mxc:mxc + 1],
                                          in1=u_mx[:, 2:3], op=ALU.max),
                    r=[u_mx], w=[u_mx])
                p.dma("sync", spill[128 * oc:128 * (oc + 1),
                                    WTS * ts:WTS * (ts + 1)], u_t[:],
                      sem="d_sp" + str(oc % 2))
    for j in range(2):
        k.t("transpose", dict(out=PSB[6][0:1, 0:128], in_=u_mx[:, j:j + 1],
                              identity=sid[:]), r=[u_mx, sid], w=[PK[6]])
        k.v("tensor_reduce", dict(out=s_sc[:, j:j + 1],
                                  in_=PSB[6][0:1, 0:128], axis=AXX,
                                  op=ALU.max), r=[PK[6]], w=[s_sc])
    k.v("memset", dict(ap=s_sc[:, 2:8], constant=0.0), w=[s_sc])
    p.dma("sync", cc5_in[0, :], s_sc[0:1, 0:8], sem="d_ccx")
    k.emit_cc("AllGather", ALU.bypass, ALL8, cc5_in, cc5_out)
    cc_scales(cc5_out, 2, 2, 32767.0, "5")

    if cfg.get("STOP_AFTER") == 6:
        k.phase_stacks.append(es_3)
        k.phase_stacks.append(es_4a)
        k.phase_stacks.append(es_4d)
        return k, locals()
    C2_16 = float(2.0 ** -16)
    k.v("memset", dict(ap=u_mx[:, 3:4], constant=0.0), w=[u_mx])
    for oc in range(NI3):
        for ts in range(NTS):
            cpar = (oc * NTS + ts) % 2
            u_t, u_sg, u_it = UTB[cpar], USG[cpar], UIT[cpar]
            p.dma("gpsimd", u_t[:], gate_spill[128 * oc:128 * (oc + 1),
                                               WTS * ts:WTS * (ts + 1)],
                  sem="d_ga" + str(cpar))
            emit_qdq(k, u_t[:], u_t, u_it[:], u_it, s_scr[:, 4:5],
                     s_scr[:, 5:6])
            k.s("activation", dict(out=u_sg[:], in_=u_t[:], func=AF.Sigmoid),
                r=[u_t], w=[u_sg])
            k.v("tensor_scalar", dict(out=u_sg[:], in0=u_sg[:],
                                      scalar1=65536.0, scalar2=C_RNE,
                                      op0=ALU.mult, op1=ALU.add),
                r=[u_sg], w=[u_sg])
            k.v("tensor_scalar", dict(out=u_sg[:], in0=u_sg[:],
                                      scalar1=C_RNE, scalar2=65535.0,
                                      op0=ALU.subtract, op1=ALU.min),
                r=[u_sg], w=[u_sg])
            k.v("tensor_copy", dict(out=u_it[:], in_=u_sg[:]),
                r=[u_sg], w=[u_it])
            k.v("tensor_copy", dict(out=u_sg[:], in_=u_it[:]),
                r=[u_it], w=[u_sg])
            k.v("scalar_tensor_tensor", dict(
                out=u_t[:], in0=u_sg[:], scalar=C2_16, in1=u_t[:],
                op0=ALU.mult, op1=ALU.mult), r=[u_sg, u_t], w=[u_t])
            k.v("tensor_reduce", dict(out=u_mx[:, 2:3], in_=u_t[:], axis=AXX,
                                      op=ALU.max, apply_absolute_value=True),
                r=[u_t], w=[u_mx])
            k.v("tensor_tensor", dict(out=u_mx[:, 3:4], in0=u_mx[:, 3:4],
                                      in1=u_mx[:, 2:3], op=ALU.max),
                r=[u_mx], w=[u_mx])
            p.dma("sync", g2_spill[128 * oc:128 * (oc + 1),
                                   WTS * ts:WTS * (ts + 1)], u_t[:],
                  sem="d_g2")
    k.t("transpose", dict(out=PSB[6][0:1, 0:128], in_=u_mx[:, 3:4],
                          identity=sid[:]), r=[u_mx, sid], w=[PK[6]])
    k.v("tensor_reduce", dict(out=s_sc[:, 0:1], in_=PSB[6][0:1, 0:128],
                              axis=AXX, op=ALU.max), r=[PK[6]], w=[s_sc])
    k.v("memset", dict(ap=s_sc[:, 1:8], constant=0.0), w=[s_sc])
    p.dma("sync", cc6_in[0, :], s_sc[0:1, 0:8], sem="d_ccx")
    k.emit_cc("AllGather", ALU.bypass, ALL8, cc6_in, cc6_out)
    cc_scales(cc6_out, 1, 6, 32767.0, "6")

    k.v("memset", dict(ap=u_mx[:, 3:4], constant=0.0), w=[u_mx])
    for oc in range(NI3):
        for ts in range(NTS):
            cpar = (oc * NTS + ts) % 2
            u_t, u_sg, u_it = UTB[cpar], USG[cpar], UIT[cpar]
            p.dma("gpsimd", u_t[:], g2_spill[128 * oc:128 * (oc + 1),
                                             WTS * ts:WTS * (ts + 1)],
                  sem="d_g2l" + str(cpar))
            emit_qdq(k, u_t[:], u_t, u_it[:], u_it, s_scr[:, 6:7],
                     s_scr[:, 7:8])
            p.dma("gpsimd", u_sg[:], up_spill[128 * oc:128 * (oc + 1),
                                              WTS * ts:WTS * (ts + 1)],
                  sem="d_upl" + str(cpar))
            emit_qdq(k, u_sg[:], u_sg, u_it[:], u_it, s_scr[:, 2:3],
                     s_scr[:, 3:4])
            k.v("tensor_tensor", dict(out=u_t[:], in0=u_t[:], in1=u_sg[:],
                                      op=ALU.mult), r=[u_t, u_sg], w=[u_t])
            k.v("tensor_reduce", dict(out=u_mx[:, 2:3], in_=u_t[:], axis=AXX,
                                      op=ALU.max, apply_absolute_value=True),
                r=[u_t], w=[u_mx])
            k.v("tensor_tensor", dict(out=u_mx[:, 3:4], in0=u_mx[:, 3:4],
                                      in1=u_mx[:, 2:3], op=ALU.max),
                r=[u_mx], w=[u_mx])
            p.dma("sync", op_spill[128 * oc:128 * (oc + 1),
                                   WTS * ts:WTS * (ts + 1)], u_t[:],
                  sem="d_op")
    k.t("transpose", dict(out=PSB[6][0:1, 0:128], in_=u_mx[:, 3:4],
                          identity=sid[:]), r=[u_mx, sid], w=[PK[6]])
    k.v("tensor_reduce", dict(out=s_sc[:, 0:1], in_=PSB[6][0:1, 0:128],
                              axis=AXX, op=ALU.max), r=[PK[6]], w=[s_sc])
    k.v("memset", dict(ap=s_sc[:, 1:8], constant=0.0), w=[s_sc])
    p.dma("sync", cc7_in[0, :], s_sc[0:1, 0:8], sem="d_ccx")
    k.emit_cc("AllGather", ALU.bypass, ALL8, cc7_in, cc7_out)
    cc_scales(cc7_out, 1, 8, 32767.0, "7")

    o_qh = k.sb("o_qh", [128, NI3 * WTS], BF16, es=es_4d)
    o_ql = k.sb("o_ql", [128, NI3 * WTS], BF16, es=es_4d)
    DWH = [k.sb(f"dw_h{i}", [128, NI3 * 128], BF16, es=es_4d) for i in range(2)]
    DWL = [k.sb(f"dw_l{i}", [128, NI3 * 128], BF16, es=es_4d) for i in range(2)]
    for ts in range(NTS):
        for ic3 in range(NI3):
            cpar = ic3 % 2
            u_t, u_it = UTB[cpar], UIT[cpar]
            p.dma("sync", u_t[:], op_spill[128 * ic3:128 * (ic3 + 1),
                                           WTS * ts:WTS * (ts + 1)],
                  sem="d_oq" + str(cpar))
            emit_qdq(k, u_t[:], u_t, u_it[:], u_it, s_scr[:, 8:9],
                     s_scr[:, 9:10])
            split16(k, o_qh[:, WTS * ic3:WTS * (ic3 + 1)],
                    o_ql[:, WTS * ic3:WTS * (ic3 + 1)], u_t[:],
                    r=[u_t], w=[o_qh, o_ql])
        for dcol in range(NDC):
            dpar = dcol % 2
            dw_h, dw_l = DWH[dpar], DWL[dpar]
            p.dma("gpsimd", dw_h[:].rearrange("p (c m) -> p c m", m=128),
                  wq_d_h[:, 128 * dcol:128 * (dcol + 1)].rearrange(
                      "(c p) m -> p c m", p=128), sem="d_dwh" + str(dpar))
            p.dma("gpsimd", dw_l[:].rearrange("p (c m) -> p c m", m=128),
                  wq_d_l[:, 128 * dcol:128 * (dcol + 1)].rearrange(
                      "(c p) m -> p c m", p=128), sem="d_dwl" + str(dpar))
            bank, bk = PSB[2 + (dcol % 2)], PK[2 + (dcol % 2)]
            for ic3 in range(NI3):
                mm3(k, bank[:, 0:WTS],
                    dw_h[:, 128 * ic3:128 * (ic3 + 1)],
                    dw_l[:, 128 * ic3:128 * (ic3 + 1)],
                    o_qh[:, WTS * ic3:WTS * (ic3 + 1)],
                    o_ql[:, WTS * ic3:WTS * (ic3 + 1)],
                    start=(ic3 == 0), stop=(ic3 == NI3 - 1),
                    r=[dw_h, dw_l, o_qh, o_ql], w=[bk])
            u_t2 = USG[dcol % 2]
            k.s("copy", dict(out=u_t2[:], in_=bank[:, 0:WTS]), r=[bk], w=[u_t2])
            for sub in range(WTS // 128):
                bank2, bk2 = PSB[6], PK[6]
                k.t("transpose", dict(out=bank2[:, 0:128],
                                      in_=u_t2[:, 128 * sub:128 * (sub + 1)],
                                      identity=sid[:]), r=[u_t2, sid], w=[bk2])
                tt2 = (WTS * ts) // 128 + sub
                k.v("tensor_tensor", dict(
                    out=q_x1[:, D * tt2 + 128 * dcol:D * tt2 + 128 * (dcol + 1)],
                    in0=q_x1[:, D * tt2 + 128 * dcol:D * tt2 + 128 * (dcol + 1)],
                    in1=bank2[:, 0:128], op=ALU.add), r=[q_x1, bk2], w=[q_x1])
    es_4d.close()

    out = k.dram("out", [TC, D], kind="ExternalOutput")
    for tt in range(NTC):
        p.dma("gpsimd", out[128 * tt:128 * (tt + 1), :],
              q_x1[:, D * tt:D * (tt + 1)], sem="d_out")
    k.phase_stacks.append(es_3)
    k.phase_stacks.append(es_4a)
    return k, locals()


def finish(k, dummy_out=False):
    p, nc = k.p, k.nc
    if dummy_out:
        o = k.dram("out", [k.TC, k.D], kind="ExternalOutput")
        es_z = contextlib.ExitStack()
        z = k.sb("z_dummy", [128, k.D], es=es_z)
        k.v("memset", dict(ap=z[:], constant=0.0), w=[z])
        for tt in range(k.TC // 128):
            p.dma("gpsimd", o[128 * tt:128 * (tt + 1), :], z[:], sem="d_out")
        es_z.close()
    for st_ in reversed(k.phase_stacks):
        st_.close()
    p.final_wait("gpsimd")
    p.raw("gpsimd", lambda g: g.nop())
    ctx = p.make_sems(k.es)
    with nc.Block() as block:
        p.emit(block, ctx)
    k.es.close()
    return nc


def host_inputs_for_core(ins, core, cfg):
    T, D = cfg["T"], cfg["D"]
    DI = 2 * D
    DIH = DI // 2
    b, ph = core // 2, core % 2
    f32 = np.float32
    cont = np.ascontiguousarray

    def t_(a):
        return cont(np.asarray(a, f32).T)

    xs_rows = slice(ph * DIH, (ph + 1) * DIH)
    z_rows = slice(DI + ph * DIH, DI + (ph + 1) * DIH)
    ipT = np.concatenate([np.asarray(ins["L0_attn_in_proj_w"], f32)[xs_rows].T,
                          np.asarray(ins["L0_attn_in_proj_w"], f32)[z_rows].T],
                         axis=1)
    d = {
        "x0": cont(np.asarray(ins["L0_attn_x"], f32)[b]),
        "w_ipT": cont(ipT),
        "conv_w": cont(np.asarray(ins["L0_attn_conv_w"], f32)[xs_rows, 0, :]),
        "conv_b": cont(np.asarray(ins["L0_attn_conv_b"], f32)[xs_rows]),
        "w_xpT": cont(np.asarray(ins["L0_attn_x_proj_w"], f32)[:, xs_rows].T),
        "w_dtT": cont(np.asarray(ins["L0_attn_dt_proj_w"], f32)[xs_rows].T),
        "dt_b": cont(np.asarray(ins["L0_attn_dt_proj_b"], f32)[xs_rows]),
        "A_log": cont(np.asarray(ins["L0_attn_A_log"], f32)[xs_rows]),
        "D_par": cont(np.asarray(ins["L0_attn_D_param"], f32)[xs_rows]),
        "w_opT": cont(np.asarray(ins["L0_attn_out_proj_w"], f32)
                      [:, xs_rows].T),
        "norm_w": cont(np.asarray(ins["L0_attn_norm_w"], f32)),
    }
    wo_loc = np.asarray(ins["L0_ffn_Wo"], f32)[512 * ph:512 * (ph + 1)]
    wo_r = np.zeros((4, 128, D), f32)
    for g in range(4):
        for r in range(128):
            wo_r[g, r] = wo_loc[(2 * g + r // 64) * 64 + (r % 64)]
    d.update({
        "ln1_w": cont(np.asarray(ins["L0_ffn_ln1_w"], f32)),
        "wq": cont(np.asarray(ins["L0_ffn_Wq"], f32)[:, 512 * ph:512 * (ph + 1)]),
        "wk": cont(np.asarray(ins["L0_ffn_Wk"], f32)[:, 128 * ph:128 * (ph + 1)]),
        "wv": cont(np.asarray(ins["L0_ffn_Wv"], f32)[:, 128 * ph:128 * (ph + 1)]),
        "qn_w": cont(np.asarray(ins["L0_ffn_qn_w"], f32)),
        "kn_w": cont(np.asarray(ins["L0_ffn_kn_w"], f32)),
        "wg": cont(np.asarray(ins["L0_ffn_Wg"], f32)),
        "wo_r": cont(wo_r),
    })
    TC = T // 2
    d.update({
        "x0o": cont(np.asarray(ins["L0_attn_x"], f32)[b][ph * TC:(ph + 1) * TC]),
        "ln2_w": cont(np.asarray(ins["L0_ffn_ln2_w"], f32)),
        "wr": cont(np.asarray(ins["L0_ffn_Wr"], f32)),
        "w1": np.ascontiguousarray(np.asarray(ins["L0_ffn_W1"], f32)),
        "w2": np.ascontiguousarray(np.asarray(ins["L0_ffn_W2"], f32)),
        "a_param": cont(np.asarray(ins["L1_attn_a_param"], f32)),
        "w_in": cont(np.asarray(ins["L1_attn_w_in"], f32)),
        "w_a": cont(np.asarray(ins["L1_attn_w_a"], f32)),
        "is_odd": np.full((1, 1), float(ph), f32),
        "wug": cont(np.stack([np.asarray(ins["L1_ffn_w_up"], f32),
                              np.asarray(ins["L1_ffn_w_gate"], f32)])),
        "wd": cont(np.asarray(ins["L1_ffn_w_down"], f32)),
    })
    return d


def kernel(**inputs):
    cfg = dict(CFG)
    k, _env = build(cfg)
    nc = finish(k)
    in_maps = [host_inputs_for_core(inputs, c, cfg) for c in range(8)]
    res = run_bass_kernel_spmd(nc, in_maps, core_ids=list(range(8)))
    T, D = cfg["T"], cfg["D"]
    TC = T // 2
    out = np.empty((4, T, D), dtype=np.float32)
    for c in range(8):
        b, ph = c // 2, c % 2
        out[b, ph * TC:(ph + 1) * TC] = np.asarray(res.results[c]["out"])
    return out
